# version 01
# baseline (speedup 1.0000x reference)
import numpy as np
from contextlib import ExitStack
import concourse.bass as bass
import concourse.mybir as mybir
from concourse.bass_utils import run_bass_kernel_spmd

F32 = mybir.dt.float32
BF16 = mybir.dt.bfloat16
U8 = mybir.dt.uint8
AF = mybir.ActivationFunctionType
ALU = mybir.AluOpType
AX = mybir.AxisListType

NCORES = 8
BPC = 2
L = 2048
D = 1024
F = 8
DH = 4096
NZ = 8
EPS = 1e-5
CH = 512
NCH = L // CH
QT = 512
NQ = L // QT
HALF = 1024
ONES_SCALE = 1.0 / 1024.0

ENGINES = ("sync", "gpsimd", "vector", "scalar", "tensor")

DBG = ()
UPTO = 99


class Prog:
    def __init__(self):
        self.q = {e: [] for e in ENGINES}
        self.cnt = {}
        self.wm = {e: {} for e in ENGINES}
        self.sems = {}
        self.evlog = {e: [] for e in ENGINES}

    def check_deadlock(self):
        ptr = {e: 0 for e in ENGINES}
        val = {}
        progress = True
        while progress:
            progress = False
            for e in ENGINES:
                log = self.evlog[e]
                while ptr[e] < len(log):
                    kind, sem, c = log[ptr[e]]
                    if kind == "w" and val.get(sem, 0) < c:
                        break
                    if kind == "i":
                        val[sem] = val.get(sem, 0) + c
                    ptr[e] += 1
                    progress = True
        stuck = [(e, ptr[e], len(self.evlog[e]), self.evlog[e][ptr[e]])
                 for e in ENGINES if ptr[e] < len(self.evlog[e])]
        return stuck

    def tok(self, *sems):
        if not sems:
            return {s: c for s, c in self.cnt.items() if c > 0}
        return {s: self.cnt.get(s, 0) for s in sems if self.cnt.get(s, 0) > 0}

    def wait(self, eng, tokd):
        assert eng in ENGINES
        for s, c in tokd.items():
            if c > self.wm[eng].get(s, 0):
                self.wm[eng][s] = c
                self.evlog[eng].append(("w", s, c))
                self.q[eng].append(
                    lambda E, self=self, s=s, c=c: E.wait_ge(self.sems[s], c))

    def emit(self, eng, fn, inc=None, step=1):
        assert eng in ENGINES
        if inc is None:
            self.q[eng].append(fn)
            self.evlog[eng].append(("n", None, 0))
            return None
        self.cnt[inc] = self.cnt.get(inc, 0) + step
        c = self.cnt[inc]
        self.evlog[eng].append(("i", inc, step))
        self.q[eng].append(
            lambda E, self=self, fn=fn, inc=inc, step=step:
                fn(E).then_inc(self.sems[inc], step))
        self.wm[eng][inc] = c
        return c

    def force_wait(self, eng, sem, c):
        self.evlog[eng].append(("w", sem, c))
        self.q[eng].append(
            lambda E, self=self, sem=sem, c=c: E.wait_ge(self.sems[sem], c))
        if c > self.wm[eng].get(sem, 0):
            self.wm[eng][sem] = c

    def fence(self):
        t = self.tok()
        for e in ENGINES:
            self.wait(e, t)


class Arena:
    def __init__(self, t, nbytes):
        self.t = t
        self.nbytes = nbytes
        self.off = 0
        self.marks = []
        self.peak = 0

    def push(self):
        self.marks.append(self.off)

    def pop(self):
        self.off = self.marks.pop()

    def _take(self, nbytes):
        o = self.off
        assert o % 4 == 0
        self.off += nbytes + ((4 - nbytes % 4) % 4)
        self.peak = max(self.peak, self.off)
        assert self.off <= self.nbytes, f"arena overflow {self.off} > {self.nbytes}"
        return o

    def f32(self, cols):
        o = self._take(4 * cols)
        return self.t[:, o:o + 4 * cols].bitcast(F32)

    def bf16(self, cols):
        o = self._take(2 * cols)
        return self.t[:, o:o + 2 * cols].bitcast(BF16)

    def f32_at(self, o, cols):
        return self.t[:, o:o + 4 * cols].bitcast(F32)

    def bf16_at(self, o, cols):
        return self.t[:, o:o + 2 * cols].bitcast(BF16)


def build(debug_names=(), upto=99):
    nc = bass.Bass("TRN2", target_bir_lowering=False)
    p = Prog()
    dbgset = {t[0] for t in debug_names}

    x_d = nc.dram_tensor("x", [BPC, L // 128, 128, D], F32, kind="ExternalInput").ap()
    out_d = nc.dram_tensor("out", [BPC, L // 128, 128, D], F32, kind="ExternalOutput").ap()
    vnames = {
        "lag": "L0_attn_ln_att_g", "lab": "L0_attn_ln_att_b",
        "lfg": "L0_attn_ln_ffn_g", "lfb": "L0_attn_ln_ffn_b",
        "amk": "L0_attn_att_mix_k", "amv": "L0_attn_att_mix_v",
        "amr": "L0_attn_att_mix_r", "fmk": "L0_attn_ffn_mix_k",
        "fmr": "L0_attn_ffn_mix_r", "td": "L0_attn_att_time_decay",
        "tf": "L0_attn_att_time_first"}
    vec_d = {k: nc.dram_tensor(n, [D], F32, kind="ExternalInput").ap()
             for k, n in vnames.items()}
    wk_d = nc.dram_tensor("L0_attn_att_wk", [D, D], F32, kind="ExternalInput").ap()
    wv_d = nc.dram_tensor("L0_attn_att_wv", [D, D], F32, kind="ExternalInput").ap()
    wr_d = nc.dram_tensor("L0_attn_att_wr", [D, D], F32, kind="ExternalInput").ap()
    wo_d = nc.dram_tensor("L0_attn_att_wo", [D, D], F32, kind="ExternalInput").ap()
    fwk_d = nc.dram_tensor("L0_attn_ffn_wk", [D, DH], F32, kind="ExternalInput").ap()
    fwv_d = nc.dram_tensor("L0_attn_ffn_wv", [DH, D], F32, kind="ExternalInput").ap()
    fwr_d = nc.dram_tensor("L0_attn_ffn_wr", [D, D], F32, kind="ExternalInput").ap()
    rW1_d = nc.dram_tensor("L0_ffn_W1", [D, 256], F32, kind="ExternalInput").ap()
    rb1_d = nc.dram_tensor("L0_ffn_b1", [256], F32, kind="ExternalInput").ap()
    rW2_d = nc.dram_tensor("L0_ffn_W2", [256, NZ], F32, kind="ExternalInput").ap()
    rb2_d = nc.dram_tensor("L0_ffn_b2", [NZ], F32, kind="ExternalInput").ap()
    wa_d = nc.dram_tensor("L0_ffn_Wa", [NZ, D, DH], F32, kind="ExternalInput").ap()
    wb_d = nc.dram_tensor("L0_ffn_Wb", [NZ, DH, D], F32, kind="ExternalInput").ap()
    dbg_d = {}
    for t in debug_names:
        n, shp = t[0], t[1]
        dt = t[2] if len(t) > 2 else F32
        dbg_d[n] = nc.dram_tensor("dbg_" + n, shp, dt, kind="ExternalOutput").ap()

    es = ExitStack()
    with es:
        AR_BYTES = 206 * 1024
        ar_t = es.enter_context(nc.sbuf_tensor("arena", [128, AR_BYTES], U8))
        ar = Arena(ar_t, AR_BYTES)
        pb = [es.enter_context(nc.psum_tensor(f"pb{i}", [128, 512], F32))
              for i in range(8)]
        for s in ("in", "out", "pe", "dve", "act", "pool"):
            p.sems[s] = es.enter_context(nc.semaphore("s_" + s))

        ident = ar.f32(128)
        ones_col_b = ar.bf16(2)
        ones_row_f = ar.f32(128)
        VEC = {}
        for n in ("lag", "lab", "lfg", "lfb", "amk", "amv", "amr", "fmk", "fmr",
                  "td", "tf", "ew", "eu", "wbc", "pcar", "qcar", "pool8", "ptmp"):
            VEC[n] = ar.f32(8)
        hcar = ar.bf16(8)
        rb1c = ar.f32(2)
        rh2 = ar.f32(2)
        rW1 = ar.f32(2048)
        rW2 = ar.f32(16)
        cEPS = ar.f32(1)
        nc.const_aps.register(F32, EPS, cEPS[:, 0:1])
        m_row = ar.f32(512)
        rs_row = ar.f32(512)
        mrs_row = ar.f32(512)
        srow1 = ar.f32(512)
        srow2 = ar.f32(512)
        rrow = ar.f32(16)
        rrow2 = ar.f32(16)
        SCRF = ar.f32(512)
        SCRB = SCRF.bitcast(BF16)

        bank_state = [dict() for _ in range(8)]
        state = {"lnmm": {}, "wslot": 0, "wtok": [dict(), dict()]}

        def mm_group(bank, pairs, opd_tok, bslice=None, free=512):
            p.wait("tensor", bank_state[bank])
            p.wait("tensor", opd_tok)
            tgt = pb[bank][:, 0:free] if bslice is None else bslice
            n = len(pairs)
            c = None
            for i, (lt, rh) in enumerate(pairs):
                if i < n - 1:
                    p.emit("tensor", lambda E, tgt=tgt, lt=lt, rh=rh, i=i:
                           E.matmul(tgt, lt, rh, start=(i == 0), stop=False))
                else:
                    c = p.emit("tensor", lambda E, tgt=tgt, lt=lt, rh=rh, i=i:
                               E.matmul(tgt, lt, rh, start=(i == 0), stop=True),
                               inc="pe")
            return {"pe": c}

        def consume(bank, eng, fn, tokd, sem):
            p.wait(eng, tokd)
            c = p.emit(eng, fn, inc=sem)
            bank_state[bank] = {sem: c}
            return {sem: c}

        def dma(dst, src, sem="in", wait_tok=None):
            if wait_tok:
                p.wait("sync", wait_tok)
            return {sem: p.emit("sync", lambda E, dst=dst, src=src:
                                E.dma_start(out=dst, in_=src), inc=sem, step=16)}

        def load_cast(dst, dram_src, wstg, nst, cols, cast_wait=None, scale_col=None):
            s = state["wslot"] % nst
            state["wslot"] += 1
            while len(state["wtok"]) < nst:
                state["wtok"].append(dict())
            stg = wstg[:, s * 1024:s * 1024 + cols]
            t = dma(stg, dram_src, wait_tok=state["wtok"][s])
            p.wait("gpsimd", t)
            if cast_wait:
                p.wait("gpsimd", cast_wait)
            if scale_col is None:
                c = p.emit("gpsimd", lambda E, dst=dst, stg=stg:
                           E.tensor_copy(out=dst, in_=stg), inc="pool")
            else:
                c = p.emit("gpsimd", lambda E, dst=dst, stg=stg, sc=scale_col:
                           E.tensor_single_scalar(out=dst, in_=stg, scalar=sc,
                                                  op=ALU.mult), inc="pool")
            state["wtok"][s] = {"pool": c}
            return {"pool": c}

        p.emit("gpsimd", lambda E: E.memset(ident, 1.0))
        p.emit("gpsimd", lambda E: E.affine_select(
            out=ident, in_=ident, pattern=[[-1, 128]], compare_op=ALU.is_equal,
            fill=0.0, base=0, channel_multiplier=1))
        p.emit("gpsimd", lambda E: E.memset(ones_col_b[:, 0:1], ONES_SCALE))
        p.emit("gpsimd", lambda E: E.memset(ones_col_b[:, 1:2], 1.0))
        p.emit("gpsimd", lambda E: E.memset(cEPS, EPS))
        c = p.emit("gpsimd", lambda E: E.memset(ones_row_f[0:1, :], 1.0), inc="pool")
        const_tok = {"pool": c}
        for e in ENGINES:
            p.wait(e, const_tok)

        vtok = {}
        for key in vnames:
            src = vec_d[key].rearrange("(f p o) -> f p o", p=128, o=1)
            for f in range(F):
                vtok = dma(VEC[key][:, f:f + 1], src[f])
        W1r = rW1_d.rearrange("(k p) h -> k p h", p=128)
        for k in range(F):
            vtok = dma(rW1[:, k * 256:(k + 1) * 256], W1r[k])
        W2r = rW2_d.rearrange("(k p) z -> k p z", p=128)
        for k in range(2):
            vtok = dma(rW2[:, k * 8:(k + 1) * 8], W2r[k])
        b1r = rb1_d.rearrange("(k p o) -> k p o", p=128, o=1)
        for k in range(2):
            vtok = dma(rb1c[:, k:k + 1], b1r[k])
        b2row = ar.f32(8)
        vtok = dma(b2row[0:1, 0:8], rb2_d.rearrange("(o z) -> o z", o=1))

        p.wait("scalar", vtok)
        p.emit("scalar", lambda E: E.activation(out=VEC["ew"], in_=VEC["td"], func=AF.Exp))
        p.emit("scalar", lambda E: E.activation(out=VEC["ew"], in_=VEC["ew"], func=AF.Exp, scale=-1.0))
        c = p.emit("scalar", lambda E: E.activation(out=VEC["eu"], in_=VEC["tf"], func=AF.Exp), inc="act")
        ew_tok = {"act": c}

        STREAM = ar.f32(F * L)

        def st_f(f):
            return STREAM[:, f * L:(f + 1) * L]

        def ln_stats(src_f, t0, n, xcb, sqb, scratch_tok):
            p.wait("gpsimd", scratch_tok)
            p.wait("gpsimd", state["lnmm"])
            p.wait("scalar", scratch_tok)
            p.wait("scalar", state["lnmm"])
            for f in range(F):
                src = src_f(f)[:, t0:t0 + n]
                p.emit("gpsimd", lambda E, d=xcb[:, f * n:(f + 1) * n], s=src:
                       E.tensor_copy(out=d, in_=s), inc="pool")
                p.emit("scalar", lambda E, d=sqb[:, f * n:(f + 1) * n], s=src:
                       E.activation(out=d, in_=s, func=AF.Square), inc="act")
            cast_tok = p.tok("pool", "act")
            tokS = mm_group(6, [(ones_col_b[:, 0:1], xcb[:, f * n:(f + 1) * n])
                                for f in range(F)], cast_tok, bslice=pb[6][0:1, 0:n])
            tokQ = mm_group(7, [(ones_col_b[:, 0:1], sqb[:, f * n:(f + 1) * n])
                                for f in range(F)], cast_tok, bslice=pb[7][0:1, 0:n])
            state["lnmm"] = {"pe": tokQ["pe"]}
            tokm = consume(6, "vector", lambda E: E.tensor_copy(
                out=m_row[0:1, 0:n], in_=pb[6][0:1, 0:n]), tokS, "dve")
            p.wait("scalar", tokm)
            ca = p.emit("scalar", lambda E: E.activation(
                out=srow1[0:1, 0:n], in_=m_row[0:1, 0:n], func=AF.Square), inc="act")
            tq = dict(tokQ)
            tq["act"] = ca
            tokv = consume(7, "vector", lambda E: E.tensor_sub(
                out=srow2[0:1, 0:n], in0=pb[7][0:1, 0:n], in1=srow1[0:1, 0:n]),
                tq, "dve")
            p.wait("scalar", tokv)
            p.emit("scalar", lambda E: E.activation(
                out=rs_row[0:1, 0:n], in_=srow2[0:1, 0:n], func=AF.Ln, bias=EPS))
            ca = p.emit("scalar", lambda E: E.activation(
                out=rs_row[0:1, 0:n], in_=rs_row[0:1, 0:n], func=AF.Exp, scale=-0.5),
                inc="act")
            p.wait("vector", {"act": ca})
            p.wait("vector", state["lnmm"])
            cd = p.emit("vector", lambda E: E.tensor_mul(
                out=mrs_row[0:1, 0:n], in0=m_row[0:1, 0:n], in1=rs_row[0:1, 0:n]),
                inc="dve")
            return {"dve": cd}

        def ln_bcast(n, rows_tok):
            mm_group(4, [(ones_row_f[0:1, :], rs_row[0:1, 0:n])], rows_tok,
                     bslice=pb[4][:, 0:n])
            t2 = mm_group(5, [(ones_row_f[0:1, :], mrs_row[0:1, 0:n])], rows_tok,
                          bslice=pb[5][:, 0:n])
            state["lnmm"] = {"pe": t2["pe"]}
            return {"pe": t2["pe"]}

        def seal45(tokd):
            bank_state[4] = dict(tokd)
            bank_state[5] = dict(tokd)

        def dbg_dump(name, src_fn, nf=F):
            if name not in dbgset:
                return
            t = p.tok()
            p.wait("sync", t)
            for f in range(nf):
                dma(dbg_d[name][f], src_fn(f), sem="out")
            p.fence()

        def load_transpose_x0(b, ch, STGx, SMFx, stg_tok, smf_free):
            ct = dict()
            for i in range(CH // 128):
                tt = ch * (CH // 128) + i
                s = (ch * (CH // 128) + i) % 3
                t = dma(STGx[:, s * 1024:(s + 1) * 1024], x_d[b, tt],
                        wait_tok=stg_tok[s])
                for f in range(F):
                    bk = 6 + (f % 2)
                    p.wait("tensor", bank_state[bk])
                    p.wait("tensor", t)
                    cpe = p.emit("tensor", lambda E, bk=bk, s=s, f=f, STGx=STGx:
                                 E.transpose(pb[bk][:, 0:128],
                                             STGx[:, s * 1024 + f * 128:s * 1024 + (f + 1) * 128],
                                             ident), inc="pe")
                    dst = SMFx[:, f * CH + i * 128:f * CH + (i + 1) * 128]
                    ct = consume(bk, "vector", lambda E, dst=dst, bk=bk:
                                 E.tensor_copy(out=dst, in_=pb[bk][:, 0:128]),
                                 dict({"pe": cpe}, **smf_free), "dve")
                stg_tok[s] = {"pe": cpe}
            return ct

        for b in range(BPC):
            bdbg = (b == 0)
            p.fence()
            p.emit("gpsimd", lambda E: E.memset(VEC["pcar"], 0.0))
            p.emit("gpsimd", lambda E: E.memset(VEC["qcar"], 0.0))
            p.emit("gpsimd", lambda E: E.memset(hcar, 0.0))
            c = p.emit("gpsimd", lambda E: E.memset(VEC["pool8"], 0.0), inc="pool")
            p.fence()

            ar.push()
            STG = ar.f32(3 * 1024)
            SMF = ar.f32(F * CH)
            XCB = ar.bf16(F * CH)
            SQB = ar.bf16(F * CH)
            stg_tok = [dict(), dict(), dict()]
            smf_free = dict(const_tok)
            for ch in range(NCH):
                tr_tok = load_transpose_x0(b, ch, STG, SMF, stg_tok, smf_free)
                stok = ln_stats(lambda f: SMF[:, f * CH:(f + 1) * CH], 0, CH,
                                XCB, SQB, tr_tok)
                btok = ln_bcast(CH, stok)
                p.wait("vector", btok)
                p.wait("vector", tr_tok)
                for f in range(F):
                    dst = st_f(f)[:, ch * CH:(ch + 1) * CH]
                    src = SMF[:, f * CH:(f + 1) * CH]
                    p.emit("vector", lambda E, dst=dst, src=src: E.tensor_mul(
                        out=dst, in0=src, in1=pb[4][:, 0:CH]))
                    c = p.emit("vector", lambda E, dst=dst: E.tensor_sub(
                        out=dst, in0=dst, in1=pb[5][:, 0:CH]), inc="dve")
                seal45({"dve": c})
                smf_free = dict(p.tok("pool", "act"), dve=c)
            p.fence()
            ar.pop()
            if bdbg:
                dbg_dump("h0", st_f)
            if upto < 2:
                continue

            ar.push()
            WK = ar.bf16(F * D)
            WV = ar.bf16(F * D)
            WR = ar.bf16(F * D)
            WSTG = ar.f32(2 * 1024)
            HBUF = ar.bf16(F * 516)
            MIXK = ar.bf16(F * QT)
            MIXV = ar.bf16(F * QT)
            MIXR = ar.bf16(F * QT)
            RSIG = ar.bf16(F * QT)
            REG = ar._take(18944)
            XCB2 = ar.bf16_at(REG, F * QT)
            SQB2 = ar.bf16_at(REG + 2 * F * QT, F * QT)
            RW = ar.bf16_at(REG, F * QT)
            EK = ar.f32_at(REG + 8192, QT)
            VV = ar.f32_at(REG + 10240, QT)
            BP = ar.f32_at(REG + 12288, QT)
            PBUF = ar.f32_at(REG + 14336, QT + 4)
            QBUF = ar.f32_at(REG + 16400, QT + 4)

            def hb(f):
                return HBUF[:, f * 516:(f + 1) * 516]

            state["wslot"] = 0
            state["wtok"] = [dict(), dict()]
            wkr = wk_d.rearrange("(k p) o -> k p o", p=128)
            wvr = wv_d.rearrange("(k p) o -> k p o", p=128)
            wrr = wr_d.rearrange("(k p) o -> k p o", p=128)
            wor = wo_d.rearrange("(k p) o -> k p o", p=128)
            for ki in range(F):
                load_cast(WK[:, ki * D:(ki + 1) * D], wkr[ki], WSTG, 2, D)
            for ki in range(F):
                load_cast(WV[:, ki * D:(ki + 1) * D], wvr[ki], WSTG, 2, D)
            wtok = dict()
            for ki in range(F):
                wtok = load_cast(WR[:, ki * D:(ki + 1) * D], wrr[ki], WSTG, 2, D)

            prev_wo_pe = dict()
            chain_tok = dict(ew_tok)
            for q in range(NQ):
                t0 = q * QT
                stok = ln_stats(st_f, t0, QT, XCB2, SQB2, prev_wo_pe)
                btok = ln_bcast(QT, stok)
                p.wait("vector", btok)
                p.wait("vector", prev_wo_pe)
                for f in range(F):
                    src = st_f(f)[:, t0:t0 + QT]
                    p.emit("vector", lambda E, src=src: E.tensor_mul(
                        out=SCRF, in0=src, in1=pb[4][:, 0:QT]))
                    p.emit("vector", lambda E: E.tensor_sub(
                        out=SCRF, in0=SCRF, in1=pb[5][:, 0:QT]))
                    p.emit("vector", lambda E, f=f: E.tensor_copy(
                        out=hb(f)[:, 0:1], in_=hcar[:, f:f + 1]))
                    cs = p.emit("vector", lambda E, f=f: E.tensor_scalar(
                        out=hb(f)[:, 1:513], in0=SCRF, scalar1=VEC["lag"][:, f:f + 1],
                        scalar2=VEC["lab"][:, f:f + 1], op0=ALU.mult, op1=ALU.add),
                        inc="dve")
                    p.force_wait("vector", "dve", cs)
                    c = p.emit("vector", lambda E, f=f: E.tensor_copy(
                        out=hcar[:, f:f + 1], in_=hb(f)[:, 512:513]), inc="dve")
                seal45({"dve": c})
                for f in range(F):
                    p.emit("vector", lambda E, f=f: E.tensor_sub(
                        out=SCRB[:, 0:QT], in0=hb(f)[:, 1:513], in1=hb(f)[:, 0:512]))
                    for mx, buf in (("amk", MIXK), ("amv", MIXV), ("amr", MIXR)):
                        p.emit("vector", lambda E, f=f, mx=mx: E.tensor_single_scalar(
                            out=SCRB[:, QT:2 * QT], in_=SCRB[:, 0:QT],
                            scalar=VEC[mx][:, f:f + 1], op=ALU.mult))
                        c = p.emit("vector", lambda E, f=f, buf=buf: E.tensor_add(
                            out=buf[:, f * QT:(f + 1) * QT], in0=SCRB[:, QT:2 * QT],
                            in1=hb(f)[:, 0:512]), inc="dve")
                mix_tok = dict(wtok, dve=c)

                rs_tok = {}
                rsig_war = p.tok("dve")
                for fo in range(F):
                    bk = fo % 4
                    tkr = mm_group(bk, [(WR[:, ki * D + fo * 128:ki * D + fo * 128 + 128],
                                         MIXR[:, ki * QT:(ki + 1) * QT]) for ki in range(F)],
                                   mix_tok)
                    rs_tok = consume(bk, "scalar", lambda E, fo=fo, bk=bk: E.activation(
                        out=RSIG[:, fo * QT:(fo + 1) * QT], in_=pb[bk][:, 0:QT],
                        func=AF.Sigmoid), dict(tkr, **rsig_war), "act")

                for fo in range(F):
                    bk_k = fo % 2
                    bk_v = 2 + (fo % 2)
                    tkk = mm_group(bk_k, [(WK[:, ki * D + fo * 128:ki * D + fo * 128 + 128],
                                           MIXK[:, ki * QT:(ki + 1) * QT]) for ki in range(F)],
                                   mix_tok)
                    tkv = mm_group(bk_v, [(WV[:, ki * D + fo * 128:ki * D + fo * 128 + 128],
                                           MIXV[:, ki * QT:(ki + 1) * QT]) for ki in range(F)],
                                   mix_tok)
                    ek_tok = consume(bk_k, "scalar", lambda E, bk=bk_k: E.activation(
                        out=EK, in_=pb[bk][:, 0:QT], func=AF.Exp),
                        dict(tkk, **chain_tok), "act")
                    vv_tok = consume(bk_v, "scalar", lambda E, bk=bk_v: E.activation(
                        out=VV, in_=pb[bk][:, 0:QT], func=AF.Copy),
                        dict(tkv, **chain_tok), "act")
                    p.wait("vector", ek_tok)
                    p.wait("vector", vv_tok)
                    p.emit("vector", lambda E: E.tensor_mul(out=BP, in0=EK, in1=VV))
                    p.emit("vector", lambda E, f=fo: E.tensor_copy(
                        out=PBUF[:, 0:1], in_=VEC["pcar"][:, f:f + 1]))
                    cs = p.emit("vector", lambda E, f=fo: E.tensor_tensor_scan(
                        out=PBUF[:, 1:513], data0=VEC["ew"][:, f:f + 1].broadcast_to([128, QT]),
                        data1=BP, initial=VEC["pcar"][:, f:f + 1],
                        op0=ALU.mult, op1=ALU.add), inc="dve")
                    p.force_wait("vector", "dve", cs)
                    p.emit("vector", lambda E, f=fo: E.tensor_copy(
                        out=VEC["pcar"][:, f:f + 1], in_=PBUF[:, 512:513]))
                    p.emit("vector", lambda E, f=fo: E.tensor_copy(
                        out=QBUF[:, 0:1], in_=VEC["qcar"][:, f:f + 1]))
                    cs = p.emit("vector", lambda E, f=fo: E.tensor_tensor_scan(
                        out=QBUF[:, 1:513], data0=VEC["ew"][:, f:f + 1].broadcast_to([128, QT]),
                        data1=EK, initial=VEC["qcar"][:, f:f + 1],
                        op0=ALU.mult, op1=ALU.add), inc="dve")
                    p.force_wait("vector", "dve", cs)
                    p.emit("vector", lambda E, f=fo: E.tensor_copy(
                        out=VEC["qcar"][:, f:f + 1], in_=QBUF[:, 512:513]))
                    p.emit("vector", lambda E, f=fo: E.scalar_tensor_tensor(
                        out=BP, in0=BP, scalar=VEC["eu"][:, f:f + 1], in1=PBUF[:, 0:512],
                        op0=ALU.mult, op1=ALU.add))
                    p.emit("vector", lambda E, f=fo: E.scalar_tensor_tensor(
                        out=EK, in0=EK, scalar=VEC["eu"][:, f:f + 1], in1=QBUF[:, 0:512],
                        op0=ALU.mult, op1=ALU.add))
                    cs = p.emit("vector", lambda E: E.reciprocal(out=VV, in_=EK),
                                inc="dve")
                    p.force_wait("vector", "dve", cs)
                    p.emit("vector", lambda E: E.tensor_mul(out=BP, in0=BP, in1=VV))
                    p.wait("vector", rs_tok)
                    c = p.emit("vector", lambda E, fo=fo: E.tensor_mul(
                        out=RW[:, fo * QT:(fo + 1) * QT], in0=BP,
                        in1=RSIG[:, fo * QT:(fo + 1) * QT]), inc="dve")
                    chain_tok = {"dve": c}
                    if "wkv1" in dbgset and b == 0 and q == 1 and fo == 0:
                        t_ = p.tok()
                        p.wait("sync", t_)
                        dma(dbg_d["wkv1"][0], PBUF, sem="out")
                        dma(dbg_d["wkv1"][1], QBUF, sem="out")
                        dma(dbg_d["wkv1"][2][:, 0:QT], BP, sem="out")
                        dma(dbg_d["wkv1"][3][:, 0:QT], EK, sem="out")
                        p.fence()
                    if "wkv0" in dbgset and b == 0 and q == 0 and fo == 0:
                        t_ = p.tok()
                        p.wait("sync", t_)
                        dma(dbg_d["wkv0"][0], PBUF, sem="out")
                        dma(dbg_d["wkv0"][1], QBUF, sem="out")
                        dma(dbg_d["wkv0"][2][:, 0:8], VEC["pcar"], sem="out")
                        dma(dbg_d["wkv0"][3][:, 0:8], VEC["qcar"], sem="out")
                        p.fence()
                rw_tok = {"dve": c}

                kv_pe = p.tok("pe")
                wo_tok = dict()
                for ki in range(F):
                    dst = (MIXK[:, ki * D:(ki + 1) * D] if ki < 4
                           else MIXV[:, (ki - 4) * D:(ki - 3) * D])
                    wo_tok = load_cast(dst, wor[ki], WSTG, 2, D, cast_wait=kv_pe)

                def wo_tile(ki, fo):
                    if ki < 4:
                        return MIXK[:, ki * D + fo * 128:ki * D + fo * 128 + 128]
                    return MIXV[:, (ki - 4) * D + fo * 128:(ki - 4) * D + fo * 128 + 128]

                for fo in range(F):
                    bk = fo % 4
                    tko = mm_group(bk, [(wo_tile(ki, fo), RW[:, ki * QT:(ki + 1) * QT])
                                        for ki in range(F)], dict(rw_tok, **wo_tok))
                    dst = st_f(fo)[:, t0:t0 + QT]
                    consume(bk, "vector", lambda E, dst=dst, bk=bk: E.tensor_add(
                        out=dst, in0=dst, in1=pb[bk][:, 0:QT]), tko, "dve")
                prev_wo_pe = p.tok("pe")
            p.fence()
            ar.pop()
            if bdbg:
                dbg_dump("xa_att", st_f)
            if upto < 3:
                continue

            ar.push()
            FWR = ar.bf16(F * D)
            WKC = ar.bf16(F * CH)
            WVR = ar.bf16(4 * 1024)
            WSTG2 = ar.f32(2 * 1024)
            H2BUF = ar.bf16(F * 516)
            KFIN = ar.bf16(F * QT)
            MIXR2 = ar.bf16(F * QT)
            RFSIG = ar.bf16(F * QT)
            UFULL = ar.bf16(32 * QT)

            def h2(f):
                return H2BUF[:, f * 516:(f + 1) * 516]

            state["wslot"] = 0
            state["wtok"] = [dict(), dict()]
            p.emit("gpsimd", lambda E: E.memset(hcar, 0.0), inc="pool")
            fwrr = fwr_d.rearrange("(k p) o -> k p o", p=128)
            fwr_tok = dict()
            for ki in range(F):
                fwr_tok = load_cast(FWR[:, ki * D:(ki + 1) * D], fwrr[ki], WSTG2, 2, D)

            prev_q_pe = dict()
            for q in range(NQ):
                t0 = q * QT
                stok = ln_stats(st_f, t0, QT, UFULL[:, 0:F * QT],
                                UFULL[:, 2 * F * QT:3 * F * QT], prev_q_pe)
                btok = ln_bcast(QT, stok)
                p.wait("vector", btok)
                p.wait("vector", prev_q_pe)
                for f in range(F):
                    src = st_f(f)[:, t0:t0 + QT]
                    p.emit("vector", lambda E, src=src: E.tensor_mul(
                        out=SCRF, in0=src, in1=pb[4][:, 0:QT]))
                    p.emit("vector", lambda E: E.tensor_sub(
                        out=SCRF, in0=SCRF, in1=pb[5][:, 0:QT]))
                    p.emit("vector", lambda E, f=f: E.tensor_copy(
                        out=h2(f)[:, 0:1], in_=hcar[:, f:f + 1]))
                    cs = p.emit("vector", lambda E, f=f: E.tensor_scalar(
                        out=h2(f)[:, 1:513], in0=SCRF, scalar1=VEC["lfg"][:, f:f + 1],
                        scalar2=VEC["lfb"][:, f:f + 1], op0=ALU.mult, op1=ALU.add),
                        inc="dve")
                    p.force_wait("vector", "dve", cs)
                    c = p.emit("vector", lambda E, f=f: E.tensor_copy(
                        out=hcar[:, f:f + 1], in_=h2(f)[:, 512:513]), inc="dve")
                seal45({"dve": c})
                for f in range(F):
                    p.emit("vector", lambda E, f=f: E.tensor_sub(
                        out=SCRB[:, 0:QT], in0=h2(f)[:, 1:513], in1=h2(f)[:, 0:512]))
                    for mx, buf in (("fmk", KFIN), ("fmr", MIXR2)):
                        p.emit("vector", lambda E, f=f, mx=mx: E.tensor_single_scalar(
                            out=SCRB[:, QT:2 * QT], in_=SCRB[:, 0:QT],
                            scalar=VEC[mx][:, f:f + 1], op=ALU.mult))
                        c = p.emit("vector", lambda E, f=f, buf=buf: E.tensor_add(
                            out=buf[:, f * QT:(f + 1) * QT], in0=SCRB[:, QT:2 * QT],
                            in1=h2(f)[:, 0:512]), inc="dve")
                mix_tok = dict(fwr_tok, dve=c)

                rf_tok = {}
                rf_war = p.tok("dve")
                for fo in range(F):
                    bk = fo % 4
                    tkr = mm_group(bk, [(FWR[:, ki * D + fo * 128:ki * D + fo * 128 + 128],
                                         MIXR2[:, ki * QT:(ki + 1) * QT]) for ki in range(F)],
                                   mix_tok)
                    rf_tok = consume(bk, "scalar", lambda E, fo=fo, bk=bk: E.activation(
                        out=RFSIG[:, fo * QT:(fo + 1) * QT], in_=pb[bk][:, 0:QT],
                        func=AF.Sigmoid), dict(tkr, **rf_war), "act")

                u_tok = dict()
                wkc_war = dict(prev_q_pe)
                for hc in range(DH // CH):
                    wk_tok = dict()
                    for ki in range(F):
                        wk_tok = load_cast(WKC[:, ki * CH:(ki + 1) * CH],
                                           fwk_d[ki * 128:(ki + 1) * 128,
                                                 hc * CH:(hc + 1) * CH],
                                           WSTG2, 2, CH, cast_wait=wkc_war)
                    for ht in range(CH // 128):
                        bk = 4 + (ht % 2)
                        hk = hc * 4 + ht
                        tku = mm_group(bk, [(WKC[:, ki * CH + ht * 128:ki * CH + ht * 128 + 128],
                                             KFIN[:, ki * QT:(ki + 1) * QT]) for ki in range(F)],
                                       dict(mix_tok, **wk_tok))
                        tmax = consume(bk, "vector", lambda E, bk=bk: E.tensor_single_scalar(
                            out=SCRF, in_=pb[bk][:, 0:QT], scalar=0.0, op=ALU.max),
                            tku, "dve")
                        p.wait("scalar", tmax)
                        ca = p.emit("scalar", lambda E, hk=hk: E.activation(
                            out=UFULL[:, hk * QT:(hk + 1) * QT], in_=SCRF,
                            func=AF.Square), inc="act")
                        p.wait("vector", {"act": ca})
                        u_tok = {"act": ca}
                    wkc_war = p.tok("pe")

                if "bdump" in dbgset and b == 0 and q == 0:
                    t_ = p.tok()
                    p.wait("sync", t_)
                    dma(dbg_d["bdump"][0][:, 0:F * QT], KFIN, sem="out")
                    dma(dbg_d["bdump"][1][:, 0:F * QT], RFSIG, sem="out")
                    dma(dbg_d["bdump"][2][:, 0:F * QT], MIXR2, sem="out")
                    dma(dbg_d["bdump"][3], UFULL, sem="out")
                    p.fence()
                for fo in range(F):
                    p.wait("tensor", bank_state[fo])
                ring_tok = [dict() for _ in range(4)]
                ztoks = {}
                for hk in range(32):
                    s = hk % 4
                    wvt = load_cast(WVR[:, s * 1024:(s + 1) * 1024], fwv_d[hk * 128:(hk + 1) * 128, :],
                                    WSTG2, 2, D, cast_wait=ring_tok[s])
                    p.wait("tensor", dict(wvt, **u_tok))
                    for fo in range(F):
                        last = (hk == 31)
                        cpe = p.emit("tensor", lambda E, s=s, fo=fo, hk=hk:
                                     E.matmul(pb[fo][:, 0:QT],
                                              WVR[:, s * 1024 + fo * 128:s * 1024 + fo * 128 + 128],
                                              UFULL[:, hk * QT:(hk + 1) * QT],
                                              start=(hk == 0), stop=(hk == 31)),
                                     inc="pe" if (last or fo == F - 1) else None)
                        if last:
                            ztoks[fo] = {"pe": cpe}
                    ring_tok[s] = {"pe": p.cnt.get("pe", 0)}
                for fo in range(F):
                    dst = st_f(fo)[:, t0:t0 + QT]
                    p.wait("vector", ztoks[fo])
                    p.wait("vector", rf_tok)
                    p.emit("vector", lambda E, fo=fo: E.tensor_mul(
                        out=SCRF, in0=pb[fo][:, 0:QT],
                        in1=RFSIG[:, fo * QT:(fo + 1) * QT]))
                    c = p.emit("vector", lambda E, dst=dst: E.tensor_add(
                        out=dst, in0=dst, in1=SCRF), inc="dve")
                    bank_state[fo] = {"dve": c}
                prev_q_pe = p.tok("pe")
            p.fence()
            ar.pop()
            if bdbg:
                dbg_dump("xa_ffn", st_f)
            if upto < 4:
                continue

            ar.push()
            STGC = ar.f32(3 * 1024)
            SMFC = ar.f32(F * CH)
            XCBc = ar.bf16(F * CH)
            SQBc = ar.bf16(F * CH)
            stg_tok = [dict(), dict(), dict()]
            smf_free = dict()
            for ch in range(NCH):
                stok = ln_stats(st_f, ch * CH, CH, XCBc, SQBc, dict())
                tr_tok = load_transpose_x0(b, ch, STGC, SMFC, stg_tok, smf_free)
                btok = ln_bcast(CH, stok)
                p.wait("vector", btok)
                p.wait("vector", tr_tok)
                for f in range(F):
                    dst = st_f(f)[:, ch * CH:(ch + 1) * CH]
                    p.emit("vector", lambda E, dst=dst: E.tensor_mul(
                        out=SCRF, in0=dst, in1=pb[4][:, 0:CH]))
                    p.emit("vector", lambda E: E.tensor_sub(
                        out=SCRF, in0=SCRF, in1=pb[5][:, 0:CH]))
                    c = p.emit("vector", lambda E, dst=dst, f=f: E.tensor_add(
                        out=dst, in0=SCRF, in1=SMFC[:, f * CH:(f + 1) * CH]), inc="dve")
                seal45({"dve": c})
                smf_free = {"dve": c}
            p.fence()
            ar.pop()
            if bdbg:
                dbg_dump("x1", st_f)
            if upto < 5:
                continue

            ar.push()
            ACC = ar.f32(F * HALF)
            HPB = ar.bf16(F * HALF)
            WA0 = ar.bf16(F * HALF)
            WA1 = ar.bf16(F * HALF)
            WBC = ar.bf16(F * HALF)
            UALL = ar.bf16(F * QT)
            WSTG3 = ar.f32(2 * 1024)
            SMFD = ar.f32(2 * D)
            XCBd = WA0
            SQBd = WA1

            state["wslot"] = 0
            state["wtok"] = [dict(), dict()]

            for ch in range(NCH):
                stok = ln_stats(st_f, ch * CH, CH, XCBd[:, 0:F * CH], SQBd[:, 0:F * CH],
                                dict())
                btok = ln_bcast(CH, stok)
                p.wait("vector", btok)
                for f in range(F):
                    src = st_f(f)[:, ch * CH:(ch + 1) * CH]
                    p.emit("vector", lambda E, src=src: E.tensor_mul(
                        out=SCRF, in0=src, in1=pb[4][:, 0:CH]))
                    cs = p.emit("vector", lambda E: E.tensor_sub(
                        out=SCRF, in0=SCRF, in1=pb[5][:, 0:CH]), inc="dve")
                    p.force_wait("vector", "dve", cs)
                    cs = p.emit("vector", lambda E: E.reduce_sum(
                        out=VEC["ptmp"][:, 0:1], in_=SCRF, axis=AX.X), inc="dve")
                    p.force_wait("vector", "dve", cs)
                    c = p.emit("vector", lambda E, f=f: E.tensor_add(
                        out=VEC["pool8"][:, f:f + 1], in0=VEC["pool8"][:, f:f + 1],
                        in1=VEC["ptmp"][:, 0:1]), inc="dve")
                seal45({"dve": c})
            c = p.emit("vector", lambda E: E.tensor_single_scalar(
                out=VEC["pool8"], in_=VEC["pool8"], scalar=1.0 / L, op=ALU.mult),
                inc="dve")
            pool_tok = {"dve": c}

            for hk in range(2):
                tkh = mm_group(4, [(rW1[:, ki * 256 + hk * 128:ki * 256 + hk * 128 + 128],
                                    VEC["pool8"][:, ki:ki + 1]) for ki in range(F)],
                               pool_tok, bslice=pb[4][:, 0:1])
                consume(4, "scalar", lambda E, hk=hk: E.activation(
                    out=rh2[:, hk:hk + 1], in_=pb[4][:, 0:1], func=AF.Tanh,
                    bias=rb1c[:, hk:hk + 1]), tkh, "act")
            tkl = mm_group(5, [(rh2[:, hk:hk + 1], rW2[:, hk * 8:(hk + 1) * 8])
                               for hk in range(2)], p.tok("act"),
                           bslice=pb[5][0:1, 0:8])
            ltok = consume(5, "vector", lambda E: E.tensor_add(
                out=rrow[0:1, 0:8], in0=pb[5][0:1, 0:8], in1=b2row[0:1, 0:8]),
                tkl, "dve")
            p.wait("vector", ltok)

            def vstep(fn):
                c = p.emit("vector", fn, inc="dve")
                p.force_wait("vector", "dve", c)
                return c

            vstep(lambda E: E.reduce_max(out=srow1[0:1, 0:1],
                                         in_=rrow[0:1, 0:8], axis=AX.X))
            c = vstep(lambda E: E.tensor_single_scalar(
                out=rrow[0:1, 0:8], in_=rrow[0:1, 0:8], scalar=srow1[0:1, 0:1],
                op=ALU.subtract))
            p.wait("scalar", {"dve": c})
            ca = p.emit("scalar", lambda E: E.activation(
                out=rrow[0:1, 0:8], in_=rrow[0:1, 0:8], func=AF.Exp), inc="act")
            p.wait("vector", {"act": ca})
            vstep(lambda E: E.reduce_sum(out=srow1[0:1, 0:1],
                                         in_=rrow[0:1, 0:8], axis=AX.X))
            vstep(lambda E: E.reciprocal(out=srow1[0:1, 4:5],
                                         in_=srow1[0:1, 0:1]))
            vstep(lambda E: E.tensor_single_scalar(
                out=rrow[0:1, 0:8], in_=rrow[0:1, 0:8], scalar=srow1[0:1, 4:5],
                op=ALU.mult))
            vstep(lambda E: E.reduce_max(out=srow1[0:1, 1:2],
                                         in_=rrow[0:1, 0:8], axis=AX.X))
            vstep(lambda E: E.tensor_single_scalar(
                out=rrow2[0:1, 0:8], in_=rrow[0:1, 0:8], scalar=srow1[0:1, 1:2],
                op=ALU.is_ge))
            vstep(lambda E: E.scalar_tensor_tensor(
                out=rrow2[0:1, 0:8], in0=rrow2[0:1, 0:8], scalar=-1e30,
                in1=rrow[0:1, 0:8], op0=ALU.mult, op1=ALU.add))
            vstep(lambda E: E.reduce_max(out=srow1[0:1, 2:3],
                                         in_=rrow2[0:1, 0:8], axis=AX.X))
            vstep(lambda E: E.tensor_single_scalar(
                out=srow2[0:1, 0:8], in_=rrow2[0:1, 0:8], scalar=srow1[0:1, 2:3],
                op=ALU.is_ge))
            vstep(lambda E: E.scalar_tensor_tensor(
                out=rrow2[0:1, 0:8], in0=srow2[0:1, 0:8], scalar=-1e30,
                in1=rrow2[0:1, 0:8], op0=ALU.mult, op1=ALU.add))
            vstep(lambda E: E.reduce_max(out=srow1[0:1, 3:4],
                                         in_=rrow2[0:1, 0:8], axis=AX.X))
            vstep(lambda E: E.tensor_single_scalar(
                out=srow2[0:1, 0:8], in_=rrow[0:1, 0:8], scalar=srow1[0:1, 3:4],
                op=ALU.is_ge))
            vstep(lambda E: E.tensor_mul(
                out=rrow2[0:1, 0:8], in0=rrow[0:1, 0:8], in1=srow2[0:1, 0:8]))
            vstep(lambda E: E.reduce_sum(out=srow1[0:1, 0:1],
                                         in_=rrow2[0:1, 0:8], axis=AX.X))
            vstep(lambda E: E.reciprocal(out=srow1[0:1, 5:6],
                                         in_=srow1[0:1, 0:1]))
            c = vstep(lambda E: E.tensor_single_scalar(
                out=rrow2[0:1, 0:8], in_=rrow2[0:1, 0:8], scalar=srow1[0:1, 5:6],
                op=ALU.mult))
            tkw = mm_group(4, [(ones_row_f[0:1, :], rrow2[0:1, 0:8])], {"dve": c},
                           bslice=pb[4][:, 0:8])
            wbc_tok = consume(4, "vector", lambda E: E.tensor_copy(
                out=VEC["wbc"], in_=pb[4][:, 0:8]), tkw, "dve")
            if bdbg and "rdump" in dbgset:
                t_ = p.tok()
                p.wait("sync", t_)
                dma(dbg_d["rdump"][0][:, 0:8], VEC["pool8"], sem="out")
                dma(dbg_d["rdump"][1][:, 0:2], rh2, sem="out")
                dma(dbg_d["rdump"][2][0:1, 0:16], rrow[0:1, 0:16], sem="out")
                dma(dbg_d["rdump"][3][0:1, 0:16], rrow2[0:1, 0:16], sem="out")
                p.fence()
            if bdbg:
                dbg_dump("wrouter", lambda f: VEC["wbc"], nf=1)
            if upto < 6:
                p.fence()
                ar.pop()
                continue

            for hf in range(2):
                p.fence()
                ht0 = hf * HALF
                acc_tok = dict()
                cp = None
                for ch in range(2):
                    t0 = ht0 + ch * CH
                    stok = ln_stats(st_f, t0, CH, XCBd[:, 0:F * CH], SQBd[:, 0:F * CH],
                                    acc_tok)
                    btok = ln_bcast(CH, stok)
                    p.wait("vector", btok)
                    for f in range(F):
                        src = st_f(f)[:, t0:t0 + CH]
                        dst = ACC[:, f * HALF + ch * CH:f * HALF + (ch + 1) * CH]
                        p.emit("vector", lambda E, src=src, dst=dst: E.tensor_mul(
                            out=dst, in0=src, in1=pb[4][:, 0:CH]))
                        c = p.emit("vector", lambda E, dst=dst: E.tensor_sub(
                            out=dst, in0=dst, in1=pb[5][:, 0:CH]), inc="dve")
                        p.wait("gpsimd", {"dve": c})
                        cp = p.emit("gpsimd", lambda E, dst=dst, f=f, ch=ch:
                                    E.tensor_copy(out=HPB[:, f * HALF + ch * CH:
                                                          f * HALF + (ch + 1) * CH],
                                                  in_=dst), inc="pool")
                    seal45({"dve": c})
                    acc_tok = {"dve": c, "pool": cp}
                hp_tok = dict(acc_tok, **wbc_tok)

                wa_bufs = [WA0, WA1]
                wa_war = [dict(hp_tok), dict(hp_tok)]
                wb_war = dict(hp_tok)
                for ez in range(NZ):
                    for hc in range(4):
                        it = ez * 4 + hc
                        wab = wa_bufs[it % 2]
                        watok = dict()
                        for ki in range(F):
                            watok = load_cast(wab[:, ki * HALF:(ki + 1) * HALF],
                                              wa_d[ez, ki * 128:(ki + 1) * 128,
                                                   hc * HALF:(hc + 1) * HALF],
                                              WSTG3, 2, HALF, cast_wait=wa_war[it % 2])
                        wbtok = dict()
                        for hk in range(F):
                            wbtok = load_cast(WBC[:, hk * HALF:(hk + 1) * HALF],
                                              wb_d[ez, hc * HALF + hk * 128:
                                                   hc * HALF + (hk + 1) * 128, :],
                                              WSTG3, 2, HALF, cast_wait=wb_war)
                        for tw in range(2):
                            ug = dict()
                            for ht in range(F):
                                bk = 4 + (ht % 2)
                                tku = mm_group(bk, [
                                    (wab[:, ki * HALF + ht * 128:ki * HALF + ht * 128 + 128],
                                     HPB[:, ki * HALF + tw * QT:ki * HALF + (tw + 1) * QT])
                                    for ki in range(F)], dict(watok, **hp_tok))
                                ug = consume(bk, "scalar", lambda E, ht=ht, bk=bk:
                                             E.activation(out=UALL[:, ht * QT:(ht + 1) * QT],
                                                          in_=pb[bk][:, 0:QT],
                                                          func=AF.Gelu_apprx_tanh),
                                             tku, "act")
                            for fg in range(2):
                                for j in range(4):
                                    fo = fg * 4 + j
                                    tkz = mm_group(j, [
                                        (WBC[:, ht * HALF + fo * 128:ht * HALF + fo * 128 + 128],
                                         UALL[:, ht * QT:(ht + 1) * QT])
                                        for ht in range(F)], dict(wbtok, **ug))
                                    dst = ACC[:, fo * HALF + tw * QT:fo * HALF + (tw + 1) * QT]
                                    consume(j, "vector", lambda E, dst=dst, j=j, ez=ez:
                                            E.scalar_tensor_tensor(
                                                out=dst, in0=pb[j][:, 0:QT],
                                                scalar=VEC["wbc"][:, ez:ez + 1],
                                                in1=dst, op0=ALU.mult, op1=ALU.add),
                                            tkz, "dve")
                        wa_war[it % 2] = p.tok("pe")
                        wb_war = p.tok("pe")
                if bdbg and hf == 0:
                    dbg_dump("acc0", lambda f: ACC[:, f * HALF:(f + 1) * HALF])

                if upto < 7:
                    continue
                zdone = p.tok("pe", "dve")
                smfd_tok = [dict(), dict()]
                for ch in range(2):
                    t0 = ht0 + ch * CH
                    stok = ln_stats(lambda f: ACC[:, f * HALF:(f + 1) * HALF],
                                    ch * CH, CH, XCBd[:, 0:F * CH], SQBd[:, 0:F * CH],
                                    zdone)
                    btok = ln_bcast(CH, stok)
                    p.wait("vector", btok)
                    for f in range(F):
                        asrc = ACC[:, f * HALF + ch * CH:f * HALF + (ch + 1) * CH]
                        x1src = st_f(f)[:, t0:t0 + CH]
                        p.emit("vector", lambda E, asrc=asrc: E.tensor_mul(
                            out=SCRF, in0=asrc, in1=pb[4][:, 0:CH]))
                        p.emit("vector", lambda E: E.tensor_sub(
                            out=SCRF, in0=SCRF, in1=pb[5][:, 0:CH]))
                        c = p.emit("vector", lambda E, asrc=asrc, x1src=x1src:
                                   E.tensor_add(out=asrc, in0=SCRF, in1=x1src),
                                   inc="dve")
                    seal45({"dve": c})
                    och = {"dve": c}
                    for i in range(CH // 128):
                        tt = (t0 + i * 128) // 128
                        sd = i % 2
                        ot = dict()
                        for f in range(F):
                            bk = 6 + (f % 2)
                            p.wait("tensor", bank_state[bk])
                            p.wait("tensor", och)
                            cpe = p.emit("tensor", lambda E, bk=bk, f=f, ch=ch, i=i:
                                         E.transpose(pb[bk][:, 0:128],
                                                     ACC[:, f * HALF + ch * CH + i * 128:
                                                         f * HALF + ch * CH + (i + 1) * 128],
                                                     ident), inc="pe")
                            dst = SMFD[:, sd * D + f * 128:sd * D + (f + 1) * 128]
                            ot = consume(bk, "vector", lambda E, dst=dst, bk=bk:
                                         E.tensor_copy(out=dst, in_=pb[bk][:, 0:128]),
                                         dict({"pe": cpe}, **smfd_tok[sd]), "dve")
                        smfd_tok[sd] = dma(out_d[b, tt], SMFD[:, sd * D:(sd + 1) * D],
                                           sem="out", wait_tok=ot)
            p.fence()
            ar.pop()

        p.wait("gpsimd", p.tok("out"))

        block = es.enter_context(nc.Block())

        @block.sync
        def _(E):
            for fn in p.q["sync"]:
                fn(E)

        @block.gpsimd
        def _(E):
            for fn in p.q["gpsimd"]:
                fn(E)

        @block.vector
        def _(E):
            for fn in p.q["vector"]:
                fn(E)

        @block.scalar
        def _(E):
            for fn in p.q["scalar"]:
                fn(E)

        @block.tensor
        def _(E):
            for fn in p.q["tensor"]:
                fn(E)

    return nc


def kernel(**inputs):
    x = np.ascontiguousarray(inputs["L0_attn_x"])
    xs = x.reshape(NCORES, BPC, L // 128, 128, D)
    nc = build(debug_names=tuple(DBG), upto=UPTO)
    in_maps = []
    for c in range(NCORES):
        m = {"x": np.ascontiguousarray(xs[c])}
        for k, v in inputs.items():
            if k != "L0_attn_x":
                m[k] = np.ascontiguousarray(v)
        in_maps.append(m)
    res = run_bass_kernel_spmd(nc, in_maps, core_ids=list(range(NCORES)))
    outs = [np.asarray(r["out"]).reshape(BPC, L, D) for r in res.results]
    return np.concatenate(outs, axis=0)


# version 02
# speedup vs baseline: 1.0528x; 1.0528x over previous
import numpy as np
from contextlib import ExitStack
import concourse.bass as bass
import concourse.mybir as mybir
from concourse.bass_utils import run_bass_kernel_spmd

F32 = mybir.dt.float32
BF16 = mybir.dt.bfloat16
U8 = mybir.dt.uint8
AF = mybir.ActivationFunctionType
ALU = mybir.AluOpType
AX = mybir.AxisListType

NCORES = 8
BPC = 2
L = 2048
D = 1024
F = 8
DH = 4096
NZ = 8
EPS = 1e-5
CH = 512
NCH = L // CH
QT = 512
NQ = L // QT
HALF = 1024
ONES_SCALE = 1.0 / 1024.0

ENGINES = ("sync", "gpsimd", "vector", "scalar", "tensor")

DBG = ()
UPTO = 99


class Prog:
    def __init__(self):
        self.q = {e: [] for e in ENGINES}
        self.cnt = {}
        self.wm = {e: {} for e in ENGINES}
        self.sems = {}
        self.evlog = {e: [] for e in ENGINES}

    def check_deadlock(self):
        ptr = {e: 0 for e in ENGINES}
        val = {}
        progress = True
        while progress:
            progress = False
            for e in ENGINES:
                log = self.evlog[e]
                while ptr[e] < len(log):
                    kind, sem, c = log[ptr[e]]
                    if kind == "w" and val.get(sem, 0) < c:
                        break
                    if kind == "i":
                        val[sem] = val.get(sem, 0) + c
                    ptr[e] += 1
                    progress = True
        stuck = [(e, ptr[e], len(self.evlog[e]), self.evlog[e][ptr[e]])
                 for e in ENGINES if ptr[e] < len(self.evlog[e])]
        return stuck

    def tok(self, *sems):
        if not sems:
            return {s: c for s, c in self.cnt.items() if c > 0}
        return {s: self.cnt.get(s, 0) for s in sems if self.cnt.get(s, 0) > 0}

    def wait(self, eng, tokd):
        assert eng in ENGINES
        for s, c in tokd.items():
            if c > self.wm[eng].get(s, 0):
                self.wm[eng][s] = c
                self.evlog[eng].append(("w", s, c))
                self.q[eng].append(
                    lambda E, self=self, s=s, c=c: E.wait_ge(self.sems[s], c))

    def emit(self, eng, fn, inc=None, step=1):
        assert eng in ENGINES
        if inc is None:
            self.q[eng].append(fn)
            self.evlog[eng].append(("n", None, 0))
            return None
        self.cnt[inc] = self.cnt.get(inc, 0) + step
        c = self.cnt[inc]
        self.evlog[eng].append(("i", inc, step))
        self.q[eng].append(
            lambda E, self=self, fn=fn, inc=inc, step=step:
                fn(E).then_inc(self.sems[inc], step))
        self.wm[eng][inc] = c
        return c

    def force_wait(self, eng, sem, c):
        self.evlog[eng].append(("w", sem, c))
        self.q[eng].append(
            lambda E, self=self, sem=sem, c=c: E.wait_ge(self.sems[sem], c))
        if c > self.wm[eng].get(sem, 0):
            self.wm[eng][sem] = c

    def fence(self):
        t = self.tok()
        for e in ENGINES:
            self.wait(e, t)


class Arena:
    def __init__(self, t, nbytes):
        self.t = t
        self.nbytes = nbytes
        self.off = 0
        self.marks = []
        self.peak = 0

    def push(self):
        self.marks.append(self.off)

    def pop(self):
        self.off = self.marks.pop()

    def _take(self, nbytes):
        o = self.off
        assert o % 4 == 0
        self.off += nbytes + ((4 - nbytes % 4) % 4)
        self.peak = max(self.peak, self.off)
        assert self.off <= self.nbytes, f"arena overflow {self.off} > {self.nbytes}"
        return o

    def f32(self, cols):
        o = self._take(4 * cols)
        return self.t[:, o:o + 4 * cols].bitcast(F32)

    def bf16(self, cols):
        o = self._take(2 * cols)
        return self.t[:, o:o + 2 * cols].bitcast(BF16)

    def f32_at(self, o, cols):
        return self.t[:, o:o + 4 * cols].bitcast(F32)

    def bf16_at(self, o, cols):
        return self.t[:, o:o + 2 * cols].bitcast(BF16)


def build(debug_names=(), upto=99):
    nc = bass.Bass("TRN2", target_bir_lowering=False)
    p = Prog()
    dbgset = {t[0] for t in debug_names}

    x_d = nc.dram_tensor("x", [BPC, L // 128, 128, D], F32, kind="ExternalInput").ap()
    out_d = nc.dram_tensor("out", [BPC, L // 128, 128, D], F32, kind="ExternalOutput").ap()
    vnames = {
        "lag": "L0_attn_ln_att_g", "lab": "L0_attn_ln_att_b",
        "lfg": "L0_attn_ln_ffn_g", "lfb": "L0_attn_ln_ffn_b",
        "amk": "L0_attn_att_mix_k", "amv": "L0_attn_att_mix_v",
        "amr": "L0_attn_att_mix_r", "fmk": "L0_attn_ffn_mix_k",
        "fmr": "L0_attn_ffn_mix_r", "td": "L0_attn_att_time_decay",
        "tf": "L0_attn_att_time_first"}
    vec_d = {k: nc.dram_tensor(n, [D], F32, kind="ExternalInput").ap()
             for k, n in vnames.items()}
    wk_d = nc.dram_tensor("L0_attn_att_wk", [D, D], F32, kind="ExternalInput").ap()
    wv_d = nc.dram_tensor("L0_attn_att_wv", [D, D], F32, kind="ExternalInput").ap()
    wr_d = nc.dram_tensor("L0_attn_att_wr", [D, D], F32, kind="ExternalInput").ap()
    wo_d = nc.dram_tensor("L0_attn_att_wo", [D, D], F32, kind="ExternalInput").ap()
    fwk_d = nc.dram_tensor("L0_attn_ffn_wk", [D, DH], F32, kind="ExternalInput").ap()
    fwv_d = nc.dram_tensor("L0_attn_ffn_wv", [DH, D], F32, kind="ExternalInput").ap()
    fwr_d = nc.dram_tensor("L0_attn_ffn_wr", [D, D], F32, kind="ExternalInput").ap()
    rW1_d = nc.dram_tensor("L0_ffn_W1", [D, 256], F32, kind="ExternalInput").ap()
    rb1_d = nc.dram_tensor("L0_ffn_b1", [256], F32, kind="ExternalInput").ap()
    rW2_d = nc.dram_tensor("L0_ffn_W2", [256, NZ], F32, kind="ExternalInput").ap()
    rb2_d = nc.dram_tensor("L0_ffn_b2", [NZ], F32, kind="ExternalInput").ap()
    wa_d = nc.dram_tensor("L0_ffn_Wa", [NZ, D, DH], F32, kind="ExternalInput").ap()
    wb_d = nc.dram_tensor("L0_ffn_Wb", [NZ, DH, D], F32, kind="ExternalInput").ap()
    dbg_d = {}
    for t in debug_names:
        n, shp = t[0], t[1]
        dt = t[2] if len(t) > 2 else F32
        dbg_d[n] = nc.dram_tensor("dbg_" + n, shp, dt, kind="ExternalOutput").ap()

    es = ExitStack()
    with es:
        AR_BYTES = 206 * 1024
        ar_t = es.enter_context(nc.sbuf_tensor("arena", [128, AR_BYTES], U8))
        ar = Arena(ar_t, AR_BYTES)
        pb = [es.enter_context(nc.psum_tensor(f"pb{i}", [128, 512], F32))
              for i in range(8)]
        for s in ("in", "out", "pe", "dve", "act", "pool"):
            p.sems[s] = es.enter_context(nc.semaphore("s_" + s))

        ident = ar.f32(128)
        ones_col_b = ar.bf16(2)
        ones_row_f = ar.f32(128)
        VEC = {}
        for n in ("lag", "lab", "lfg", "lfb", "amk", "amv", "amr", "fmk", "fmr",
                  "td", "tf", "ew", "eu", "wbc", "pcar", "qcar", "pool8", "ptmp"):
            VEC[n] = ar.f32(8)
        hcar = ar.bf16(8)
        rb1c = ar.f32(2)
        rh2 = ar.f32(2)
        rW2 = ar.f32(16)
        cEPS = ar.f32(1)
        nc.const_aps.register(F32, EPS, cEPS[:, 0:1])
        m_row = ar.f32(512)
        rs_row = ar.f32(512)
        mrs_row = ar.f32(512)
        srow1 = ar.f32(512)
        srow2 = ar.f32(512)
        rrow = ar.f32(16)
        rrow2 = ar.f32(16)
        SCRF = ar.f32(512)
        SCRB = SCRF.bitcast(BF16)

        bank_state = [dict() for _ in range(8)]
        state = {"lnmm": {}, "wslot": 0, "wtok": [dict(), dict()]}

        def mm_group(bank, pairs, opd_tok, bslice=None, free=512):
            p.wait("tensor", bank_state[bank])
            p.wait("tensor", opd_tok)
            tgt = pb[bank][:, 0:free] if bslice is None else bslice
            n = len(pairs)
            c = None
            for i, (lt, rh) in enumerate(pairs):
                if i < n - 1:
                    p.emit("tensor", lambda E, tgt=tgt, lt=lt, rh=rh, i=i:
                           E.matmul(tgt, lt, rh, start=(i == 0), stop=False))
                else:
                    c = p.emit("tensor", lambda E, tgt=tgt, lt=lt, rh=rh, i=i:
                               E.matmul(tgt, lt, rh, start=(i == 0), stop=True),
                               inc="pe")
            return {"pe": c}

        def consume(bank, eng, fn, tokd, sem):
            p.wait(eng, tokd)
            c = p.emit(eng, fn, inc=sem)
            bank_state[bank] = {sem: c}
            return {sem: c}

        def dma(dst, src, sem="in", wait_tok=None):
            if wait_tok:
                p.wait("sync", wait_tok)
            return {sem: p.emit("sync", lambda E, dst=dst, src=src:
                                E.dma_start(out=dst, in_=src), inc=sem, step=16)}

        def load_cast(dst, dram_src, wstg, nst, cols, cast_wait=None, scale_col=None):
            s = state["wslot"] % nst
            state["wslot"] += 1
            while len(state["wtok"]) < nst:
                state["wtok"].append(dict())
            stg = wstg[:, s * 1024:s * 1024 + cols]
            t = dma(stg, dram_src, wait_tok=state["wtok"][s])
            p.wait("gpsimd", t)
            if cast_wait:
                p.wait("gpsimd", cast_wait)
            if scale_col is None:
                c = p.emit("gpsimd", lambda E, dst=dst, stg=stg:
                           E.tensor_copy(out=dst, in_=stg), inc="pool")
            else:
                c = p.emit("gpsimd", lambda E, dst=dst, stg=stg, sc=scale_col:
                           E.tensor_single_scalar(out=dst, in_=stg, scalar=sc,
                                                  op=ALU.mult), inc="pool")
            state["wtok"][s] = {"pool": c}
            return {"pool": c}

        def mk_stager(wstg_ap, slot_cols, nslots):
            stt = {"i": 0, "tok": [dict() for _ in range(nslots)]}

            def lc(dst, dram_src, cols, eng="vector", cast_wait=None):
                s = stt["i"] % nslots
                stt["i"] += 1
                stg = wstg_ap[:, s * slot_cols:s * slot_cols + cols]
                t = dma(stg, dram_src, wait_tok=stt["tok"][s])
                sem = {"vector": "dve", "scalar": "act", "gpsimd": "pool"}[eng]
                p.wait(eng, t)
                if cast_wait:
                    p.wait(eng, cast_wait)
                if eng == "scalar":
                    c = p.emit("scalar", lambda E, dst=dst, stg=stg:
                               E.activation(out=dst, in_=stg, func=AF.Copy),
                               inc="act")
                else:
                    c = p.emit(eng, lambda E, dst=dst, stg=stg:
                               E.tensor_copy(out=dst, in_=stg), inc=sem)
                stt["tok"][s] = {sem: c}
                return {sem: c}
            return lc

        p.emit("gpsimd", lambda E: E.memset(ident, 1.0))
        p.emit("gpsimd", lambda E: E.affine_select(
            out=ident, in_=ident, pattern=[[-1, 128]], compare_op=ALU.is_equal,
            fill=0.0, base=0, channel_multiplier=1))
        p.emit("gpsimd", lambda E: E.memset(ones_col_b[:, 0:1], ONES_SCALE))
        p.emit("gpsimd", lambda E: E.memset(ones_col_b[:, 1:2], 1.0))
        p.emit("gpsimd", lambda E: E.memset(cEPS, EPS))
        c = p.emit("gpsimd", lambda E: E.memset(ones_row_f[0:1, :], 1.0), inc="pool")
        const_tok = {"pool": c}
        for e in ENGINES:
            p.wait(e, const_tok)

        vtok = {}
        for key in vnames:
            src = vec_d[key].rearrange("(f p o) -> f p o", p=128, o=1)
            for f in range(F):
                vtok = dma(VEC[key][:, f:f + 1], src[f])
        W2r = rW2_d.rearrange("(k p) z -> k p z", p=128)
        for k in range(2):
            vtok = dma(rW2[:, k * 8:(k + 1) * 8], W2r[k])
        b1r = rb1_d.rearrange("(k p o) -> k p o", p=128, o=1)
        for k in range(2):
            vtok = dma(rb1c[:, k:k + 1], b1r[k])
        b2row = ar.f32(8)
        vtok = dma(b2row[0:1, 0:8], rb2_d.rearrange("(o z) -> o z", o=1))

        p.wait("scalar", vtok)
        p.emit("scalar", lambda E: E.activation(out=VEC["ew"], in_=VEC["td"], func=AF.Exp))
        p.emit("scalar", lambda E: E.activation(out=VEC["ew"], in_=VEC["ew"], func=AF.Exp, scale=-1.0))
        c = p.emit("scalar", lambda E: E.activation(out=VEC["eu"], in_=VEC["tf"], func=AF.Exp), inc="act")
        ew_tok = {"act": c}

        STREAM = ar.f32(F * L)

        def st_f(f):
            return STREAM[:, f * L:(f + 1) * L]

        def ln_stats(src_f, t0, n, xcb, sqb, scratch_tok):
            p.wait("gpsimd", scratch_tok)
            p.wait("gpsimd", state["lnmm"])
            p.wait("scalar", scratch_tok)
            p.wait("scalar", state["lnmm"])
            for f in range(F):
                src = src_f(f)[:, t0:t0 + n]
                p.emit("gpsimd", lambda E, d=xcb[:, f * n:(f + 1) * n], s=src:
                       E.tensor_copy(out=d, in_=s), inc="pool")
                p.emit("scalar", lambda E, d=sqb[:, f * n:(f + 1) * n], s=src:
                       E.activation(out=d, in_=s, func=AF.Square), inc="act")
            cast_tok = p.tok("pool", "act")
            tokS = mm_group(6, [(ones_col_b[:, 0:1], xcb[:, f * n:(f + 1) * n])
                                for f in range(F)], cast_tok, bslice=pb[6][0:1, 0:n])
            tokQ = mm_group(7, [(ones_col_b[:, 0:1], sqb[:, f * n:(f + 1) * n])
                                for f in range(F)], cast_tok, bslice=pb[7][0:1, 0:n])
            state["lnmm"] = {"pe": tokQ["pe"]}
            tokm = consume(6, "vector", lambda E: E.tensor_copy(
                out=m_row[0:1, 0:n], in_=pb[6][0:1, 0:n]), tokS, "dve")
            p.wait("scalar", tokm)
            ca = p.emit("scalar", lambda E: E.activation(
                out=srow1[0:1, 0:n], in_=m_row[0:1, 0:n], func=AF.Square), inc="act")
            tq = dict(tokQ)
            tq["act"] = ca
            tokv = consume(7, "vector", lambda E: E.tensor_sub(
                out=srow2[0:1, 0:n], in0=pb[7][0:1, 0:n], in1=srow1[0:1, 0:n]),
                tq, "dve")
            p.wait("scalar", tokv)
            p.emit("scalar", lambda E: E.activation(
                out=rs_row[0:1, 0:n], in_=srow2[0:1, 0:n], func=AF.Ln, bias=EPS))
            ca = p.emit("scalar", lambda E: E.activation(
                out=rs_row[0:1, 0:n], in_=rs_row[0:1, 0:n], func=AF.Exp, scale=-0.5),
                inc="act")
            p.wait("vector", {"act": ca})
            p.wait("vector", state["lnmm"])
            cd = p.emit("vector", lambda E: E.tensor_mul(
                out=mrs_row[0:1, 0:n], in0=m_row[0:1, 0:n], in1=rs_row[0:1, 0:n]),
                inc="dve")
            return {"dve": cd}

        def ln_bcast(n, rows_tok):
            mm_group(4, [(ones_row_f[0:1, :], rs_row[0:1, 0:n])], rows_tok,
                     bslice=pb[4][:, 0:n])
            t2 = mm_group(5, [(ones_row_f[0:1, :], mrs_row[0:1, 0:n])], rows_tok,
                          bslice=pb[5][:, 0:n])
            state["lnmm"] = {"pe": t2["pe"]}
            return {"pe": t2["pe"]}

        def seal45(tokd):
            bank_state[4] = dict(tokd)
            bank_state[5] = dict(tokd)

        def dbg_dump(name, src_fn, nf=F):
            if name not in dbgset:
                return
            t = p.tok()
            p.wait("sync", t)
            for f in range(nf):
                dma(dbg_d[name][f], src_fn(f), sem="out")
            p.fence()

        def load_transpose_x0(b, ch, STGx, SMFx, stg_tok, smf_free):
            ct = dict()
            for i in range(CH // 128):
                tt = ch * (CH // 128) + i
                s = (ch * (CH // 128) + i) % 3
                t = dma(STGx[:, s * 1024:(s + 1) * 1024], x_d[b, tt],
                        wait_tok=stg_tok[s])
                for f in range(F):
                    bk = 6 + (f % 2)
                    p.wait("tensor", bank_state[bk])
                    p.wait("tensor", t)
                    cpe = p.emit("tensor", lambda E, bk=bk, s=s, f=f, STGx=STGx:
                                 E.transpose(pb[bk][:, 0:128],
                                             STGx[:, s * 1024 + f * 128:s * 1024 + (f + 1) * 128],
                                             ident), inc="pe")
                    dst = SMFx[:, f * CH + i * 128:f * CH + (i + 1) * 128]
                    ct = consume(bk, "vector", lambda E, dst=dst, bk=bk:
                                 E.tensor_copy(out=dst, in_=pb[bk][:, 0:128]),
                                 dict({"pe": cpe}, **smf_free), "dve")
                stg_tok[s] = {"pe": cpe}
            return ct

        for b in range(BPC):
            bdbg = (b == 0)
            p.fence()
            p.emit("gpsimd", lambda E: E.memset(VEC["pcar"], 0.0))
            p.emit("gpsimd", lambda E: E.memset(VEC["qcar"], 0.0))
            p.emit("gpsimd", lambda E: E.memset(hcar, 0.0))
            c = p.emit("gpsimd", lambda E: E.memset(VEC["pool8"], 0.0), inc="pool")
            p.fence()

            ar.push()
            STG = ar.f32(3 * 1024)
            SMF = ar.f32(F * CH)
            XCB = ar.bf16(F * CH)
            SQB = ar.bf16(F * CH)
            stg_tok = [dict(), dict(), dict()]
            smf_free = dict(const_tok)
            for ch in range(NCH):
                tr_tok = load_transpose_x0(b, ch, STG, SMF, stg_tok, smf_free)
                stok = ln_stats(lambda f: SMF[:, f * CH:(f + 1) * CH], 0, CH,
                                XCB, SQB, tr_tok)
                btok = ln_bcast(CH, stok)
                p.wait("vector", btok)
                p.wait("vector", tr_tok)
                for f in range(F):
                    dst = st_f(f)[:, ch * CH:(ch + 1) * CH]
                    src = SMF[:, f * CH:(f + 1) * CH]
                    p.emit("vector", lambda E, dst=dst, src=src: E.tensor_mul(
                        out=dst, in0=src, in1=pb[4][:, 0:CH]))
                    c = p.emit("vector", lambda E, dst=dst: E.tensor_sub(
                        out=dst, in0=dst, in1=pb[5][:, 0:CH]), inc="dve")
                seal45({"dve": c})
                smf_free = dict(p.tok("pool", "act"), dve=c)
            p.fence()
            ar.pop()
            if bdbg:
                dbg_dump("h0", st_f)
            if upto < 2:
                continue

            ar.push()
            WK = ar.bf16(F * D)
            WV = ar.bf16(F * D)
            WR = ar.bf16(F * D)
            WSTG = ar.f32(2 * 1024)
            HBUF = ar.bf16(F * 516)
            MIXK = ar.bf16(F * QT)
            MIXV = ar.bf16(F * QT)
            MIXR = ar.bf16(F * QT)
            RSIG = ar.bf16(F * QT)
            REG = ar._take(18944)
            XCB2 = ar.bf16_at(REG, F * QT)
            SQB2 = ar.bf16_at(REG + 2 * F * QT, F * QT)
            RW = ar.bf16_at(REG, F * QT)
            EK = ar.f32_at(REG + 8192, QT)
            VV = ar.f32_at(REG + 10240, QT)
            BP = ar.f32_at(REG + 12288, QT)
            PBUF = ar.f32_at(REG + 14336, QT + 4)
            QBUF = ar.f32_at(REG + 16400, QT + 4)

            def hb(f):
                return HBUF[:, f * 516:(f + 1) * 516]

            state["wslot"] = 0
            state["wtok"] = [dict(), dict()]
            wkr = wk_d.rearrange("(k p) o -> k p o", p=128)
            wvr = wv_d.rearrange("(k p) o -> k p o", p=128)
            wrr = wr_d.rearrange("(k p) o -> k p o", p=128)
            wor = wo_d.rearrange("(k p) o -> k p o", p=128)
            for ki in range(F):
                load_cast(WK[:, ki * D:(ki + 1) * D], wkr[ki], WSTG, 2, D)
            for ki in range(F):
                load_cast(WV[:, ki * D:(ki + 1) * D], wvr[ki], WSTG, 2, D)
            wtok = dict()
            for ki in range(F):
                wtok = load_cast(WR[:, ki * D:(ki + 1) * D], wrr[ki], WSTG, 2, D)

            prev_wo_pe = dict()
            chain_tok = dict(ew_tok)
            for q in range(NQ):
                t0 = q * QT
                stok = ln_stats(st_f, t0, QT, XCB2, SQB2, prev_wo_pe)
                btok = ln_bcast(QT, stok)
                p.wait("vector", btok)
                p.wait("vector", prev_wo_pe)
                for f in range(F):
                    src = st_f(f)[:, t0:t0 + QT]
                    p.emit("vector", lambda E, src=src: E.tensor_mul(
                        out=SCRF, in0=src, in1=pb[4][:, 0:QT]))
                    p.emit("vector", lambda E: E.tensor_sub(
                        out=SCRF, in0=SCRF, in1=pb[5][:, 0:QT]))
                    p.emit("vector", lambda E, f=f: E.tensor_copy(
                        out=hb(f)[:, 0:1], in_=hcar[:, f:f + 1]))
                    cs = p.emit("vector", lambda E, f=f: E.tensor_scalar(
                        out=hb(f)[:, 1:513], in0=SCRF, scalar1=VEC["lag"][:, f:f + 1],
                        scalar2=VEC["lab"][:, f:f + 1], op0=ALU.mult, op1=ALU.add),
                        inc="dve")
                    p.force_wait("vector", "dve", cs)
                    c = p.emit("vector", lambda E, f=f: E.tensor_copy(
                        out=hcar[:, f:f + 1], in_=hb(f)[:, 512:513]), inc="dve")
                seal45({"dve": c})
                for f in range(F):
                    p.emit("vector", lambda E, f=f: E.tensor_sub(
                        out=SCRB[:, 0:QT], in0=hb(f)[:, 1:513], in1=hb(f)[:, 0:512]))
                    for mx, buf in (("amk", MIXK), ("amv", MIXV), ("amr", MIXR)):
                        p.emit("vector", lambda E, f=f, mx=mx: E.tensor_single_scalar(
                            out=SCRB[:, QT:2 * QT], in_=SCRB[:, 0:QT],
                            scalar=VEC[mx][:, f:f + 1], op=ALU.mult))
                        c = p.emit("vector", lambda E, f=f, buf=buf: E.tensor_add(
                            out=buf[:, f * QT:(f + 1) * QT], in0=SCRB[:, QT:2 * QT],
                            in1=hb(f)[:, 0:512]), inc="dve")
                mix_tok = dict(wtok, dve=c)

                rs_tok = {}
                rsig_war = p.tok("dve")
                for fo in range(F):
                    bk = fo % 4
                    tkr = mm_group(bk, [(WR[:, ki * D + fo * 128:ki * D + fo * 128 + 128],
                                         MIXR[:, ki * QT:(ki + 1) * QT]) for ki in range(F)],
                                   mix_tok)
                    rs_tok = consume(bk, "scalar", lambda E, fo=fo, bk=bk: E.activation(
                        out=RSIG[:, fo * QT:(fo + 1) * QT], in_=pb[bk][:, 0:QT],
                        func=AF.Sigmoid), dict(tkr, **rsig_war), "act")

                for fo in range(F):
                    bk_k = fo % 2
                    bk_v = 2 + (fo % 2)
                    tkk = mm_group(bk_k, [(WK[:, ki * D + fo * 128:ki * D + fo * 128 + 128],
                                           MIXK[:, ki * QT:(ki + 1) * QT]) for ki in range(F)],
                                   mix_tok)
                    tkv = mm_group(bk_v, [(WV[:, ki * D + fo * 128:ki * D + fo * 128 + 128],
                                           MIXV[:, ki * QT:(ki + 1) * QT]) for ki in range(F)],
                                   mix_tok)
                    ek_tok = consume(bk_k, "scalar", lambda E, bk=bk_k: E.activation(
                        out=EK, in_=pb[bk][:, 0:QT], func=AF.Exp),
                        dict(tkk, **chain_tok), "act")
                    vv_tok = consume(bk_v, "scalar", lambda E, bk=bk_v: E.activation(
                        out=VV, in_=pb[bk][:, 0:QT], func=AF.Copy),
                        dict(tkv, **chain_tok), "act")
                    p.wait("vector", ek_tok)
                    p.wait("vector", vv_tok)
                    p.emit("vector", lambda E: E.tensor_mul(out=BP, in0=EK, in1=VV))
                    p.emit("vector", lambda E, f=fo: E.tensor_copy(
                        out=PBUF[:, 0:1], in_=VEC["pcar"][:, f:f + 1]))
                    cs = p.emit("vector", lambda E, f=fo: E.tensor_tensor_scan(
                        out=PBUF[:, 1:513], data0=VEC["ew"][:, f:f + 1].broadcast_to([128, QT]),
                        data1=BP, initial=VEC["pcar"][:, f:f + 1],
                        op0=ALU.mult, op1=ALU.add), inc="dve")
                    p.force_wait("vector", "dve", cs)
                    p.emit("vector", lambda E, f=fo: E.tensor_copy(
                        out=VEC["pcar"][:, f:f + 1], in_=PBUF[:, 512:513]))
                    p.emit("vector", lambda E, f=fo: E.tensor_copy(
                        out=QBUF[:, 0:1], in_=VEC["qcar"][:, f:f + 1]))
                    cs = p.emit("vector", lambda E, f=fo: E.tensor_tensor_scan(
                        out=QBUF[:, 1:513], data0=VEC["ew"][:, f:f + 1].broadcast_to([128, QT]),
                        data1=EK, initial=VEC["qcar"][:, f:f + 1],
                        op0=ALU.mult, op1=ALU.add), inc="dve")
                    p.force_wait("vector", "dve", cs)
                    p.emit("vector", lambda E, f=fo: E.tensor_copy(
                        out=VEC["qcar"][:, f:f + 1], in_=QBUF[:, 512:513]))
                    p.emit("vector", lambda E, f=fo: E.scalar_tensor_tensor(
                        out=BP, in0=BP, scalar=VEC["eu"][:, f:f + 1], in1=PBUF[:, 0:512],
                        op0=ALU.mult, op1=ALU.add))
                    p.emit("vector", lambda E, f=fo: E.scalar_tensor_tensor(
                        out=EK, in0=EK, scalar=VEC["eu"][:, f:f + 1], in1=QBUF[:, 0:512],
                        op0=ALU.mult, op1=ALU.add))
                    cs = p.emit("vector", lambda E: E.reciprocal(out=VV, in_=EK),
                                inc="dve")
                    p.force_wait("vector", "dve", cs)
                    p.emit("vector", lambda E: E.tensor_mul(out=BP, in0=BP, in1=VV))
                    p.wait("vector", rs_tok)
                    c = p.emit("vector", lambda E, fo=fo: E.tensor_mul(
                        out=RW[:, fo * QT:(fo + 1) * QT], in0=BP,
                        in1=RSIG[:, fo * QT:(fo + 1) * QT]), inc="dve")
                    chain_tok = {"dve": c}
                    if "wkv1" in dbgset and b == 0 and q == 1 and fo == 0:
                        t_ = p.tok()
                        p.wait("sync", t_)
                        dma(dbg_d["wkv1"][0], PBUF, sem="out")
                        dma(dbg_d["wkv1"][1], QBUF, sem="out")
                        dma(dbg_d["wkv1"][2][:, 0:QT], BP, sem="out")
                        dma(dbg_d["wkv1"][3][:, 0:QT], EK, sem="out")
                        p.fence()
                    if "wkv0" in dbgset and b == 0 and q == 0 and fo == 0:
                        t_ = p.tok()
                        p.wait("sync", t_)
                        dma(dbg_d["wkv0"][0], PBUF, sem="out")
                        dma(dbg_d["wkv0"][1], QBUF, sem="out")
                        dma(dbg_d["wkv0"][2][:, 0:8], VEC["pcar"], sem="out")
                        dma(dbg_d["wkv0"][3][:, 0:8], VEC["qcar"], sem="out")
                        p.fence()
                rw_tok = {"dve": c}

                kv_pe = p.tok("pe")
                wo_tok = dict()
                for ki in range(F):
                    dst = (MIXK[:, ki * D:(ki + 1) * D] if ki < 4
                           else MIXV[:, (ki - 4) * D:(ki - 3) * D])
                    wo_tok = load_cast(dst, wor[ki], WSTG, 2, D, cast_wait=kv_pe)

                def wo_tile(ki, fo):
                    if ki < 4:
                        return MIXK[:, ki * D + fo * 128:ki * D + fo * 128 + 128]
                    return MIXV[:, (ki - 4) * D + fo * 128:(ki - 4) * D + fo * 128 + 128]

                for fo in range(F):
                    bk = fo % 4
                    tko = mm_group(bk, [(wo_tile(ki, fo), RW[:, ki * QT:(ki + 1) * QT])
                                        for ki in range(F)], dict(rw_tok, **wo_tok))
                    dst = st_f(fo)[:, t0:t0 + QT]
                    consume(bk, "vector", lambda E, dst=dst, bk=bk: E.tensor_add(
                        out=dst, in0=dst, in1=pb[bk][:, 0:QT]), tko, "dve")
                prev_wo_pe = p.tok("pe")
            p.fence()
            ar.pop()
            if bdbg:
                dbg_dump("xa_att", st_f)
            if upto < 3:
                continue

            ar.push()
            FWR = ar.bf16(F * D)
            WKC = ar.bf16(F * CH)
            WVR = ar.bf16(4 * 1024)
            WSTG2 = ar.f32(2 * 1024)
            H2BUF = ar.bf16(F * 516)
            KFIN = ar.bf16(F * QT)
            MIXR2 = ar.bf16(F * QT)
            RFSIG = ar.bf16(F * QT)
            UFULL = ar.bf16(32 * QT)

            def h2(f):
                return H2BUF[:, f * 516:(f + 1) * 516]

            state["wslot"] = 0
            state["wtok"] = [dict(), dict()]
            p.emit("gpsimd", lambda E: E.memset(hcar, 0.0), inc="pool")
            fwrr = fwr_d.rearrange("(k p) o -> k p o", p=128)
            fwr_tok = dict()
            for ki in range(F):
                fwr_tok = load_cast(FWR[:, ki * D:(ki + 1) * D], fwrr[ki], WSTG2, 2, D)

            prev_q_pe = dict()
            for q in range(NQ):
                t0 = q * QT
                stok = ln_stats(st_f, t0, QT, UFULL[:, 0:F * QT],
                                UFULL[:, 2 * F * QT:3 * F * QT], prev_q_pe)
                btok = ln_bcast(QT, stok)
                p.wait("vector", btok)
                p.wait("vector", prev_q_pe)
                for f in range(F):
                    src = st_f(f)[:, t0:t0 + QT]
                    p.emit("vector", lambda E, src=src: E.tensor_mul(
                        out=SCRF, in0=src, in1=pb[4][:, 0:QT]))
                    p.emit("vector", lambda E: E.tensor_sub(
                        out=SCRF, in0=SCRF, in1=pb[5][:, 0:QT]))
                    p.emit("vector", lambda E, f=f: E.tensor_copy(
                        out=h2(f)[:, 0:1], in_=hcar[:, f:f + 1]))
                    cs = p.emit("vector", lambda E, f=f: E.tensor_scalar(
                        out=h2(f)[:, 1:513], in0=SCRF, scalar1=VEC["lfg"][:, f:f + 1],
                        scalar2=VEC["lfb"][:, f:f + 1], op0=ALU.mult, op1=ALU.add),
                        inc="dve")
                    p.force_wait("vector", "dve", cs)
                    c = p.emit("vector", lambda E, f=f: E.tensor_copy(
                        out=hcar[:, f:f + 1], in_=h2(f)[:, 512:513]), inc="dve")
                seal45({"dve": c})
                for f in range(F):
                    p.emit("vector", lambda E, f=f: E.tensor_sub(
                        out=SCRB[:, 0:QT], in0=h2(f)[:, 1:513], in1=h2(f)[:, 0:512]))
                    for mx, buf in (("fmk", KFIN), ("fmr", MIXR2)):
                        p.emit("vector", lambda E, f=f, mx=mx: E.tensor_single_scalar(
                            out=SCRB[:, QT:2 * QT], in_=SCRB[:, 0:QT],
                            scalar=VEC[mx][:, f:f + 1], op=ALU.mult))
                        c = p.emit("vector", lambda E, f=f, buf=buf: E.tensor_add(
                            out=buf[:, f * QT:(f + 1) * QT], in0=SCRB[:, QT:2 * QT],
                            in1=h2(f)[:, 0:512]), inc="dve")
                mix_tok = dict(fwr_tok, dve=c)

                rf_tok = {}
                rf_war = p.tok("dve")
                for fo in range(F):
                    bk = fo % 4
                    tkr = mm_group(bk, [(FWR[:, ki * D + fo * 128:ki * D + fo * 128 + 128],
                                         MIXR2[:, ki * QT:(ki + 1) * QT]) for ki in range(F)],
                                   mix_tok)
                    rf_tok = consume(bk, "scalar", lambda E, fo=fo, bk=bk: E.activation(
                        out=RFSIG[:, fo * QT:(fo + 1) * QT], in_=pb[bk][:, 0:QT],
                        func=AF.Sigmoid), dict(tkr, **rf_war), "act")

                u_tok = dict()
                wkc_war = dict(prev_q_pe)
                for hc in range(DH // CH):
                    wk_tok = dict()
                    for ki in range(F):
                        wk_tok = load_cast(WKC[:, ki * CH:(ki + 1) * CH],
                                           fwk_d[ki * 128:(ki + 1) * 128,
                                                 hc * CH:(hc + 1) * CH],
                                           WSTG2, 2, CH, cast_wait=wkc_war)
                    for ht in range(CH // 128):
                        bk = 4 + (ht % 2)
                        hk = hc * 4 + ht
                        tku = mm_group(bk, [(WKC[:, ki * CH + ht * 128:ki * CH + ht * 128 + 128],
                                             KFIN[:, ki * QT:(ki + 1) * QT]) for ki in range(F)],
                                       dict(mix_tok, **wk_tok))
                        tmax = consume(bk, "vector", lambda E, bk=bk: E.tensor_single_scalar(
                            out=SCRF, in_=pb[bk][:, 0:QT], scalar=0.0, op=ALU.max),
                            tku, "dve")
                        p.wait("scalar", tmax)
                        ca = p.emit("scalar", lambda E, hk=hk: E.activation(
                            out=UFULL[:, hk * QT:(hk + 1) * QT], in_=SCRF,
                            func=AF.Square), inc="act")
                        p.wait("vector", {"act": ca})
                        u_tok = {"act": ca}
                    wkc_war = p.tok("pe")

                if "bdump" in dbgset and b == 0 and q == 0:
                    t_ = p.tok()
                    p.wait("sync", t_)
                    dma(dbg_d["bdump"][0][:, 0:F * QT], KFIN, sem="out")
                    dma(dbg_d["bdump"][1][:, 0:F * QT], RFSIG, sem="out")
                    dma(dbg_d["bdump"][2][:, 0:F * QT], MIXR2, sem="out")
                    dma(dbg_d["bdump"][3], UFULL, sem="out")
                    p.fence()
                for fo in range(F):
                    p.wait("tensor", bank_state[fo])
                ring_tok = [dict() for _ in range(4)]
                ztoks = {}
                for hk in range(32):
                    s = hk % 4
                    wvt = load_cast(WVR[:, s * 1024:(s + 1) * 1024], fwv_d[hk * 128:(hk + 1) * 128, :],
                                    WSTG2, 2, D, cast_wait=ring_tok[s])
                    p.wait("tensor", dict(wvt, **u_tok))
                    for fo in range(F):
                        last = (hk == 31)
                        cpe = p.emit("tensor", lambda E, s=s, fo=fo, hk=hk:
                                     E.matmul(pb[fo][:, 0:QT],
                                              WVR[:, s * 1024 + fo * 128:s * 1024 + fo * 128 + 128],
                                              UFULL[:, hk * QT:(hk + 1) * QT],
                                              start=(hk == 0), stop=(hk == 31)),
                                     inc="pe" if (last or fo == F - 1) else None)
                        if last:
                            ztoks[fo] = {"pe": cpe}
                    ring_tok[s] = {"pe": p.cnt.get("pe", 0)}
                for fo in range(F):
                    dst = st_f(fo)[:, t0:t0 + QT]
                    p.wait("vector", ztoks[fo])
                    p.wait("vector", rf_tok)
                    p.emit("vector", lambda E, fo=fo: E.tensor_mul(
                        out=SCRF, in0=pb[fo][:, 0:QT],
                        in1=RFSIG[:, fo * QT:(fo + 1) * QT]))
                    c = p.emit("vector", lambda E, dst=dst: E.tensor_add(
                        out=dst, in0=dst, in1=SCRF), inc="dve")
                    bank_state[fo] = {"dve": c}
                prev_q_pe = p.tok("pe")
            p.fence()
            ar.pop()
            if bdbg:
                dbg_dump("xa_ffn", st_f)
            if upto < 4:
                continue

            ar.push()
            STGC = ar.f32(3 * 1024)
            SMFC = ar.f32(F * CH)
            XCBc = ar.bf16(F * CH)
            SQBc = ar.bf16(F * CH)
            stg_tok = [dict(), dict(), dict()]
            smf_free = dict()
            for ch in range(NCH):
                stok = ln_stats(st_f, ch * CH, CH, XCBc, SQBc, dict())
                tr_tok = load_transpose_x0(b, ch, STGC, SMFC, stg_tok, smf_free)
                btok = ln_bcast(CH, stok)
                p.wait("vector", btok)
                p.wait("vector", tr_tok)
                for f in range(F):
                    dst = st_f(f)[:, ch * CH:(ch + 1) * CH]
                    p.emit("vector", lambda E, dst=dst: E.tensor_mul(
                        out=SCRF, in0=dst, in1=pb[4][:, 0:CH]))
                    p.emit("vector", lambda E: E.tensor_sub(
                        out=SCRF, in0=SCRF, in1=pb[5][:, 0:CH]))
                    c = p.emit("vector", lambda E, dst=dst, f=f: E.tensor_add(
                        out=dst, in0=SCRF, in1=SMFC[:, f * CH:(f + 1) * CH]), inc="dve")
                seal45({"dve": c})
                smf_free = {"dve": c}
            p.fence()
            ar.pop()
            if bdbg:
                dbg_dump("x1", st_f)
            if upto < 5:
                continue

            ar.push()
            ACC = ar.f32(F * HALF)
            HPB = ar.bf16(F * HALF)
            WA0 = ar.bf16(F * HALF)
            WA1 = ar.bf16(F * HALF)
            wbc_off = ar.off
            WBC = ar.bf16(F * HALF)
            UALL = ar.bf16(F * QT)
            WSTG3 = ar.f32(2 * 2048)
            SMFD = ar.f32(2 * D)
            XCBd = WA0
            SQBd = WA1
            rW1 = ar.f32_at(wbc_off, 2048)
            lcd = mk_stager(WSTG3, 2048, 2)

            W1r = rW1_d.rearrange("(k p) h -> k p h", p=128)
            for k in range(F):
                rw1_tok = dma(rW1[:, k * 256:(k + 1) * 256], W1r[k])
            p.wait("tensor", rw1_tok)

            state["wslot"] = 0
            state["wtok"] = [dict(), dict()]

            for ch in range(NCH):
                stok = ln_stats(st_f, ch * CH, CH, XCBd[:, 0:F * CH], SQBd[:, 0:F * CH],
                                dict())
                btok = ln_bcast(CH, stok)
                p.wait("vector", btok)
                for f in range(F):
                    src = st_f(f)[:, ch * CH:(ch + 1) * CH]
                    p.emit("vector", lambda E, src=src: E.tensor_mul(
                        out=SCRF, in0=src, in1=pb[4][:, 0:CH]))
                    cs = p.emit("vector", lambda E: E.tensor_sub(
                        out=SCRF, in0=SCRF, in1=pb[5][:, 0:CH]), inc="dve")
                    p.force_wait("vector", "dve", cs)
                    cs = p.emit("vector", lambda E: E.reduce_sum(
                        out=VEC["ptmp"][:, 0:1], in_=SCRF, axis=AX.X), inc="dve")
                    p.force_wait("vector", "dve", cs)
                    c = p.emit("vector", lambda E, f=f: E.tensor_add(
                        out=VEC["pool8"][:, f:f + 1], in0=VEC["pool8"][:, f:f + 1],
                        in1=VEC["ptmp"][:, 0:1]), inc="dve")
                seal45({"dve": c})
            c = p.emit("vector", lambda E: E.tensor_single_scalar(
                out=VEC["pool8"], in_=VEC["pool8"], scalar=1.0 / L, op=ALU.mult),
                inc="dve")
            pool_tok = {"dve": c}

            for hk in range(2):
                tkh = mm_group(4, [(rW1[:, ki * 256 + hk * 128:ki * 256 + hk * 128 + 128],
                                    VEC["pool8"][:, ki:ki + 1]) for ki in range(F)],
                               pool_tok, bslice=pb[4][:, 0:1])
                consume(4, "scalar", lambda E, hk=hk: E.activation(
                    out=rh2[:, hk:hk + 1], in_=pb[4][:, 0:1], func=AF.Tanh,
                    bias=rb1c[:, hk:hk + 1]), tkh, "act")
            tkl = mm_group(5, [(rh2[:, hk:hk + 1], rW2[:, hk * 8:(hk + 1) * 8])
                               for hk in range(2)], p.tok("act"),
                           bslice=pb[5][0:1, 0:8])
            ltok = consume(5, "vector", lambda E: E.tensor_add(
                out=rrow[0:1, 0:8], in0=pb[5][0:1, 0:8], in1=b2row[0:1, 0:8]),
                tkl, "dve")
            p.wait("vector", ltok)

            def vstep(fn):
                c = p.emit("vector", fn, inc="dve")
                p.force_wait("vector", "dve", c)
                return c

            vstep(lambda E: E.reduce_max(out=srow1[0:1, 0:1],
                                         in_=rrow[0:1, 0:8], axis=AX.X))
            c = vstep(lambda E: E.tensor_single_scalar(
                out=rrow[0:1, 0:8], in_=rrow[0:1, 0:8], scalar=srow1[0:1, 0:1],
                op=ALU.subtract))
            p.wait("scalar", {"dve": c})
            ca = p.emit("scalar", lambda E: E.activation(
                out=rrow[0:1, 0:8], in_=rrow[0:1, 0:8], func=AF.Exp), inc="act")
            p.wait("vector", {"act": ca})
            vstep(lambda E: E.reduce_sum(out=srow1[0:1, 0:1],
                                         in_=rrow[0:1, 0:8], axis=AX.X))
            vstep(lambda E: E.reciprocal(out=srow1[0:1, 4:5],
                                         in_=srow1[0:1, 0:1]))
            vstep(lambda E: E.tensor_single_scalar(
                out=rrow[0:1, 0:8], in_=rrow[0:1, 0:8], scalar=srow1[0:1, 4:5],
                op=ALU.mult))
            vstep(lambda E: E.reduce_max(out=srow1[0:1, 1:2],
                                         in_=rrow[0:1, 0:8], axis=AX.X))
            vstep(lambda E: E.tensor_single_scalar(
                out=rrow2[0:1, 0:8], in_=rrow[0:1, 0:8], scalar=srow1[0:1, 1:2],
                op=ALU.is_ge))
            vstep(lambda E: E.scalar_tensor_tensor(
                out=rrow2[0:1, 0:8], in0=rrow2[0:1, 0:8], scalar=-1e30,
                in1=rrow[0:1, 0:8], op0=ALU.mult, op1=ALU.add))
            vstep(lambda E: E.reduce_max(out=srow1[0:1, 2:3],
                                         in_=rrow2[0:1, 0:8], axis=AX.X))
            vstep(lambda E: E.tensor_single_scalar(
                out=srow2[0:1, 0:8], in_=rrow2[0:1, 0:8], scalar=srow1[0:1, 2:3],
                op=ALU.is_ge))
            vstep(lambda E: E.scalar_tensor_tensor(
                out=rrow2[0:1, 0:8], in0=srow2[0:1, 0:8], scalar=-1e30,
                in1=rrow2[0:1, 0:8], op0=ALU.mult, op1=ALU.add))
            vstep(lambda E: E.reduce_max(out=srow1[0:1, 3:4],
                                         in_=rrow2[0:1, 0:8], axis=AX.X))
            vstep(lambda E: E.tensor_single_scalar(
                out=srow2[0:1, 0:8], in_=rrow[0:1, 0:8], scalar=srow1[0:1, 3:4],
                op=ALU.is_ge))
            vstep(lambda E: E.tensor_mul(
                out=rrow2[0:1, 0:8], in0=rrow[0:1, 0:8], in1=srow2[0:1, 0:8]))
            vstep(lambda E: E.reduce_sum(out=srow1[0:1, 0:1],
                                         in_=rrow2[0:1, 0:8], axis=AX.X))
            vstep(lambda E: E.reciprocal(out=srow1[0:1, 5:6],
                                         in_=srow1[0:1, 0:1]))
            c = vstep(lambda E: E.tensor_single_scalar(
                out=rrow2[0:1, 0:8], in_=rrow2[0:1, 0:8], scalar=srow1[0:1, 5:6],
                op=ALU.mult))
            tkw = mm_group(4, [(ones_row_f[0:1, :], rrow2[0:1, 0:8])], {"dve": c},
                           bslice=pb[4][:, 0:8])
            wbc_tok = consume(4, "vector", lambda E: E.tensor_copy(
                out=VEC["wbc"], in_=pb[4][:, 0:8]), tkw, "dve")
            if bdbg and "rdump" in dbgset:
                t_ = p.tok()
                p.wait("sync", t_)
                dma(dbg_d["rdump"][0][:, 0:8], VEC["pool8"], sem="out")
                dma(dbg_d["rdump"][1][:, 0:2], rh2, sem="out")
                dma(dbg_d["rdump"][2][0:1, 0:16], rrow[0:1, 0:16], sem="out")
                dma(dbg_d["rdump"][3][0:1, 0:16], rrow2[0:1, 0:16], sem="out")
                p.fence()
            if bdbg:
                dbg_dump("wrouter", lambda f: VEC["wbc"], nf=1)
            if upto < 6:
                p.fence()
                ar.pop()
                continue

            for hf in range(2):
                p.fence()
                ht0 = hf * HALF
                acc_tok = dict()
                cp = None
                for ch in range(2):
                    t0 = ht0 + ch * CH
                    stok = ln_stats(st_f, t0, CH, XCBd[:, 0:F * CH], SQBd[:, 0:F * CH],
                                    acc_tok)
                    btok = ln_bcast(CH, stok)
                    p.wait("vector", btok)
                    for f in range(F):
                        src = st_f(f)[:, t0:t0 + CH]
                        dst = ACC[:, f * HALF + ch * CH:f * HALF + (ch + 1) * CH]
                        p.emit("vector", lambda E, src=src, dst=dst: E.tensor_mul(
                            out=dst, in0=src, in1=pb[4][:, 0:CH]))
                        c = p.emit("vector", lambda E, dst=dst: E.tensor_sub(
                            out=dst, in0=dst, in1=pb[5][:, 0:CH]), inc="dve")
                        p.wait("gpsimd", {"dve": c})
                        cp = p.emit("gpsimd", lambda E, dst=dst, f=f, ch=ch:
                                    E.tensor_copy(out=HPB[:, f * HALF + ch * CH:
                                                          f * HALF + (ch + 1) * CH],
                                                  in_=dst), inc="pool")
                    seal45({"dve": c})
                    acc_tok = {"dve": c, "pool": cp}
                hp_tok = dict(acc_tok, **wbc_tok)

                wa_bufs = [WA0, WA1]
                war3 = wa_d.rearrange("z (ki p) h -> z p ki h", p=128)
                wbr3 = wb_d.rearrange("z (hk p) o -> z p hk o", p=128)

                def load_wa(it, war_tok):
                    ez, hc = divmod(it, 4)
                    wab = wa_bufs[it % 2]
                    t = dict()
                    for k2 in range(4):
                        t = lcd(wab[:, k2 * 2 * HALF:(k2 + 1) * 2 * HALF],
                                war3[ez][:, 2 * k2:2 * k2 + 2,
                                          hc * HALF:(hc + 1) * HALF],
                                2 * HALF, eng="gpsimd", cast_wait=war_tok)
                    return t

                def load_wb(it, war_tok):
                    ez, hc = divmod(it, 4)
                    t = dict()
                    for j2 in range(4):
                        t = lcd(WBC[:, j2 * 2 * HALF:(j2 + 1) * 2 * HALF],
                                wbr3[ez][:, hc * 8 + 2 * j2:hc * 8 + 2 * j2 + 2, :],
                                2 * HALF, eng="gpsimd", cast_wait=war_tok)
                    return t

                watoks = {0: load_wa(0, dict(hp_tok))}
                for ez in range(NZ):
                    for hc in range(4):
                        it = ez * 4 + hc
                        wab = wa_bufs[it % 2]
                        wbtok = load_wb(it, dict(hp_tok) if it == 0 else p.tok("pe"))
                        if it + 1 < NZ * 4:
                            watoks[it + 1] = load_wa(it + 1, p.tok("pe"))
                        watok = watoks[it]
                        for tw in range(2):
                            ug = dict()
                            for ht in range(F):
                                bk = 4 + (ht % 2)
                                tku = mm_group(bk, [
                                    (wab[:, ki * HALF + ht * 128:ki * HALF + ht * 128 + 128],
                                     HPB[:, ki * HALF + tw * QT:ki * HALF + (tw + 1) * QT])
                                    for ki in range(F)], dict(watok, **hp_tok))
                                ug = consume(bk, "scalar", lambda E, ht=ht, bk=bk:
                                             E.activation(out=UALL[:, ht * QT:(ht + 1) * QT],
                                                          in_=pb[bk][:, 0:QT],
                                                          func=AF.Gelu_apprx_tanh),
                                             tku, "act")
                            for fg in range(2):
                                for j in range(4):
                                    fo = fg * 4 + j
                                    tkz = mm_group(j, [
                                        (WBC[:, ht * HALF + fo * 128:ht * HALF + fo * 128 + 128],
                                         UALL[:, ht * QT:(ht + 1) * QT])
                                        for ht in range(F)], dict(wbtok, **ug))
                                    dst = ACC[:, fo * HALF + tw * QT:fo * HALF + (tw + 1) * QT]
                                    consume(j, "vector", lambda E, dst=dst, j=j, ez=ez:
                                            E.scalar_tensor_tensor(
                                                out=dst, in0=pb[j][:, 0:QT],
                                                scalar=VEC["wbc"][:, ez:ez + 1],
                                                in1=dst, op0=ALU.mult, op1=ALU.add),
                                            tkz, "dve")
                if bdbg and hf == 0:
                    dbg_dump("acc0", lambda f: ACC[:, f * HALF:(f + 1) * HALF])
                    dbg_dump("vecdump", lambda f: VEC["wbc"], nf=1)

                if upto < 7:
                    continue
                zdone = p.tok("pe", "dve")
                smfd_tok = [dict(), dict()]
                for ch in range(2):
                    t0 = ht0 + ch * CH
                    stok = ln_stats(lambda f: ACC[:, f * HALF:(f + 1) * HALF],
                                    ch * CH, CH, XCBd[:, 0:F * CH], SQBd[:, 0:F * CH],
                                    zdone)
                    btok = ln_bcast(CH, stok)
                    p.wait("vector", btok)
                    for f in range(F):
                        asrc = ACC[:, f * HALF + ch * CH:f * HALF + (ch + 1) * CH]
                        x1src = st_f(f)[:, t0:t0 + CH]
                        p.emit("vector", lambda E, asrc=asrc: E.tensor_mul(
                            out=SCRF, in0=asrc, in1=pb[4][:, 0:CH]))
                        p.emit("vector", lambda E: E.tensor_sub(
                            out=SCRF, in0=SCRF, in1=pb[5][:, 0:CH]))
                        c = p.emit("vector", lambda E, asrc=asrc, x1src=x1src:
                                   E.tensor_add(out=asrc, in0=SCRF, in1=x1src),
                                   inc="dve")
                    seal45({"dve": c})
                    och = {"dve": c}
                    for i in range(CH // 128):
                        tt = (t0 + i * 128) // 128
                        sd = i % 2
                        ot = dict()
                        for f in range(F):
                            bk = 6 + (f % 2)
                            p.wait("tensor", bank_state[bk])
                            p.wait("tensor", och)
                            cpe = p.emit("tensor", lambda E, bk=bk, f=f, ch=ch, i=i:
                                         E.transpose(pb[bk][:, 0:128],
                                                     ACC[:, f * HALF + ch * CH + i * 128:
                                                         f * HALF + ch * CH + (i + 1) * 128],
                                                     ident), inc="pe")
                            dst = SMFD[:, sd * D + f * 128:sd * D + (f + 1) * 128]
                            ot = consume(bk, "vector", lambda E, dst=dst, bk=bk:
                                         E.tensor_copy(out=dst, in_=pb[bk][:, 0:128]),
                                         dict({"pe": cpe}, **smfd_tok[sd]), "dve")
                        smfd_tok[sd] = dma(out_d[b, tt], SMFD[:, sd * D:(sd + 1) * D],
                                           sem="out", wait_tok=ot)
            p.fence()
            ar.pop()

        p.wait("gpsimd", p.tok("out"))

        block = es.enter_context(nc.Block())

        @block.sync
        def _(E):
            for fn in p.q["sync"]:
                fn(E)

        @block.gpsimd
        def _(E):
            for fn in p.q["gpsimd"]:
                fn(E)

        @block.vector
        def _(E):
            for fn in p.q["vector"]:
                fn(E)

        @block.scalar
        def _(E):
            for fn in p.q["scalar"]:
                fn(E)

        @block.tensor
        def _(E):
            for fn in p.q["tensor"]:
                fn(E)

    return nc


def kernel(**inputs):
    x = np.ascontiguousarray(inputs["L0_attn_x"])
    xs = x.reshape(NCORES, BPC, L // 128, 128, D)
    nc = build(debug_names=tuple(DBG), upto=UPTO)
    in_maps = []
    for c in range(NCORES):
        m = {"x": np.ascontiguousarray(xs[c])}
        for k, v in inputs.items():
            if k != "L0_attn_x":
                m[k] = np.ascontiguousarray(v)
        in_maps.append(m)
    res = run_bass_kernel_spmd(nc, in_maps, core_ids=list(range(NCORES)))
    outs = [np.asarray(r["out"]).reshape(BPC, L, D) for r in res.results]
    return np.concatenate(outs, axis=0)


# version 03
# speedup vs baseline: 1.1704x; 1.1118x over previous
import numpy as np
from contextlib import ExitStack
import concourse.bass as bass
import concourse.mybir as mybir
from concourse.bass_utils import run_bass_kernel_spmd

F32 = mybir.dt.float32
BF16 = mybir.dt.bfloat16
U8 = mybir.dt.uint8
AF = mybir.ActivationFunctionType
ALU = mybir.AluOpType
AX = mybir.AxisListType

NCORES = 8
BPC = 2
L = 2048
D = 1024
F = 8
DH = 4096
NZ = 8
EPS = 1e-5
CH = 512
NCH = L // CH
QT = 512
NQ = L // QT
HALF = 1024
ONES_SCALE = 1.0 / 1024.0

ENGINES = ("sync", "gpsimd", "vector", "scalar", "tensor")

DBG = ()
UPTO = 99


class Prog:
    def __init__(self):
        self.q = {e: [] for e in ENGINES}
        self.cnt = {}
        self.wm = {e: {} for e in ENGINES}
        self.sems = {}
        self.evlog = {e: [] for e in ENGINES}

    def check_deadlock(self):
        ptr = {e: 0 for e in ENGINES}
        val = {}
        progress = True
        while progress:
            progress = False
            for e in ENGINES:
                log = self.evlog[e]
                while ptr[e] < len(log):
                    kind, sem, c = log[ptr[e]]
                    if kind == "w" and val.get(sem, 0) < c:
                        break
                    if kind == "i":
                        val[sem] = val.get(sem, 0) + c
                    ptr[e] += 1
                    progress = True
        stuck = [(e, ptr[e], len(self.evlog[e]), self.evlog[e][ptr[e]])
                 for e in ENGINES if ptr[e] < len(self.evlog[e])]
        return stuck

    def tok(self, *sems):
        if not sems:
            return {s: c for s, c in self.cnt.items() if c > 0}
        return {s: self.cnt.get(s, 0) for s in sems if self.cnt.get(s, 0) > 0}

    def wait(self, eng, tokd):
        assert eng in ENGINES
        for s, c in tokd.items():
            if c > self.wm[eng].get(s, 0):
                self.wm[eng][s] = c
                self.evlog[eng].append(("w", s, c))
                self.q[eng].append(
                    lambda E, self=self, s=s, c=c: E.wait_ge(self.sems[s], c))

    def emit(self, eng, fn, inc=None, step=1):
        assert eng in ENGINES
        if inc is None:
            self.q[eng].append(fn)
            self.evlog[eng].append(("n", None, 0))
            return None
        self.cnt[inc] = self.cnt.get(inc, 0) + step
        c = self.cnt[inc]
        self.evlog[eng].append(("i", inc, step))
        self.q[eng].append(
            lambda E, self=self, fn=fn, inc=inc, step=step:
                fn(E).then_inc(self.sems[inc], step))
        self.wm[eng][inc] = c
        return c

    def force_wait(self, eng, sem, c):
        self.evlog[eng].append(("w", sem, c))
        self.q[eng].append(
            lambda E, self=self, sem=sem, c=c: E.wait_ge(self.sems[sem], c))
        if c > self.wm[eng].get(sem, 0):
            self.wm[eng][sem] = c

    def fence(self):
        t = self.tok()
        for e in ENGINES:
            self.wait(e, t)


class Arena:
    def __init__(self, t, nbytes):
        self.t = t
        self.nbytes = nbytes
        self.off = 0
        self.marks = []
        self.peak = 0

    def push(self):
        self.marks.append(self.off)

    def pop(self):
        self.off = self.marks.pop()

    def _take(self, nbytes):
        o = self.off
        assert o % 4 == 0
        self.off += nbytes + ((4 - nbytes % 4) % 4)
        self.peak = max(self.peak, self.off)
        assert self.off <= self.nbytes, f"arena overflow {self.off} > {self.nbytes}"
        return o

    def f32(self, cols):
        o = self._take(4 * cols)
        return self.t[:, o:o + 4 * cols].bitcast(F32)

    def bf16(self, cols):
        o = self._take(2 * cols)
        return self.t[:, o:o + 2 * cols].bitcast(BF16)

    def f32_at(self, o, cols):
        return self.t[:, o:o + 4 * cols].bitcast(F32)

    def bf16_at(self, o, cols):
        return self.t[:, o:o + 2 * cols].bitcast(BF16)


def build(debug_names=(), upto=99):
    nc = bass.Bass("TRN2", target_bir_lowering=False)
    p = Prog()
    dbgset = {t[0] for t in debug_names}

    x_d = nc.dram_tensor("x", [BPC, L // 128, 128, D], F32, kind="ExternalInput").ap()
    out_d = nc.dram_tensor("out", [BPC, L // 128, 128, D], F32, kind="ExternalOutput").ap()
    vnames = {
        "lag": "L0_attn_ln_att_g", "lab": "L0_attn_ln_att_b",
        "lfg": "L0_attn_ln_ffn_g", "lfb": "L0_attn_ln_ffn_b",
        "amk": "L0_attn_att_mix_k", "amv": "L0_attn_att_mix_v",
        "amr": "L0_attn_att_mix_r", "fmk": "L0_attn_ffn_mix_k",
        "fmr": "L0_attn_ffn_mix_r", "td": "L0_attn_att_time_decay",
        "tf": "L0_attn_att_time_first"}
    vec_d = {k: nc.dram_tensor(n, [D], F32, kind="ExternalInput").ap()
             for k, n in vnames.items()}
    wk_d = nc.dram_tensor("L0_attn_att_wk", [D, D], F32, kind="ExternalInput").ap()
    wv_d = nc.dram_tensor("L0_attn_att_wv", [D, D], F32, kind="ExternalInput").ap()
    wr_d = nc.dram_tensor("L0_attn_att_wr", [D, D], F32, kind="ExternalInput").ap()
    wo_d = nc.dram_tensor("L0_attn_att_wo", [D, D], F32, kind="ExternalInput").ap()
    fwk_d = nc.dram_tensor("L0_attn_ffn_wk", [D, DH], F32, kind="ExternalInput").ap()
    fwv_d = nc.dram_tensor("L0_attn_ffn_wv", [DH, D], F32, kind="ExternalInput").ap()
    fwr_d = nc.dram_tensor("L0_attn_ffn_wr", [D, D], F32, kind="ExternalInput").ap()
    rW1_d = nc.dram_tensor("L0_ffn_W1", [D, 256], F32, kind="ExternalInput").ap()
    rb1_d = nc.dram_tensor("L0_ffn_b1", [256], F32, kind="ExternalInput").ap()
    rW2_d = nc.dram_tensor("L0_ffn_W2", [256, NZ], F32, kind="ExternalInput").ap()
    rb2_d = nc.dram_tensor("L0_ffn_b2", [NZ], F32, kind="ExternalInput").ap()
    wa_d = nc.dram_tensor("L0_ffn_Wa", [NZ, D, DH], F32, kind="ExternalInput").ap()
    wb_d = nc.dram_tensor("L0_ffn_Wb", [NZ, DH, D], F32, kind="ExternalInput").ap()
    wa_c_d = nc.dram_tensor("wa_cache", [NZ, 4, 128, F * HALF], BF16,
                            kind="Internal").ap()
    wb_c_d = nc.dram_tensor("wb_cache", [NZ, 4, 128, F * HALF], BF16,
                            kind="Internal").ap()
    fwk_c_d = nc.dram_tensor("fwk_cache", [DH // CH, 128, F * CH], BF16,
                             kind="Internal").ap()
    fwv_c_d = nc.dram_tensor("fwv_cache", [32, 128, D], BF16, kind="Internal").ap()
    dbg_d = {}
    for t in debug_names:
        n, shp = t[0], t[1]
        dt = t[2] if len(t) > 2 else F32
        dbg_d[n] = nc.dram_tensor("dbg_" + n, shp, dt, kind="ExternalOutput").ap()

    es = ExitStack()
    with es:
        AR_BYTES = 206 * 1024
        ar_t = es.enter_context(nc.sbuf_tensor("arena", [128, AR_BYTES], U8))
        ar = Arena(ar_t, AR_BYTES)
        pb = [es.enter_context(nc.psum_tensor(f"pb{i}", [128, 512], F32))
              for i in range(8)]
        for s in ("in", "out", "pe", "dve", "act", "pool"):
            p.sems[s] = es.enter_context(nc.semaphore("s_" + s))

        ident = ar.f32(128)
        ones_col_b = ar.bf16(2)
        ones_row_f = ar.f32(128)
        VEC = {}
        for n in ("lag", "lab", "lfg", "lfb", "amk", "amv", "amr", "fmk", "fmr",
                  "td", "tf", "ew", "eu", "wbc", "pcar", "qcar", "pool8", "ptmp"):
            VEC[n] = ar.f32(8)
        hcar = ar.bf16(8)
        rb1c = ar.f32(2)
        rh2 = ar.f32(2)
        rW2 = ar.f32(16)
        cEPS = ar.f32(1)
        nc.const_aps.register(F32, EPS, cEPS[:, 0:1])
        m_row = ar.f32(512)
        rs_row = ar.f32(512)
        mrs_row = ar.f32(512)
        srow1 = ar.f32(512)
        srow2 = ar.f32(512)
        rrow = ar.f32(16)
        rrow2 = ar.f32(16)
        SCRF = ar.f32(512)
        SCRB = SCRF.bitcast(BF16)

        bank_state = [dict() for _ in range(8)]
        state = {"lnmm": {}, "wslot": 0, "wtok": [dict(), dict()]}

        def mm_group(bank, pairs, opd_tok, bslice=None, free=512):
            p.wait("tensor", bank_state[bank])
            p.wait("tensor", opd_tok)
            tgt = pb[bank][:, 0:free] if bslice is None else bslice
            n = len(pairs)
            c = None
            for i, (lt, rh) in enumerate(pairs):
                if i < n - 1:
                    p.emit("tensor", lambda E, tgt=tgt, lt=lt, rh=rh, i=i:
                           E.matmul(tgt, lt, rh, start=(i == 0), stop=False))
                else:
                    c = p.emit("tensor", lambda E, tgt=tgt, lt=lt, rh=rh, i=i:
                               E.matmul(tgt, lt, rh, start=(i == 0), stop=True),
                               inc="pe")
            return {"pe": c}

        def consume(bank, eng, fn, tokd, sem):
            p.wait(eng, tokd)
            c = p.emit(eng, fn, inc=sem)
            bank_state[bank] = {sem: c}
            return {sem: c}

        def dma(dst, src, sem="in", wait_tok=None):
            if wait_tok:
                p.wait("sync", wait_tok)
            return {sem: p.emit("sync", lambda E, dst=dst, src=src:
                                E.dma_start(out=dst, in_=src), inc=sem, step=16)}

        def load_cast(dst, dram_src, wstg, nst, cols, cast_wait=None, scale_col=None):
            s = state["wslot"] % nst
            state["wslot"] += 1
            while len(state["wtok"]) < nst:
                state["wtok"].append(dict())
            stg = wstg[:, s * 1024:s * 1024 + cols]
            t = dma(stg, dram_src, wait_tok=state["wtok"][s])
            p.wait("gpsimd", t)
            if cast_wait:
                p.wait("gpsimd", cast_wait)
            if scale_col is None:
                c = p.emit("gpsimd", lambda E, dst=dst, stg=stg:
                           E.tensor_copy(out=dst, in_=stg), inc="pool")
            else:
                c = p.emit("gpsimd", lambda E, dst=dst, stg=stg, sc=scale_col:
                           E.tensor_single_scalar(out=dst, in_=stg, scalar=sc,
                                                  op=ALU.mult), inc="pool")
            state["wtok"][s] = {"pool": c}
            return {"pool": c}

        def mk_stager(wstg_ap, slot_cols, nslots):
            stt = {"i": 0, "tok": [dict() for _ in range(nslots)]}

            def lc(dst, dram_src, cols, eng="vector", cast_wait=None):
                s = stt["i"] % nslots
                stt["i"] += 1
                stg = wstg_ap[:, s * slot_cols:s * slot_cols + cols]
                t = dma(stg, dram_src, wait_tok=stt["tok"][s])
                sem = {"vector": "dve", "scalar": "act", "gpsimd": "pool"}[eng]
                p.wait(eng, t)
                if cast_wait:
                    p.wait(eng, cast_wait)
                if eng == "scalar":
                    c = p.emit("scalar", lambda E, dst=dst, stg=stg:
                               E.activation(out=dst, in_=stg, func=AF.Copy),
                               inc="act")
                else:
                    c = p.emit(eng, lambda E, dst=dst, stg=stg:
                               E.tensor_copy(out=dst, in_=stg), inc=sem)
                stt["tok"][s] = {sem: c}
                return {sem: c}
            return lc

        p.emit("gpsimd", lambda E: E.memset(ident, 1.0))
        p.emit("gpsimd", lambda E: E.affine_select(
            out=ident, in_=ident, pattern=[[-1, 128]], compare_op=ALU.is_equal,
            fill=0.0, base=0, channel_multiplier=1))
        p.emit("gpsimd", lambda E: E.memset(ones_col_b[:, 0:1], ONES_SCALE))
        p.emit("gpsimd", lambda E: E.memset(ones_col_b[:, 1:2], 1.0))
        p.emit("gpsimd", lambda E: E.memset(cEPS, EPS))
        c = p.emit("gpsimd", lambda E: E.memset(ones_row_f[0:1, :], 1.0), inc="pool")
        const_tok = {"pool": c}
        for e in ENGINES:
            p.wait(e, const_tok)

        vtok = {}
        for key in vnames:
            src = vec_d[key].rearrange("(f p o) -> f p o", p=128, o=1)
            for f in range(F):
                vtok = dma(VEC[key][:, f:f + 1], src[f])
        W2r = rW2_d.rearrange("(k p) z -> k p z", p=128)
        for k in range(2):
            vtok = dma(rW2[:, k * 8:(k + 1) * 8], W2r[k])
        b1r = rb1_d.rearrange("(k p o) -> k p o", p=128, o=1)
        for k in range(2):
            vtok = dma(rb1c[:, k:k + 1], b1r[k])
        b2row = ar.f32(8)
        vtok = dma(b2row[0:1, 0:8], rb2_d.rearrange("(o z) -> o z", o=1))

        p.wait("scalar", vtok)
        p.emit("scalar", lambda E: E.activation(out=VEC["ew"], in_=VEC["td"], func=AF.Exp))
        p.emit("scalar", lambda E: E.activation(out=VEC["ew"], in_=VEC["ew"], func=AF.Exp, scale=-1.0))
        c = p.emit("scalar", lambda E: E.activation(out=VEC["eu"], in_=VEC["tf"], func=AF.Exp), inc="act")
        ew_tok = {"act": c}

        STREAM = ar.f32(F * L)

        def st_f(f):
            return STREAM[:, f * L:(f + 1) * L]

        def ln_stats(src_f, t0, n, xcb, sqb, scratch_tok):
            p.wait("gpsimd", scratch_tok)
            p.wait("gpsimd", state["lnmm"])
            p.wait("scalar", scratch_tok)
            p.wait("scalar", state["lnmm"])
            for f in range(F):
                src = src_f(f)[:, t0:t0 + n]
                p.emit("gpsimd", lambda E, d=xcb[:, f * n:(f + 1) * n], s=src:
                       E.tensor_copy(out=d, in_=s), inc="pool")
                p.emit("scalar", lambda E, d=sqb[:, f * n:(f + 1) * n], s=src:
                       E.activation(out=d, in_=s, func=AF.Square), inc="act")
            cast_tok = p.tok("pool", "act")
            tokS = mm_group(6, [(ones_col_b[:, 0:1], xcb[:, f * n:(f + 1) * n])
                                for f in range(F)], cast_tok, bslice=pb[6][0:1, 0:n])
            tokQ = mm_group(7, [(ones_col_b[:, 0:1], sqb[:, f * n:(f + 1) * n])
                                for f in range(F)], cast_tok, bslice=pb[7][0:1, 0:n])
            state["lnmm"] = {"pe": tokQ["pe"]}
            tokm = consume(6, "vector", lambda E: E.tensor_copy(
                out=m_row[0:1, 0:n], in_=pb[6][0:1, 0:n]), tokS, "dve")
            p.wait("scalar", tokm)
            ca = p.emit("scalar", lambda E: E.activation(
                out=srow1[0:1, 0:n], in_=m_row[0:1, 0:n], func=AF.Square), inc="act")
            tq = dict(tokQ)
            tq["act"] = ca
            tokv = consume(7, "vector", lambda E: E.tensor_sub(
                out=srow2[0:1, 0:n], in0=pb[7][0:1, 0:n], in1=srow1[0:1, 0:n]),
                tq, "dve")
            p.wait("scalar", tokv)
            p.emit("scalar", lambda E: E.activation(
                out=rs_row[0:1, 0:n], in_=srow2[0:1, 0:n], func=AF.Ln, bias=EPS))
            ca = p.emit("scalar", lambda E: E.activation(
                out=rs_row[0:1, 0:n], in_=rs_row[0:1, 0:n], func=AF.Exp, scale=-0.5),
                inc="act")
            p.wait("vector", {"act": ca})
            p.wait("vector", state["lnmm"])
            cd = p.emit("vector", lambda E: E.tensor_mul(
                out=mrs_row[0:1, 0:n], in0=m_row[0:1, 0:n], in1=rs_row[0:1, 0:n]),
                inc="dve")
            return {"dve": cd}

        def ln_bcast(n, rows_tok):
            mm_group(4, [(ones_row_f[0:1, :], rs_row[0:1, 0:n])], rows_tok,
                     bslice=pb[4][:, 0:n])
            t2 = mm_group(5, [(ones_row_f[0:1, :], mrs_row[0:1, 0:n])], rows_tok,
                          bslice=pb[5][:, 0:n])
            state["lnmm"] = {"pe": t2["pe"]}
            return {"pe": t2["pe"]}

        def seal45(tokd):
            bank_state[4] = dict(tokd)
            bank_state[5] = dict(tokd)

        def dbg_dump(name, src_fn, nf=F):
            if name not in dbgset:
                return
            t = p.tok()
            p.wait("sync", t)
            for f in range(nf):
                dma(dbg_d[name][f], src_fn(f), sem="out")
            p.fence()

        def load_transpose_x0(b, ch, STGx, SMFx, stg_tok, smf_free):
            ct = dict()
            for i in range(CH // 128):
                tt = ch * (CH // 128) + i
                s = (ch * (CH // 128) + i) % 3
                t = dma(STGx[:, s * 1024:(s + 1) * 1024], x_d[b, tt],
                        wait_tok=stg_tok[s])
                for f in range(F):
                    bk = 6 + (f % 2)
                    p.wait("tensor", bank_state[bk])
                    p.wait("tensor", t)
                    cpe = p.emit("tensor", lambda E, bk=bk, s=s, f=f, STGx=STGx:
                                 E.transpose(pb[bk][:, 0:128],
                                             STGx[:, s * 1024 + f * 128:s * 1024 + (f + 1) * 128],
                                             ident), inc="pe")
                    dst = SMFx[:, f * CH + i * 128:f * CH + (i + 1) * 128]
                    ct = consume(bk, "vector", lambda E, dst=dst, bk=bk:
                                 E.tensor_copy(out=dst, in_=pb[bk][:, 0:128]),
                                 dict({"pe": cpe}, **smf_free), "dve")
                stg_tok[s] = {"pe": cpe}
            return ct

        cache_tok = {}

        for b in range(BPC):
            bdbg = (b == 0)
            p.fence()
            p.emit("gpsimd", lambda E: E.memset(VEC["pcar"], 0.0))
            p.emit("gpsimd", lambda E: E.memset(VEC["qcar"], 0.0))
            p.emit("gpsimd", lambda E: E.memset(hcar, 0.0))
            c = p.emit("gpsimd", lambda E: E.memset(VEC["pool8"], 0.0), inc="pool")
            p.fence()

            ar.push()
            STG = ar.f32(3 * 1024)
            SMF = ar.f32(F * CH)
            XCB = ar.bf16(F * CH)
            SQB = ar.bf16(F * CH)
            stg_tok = [dict(), dict(), dict()]
            smf_free = dict(const_tok)
            for ch in range(NCH):
                tr_tok = load_transpose_x0(b, ch, STG, SMF, stg_tok, smf_free)
                stok = ln_stats(lambda f: SMF[:, f * CH:(f + 1) * CH], 0, CH,
                                XCB, SQB, tr_tok)
                btok = ln_bcast(CH, stok)
                p.wait("vector", btok)
                p.wait("vector", tr_tok)
                for f in range(F):
                    dst = st_f(f)[:, ch * CH:(ch + 1) * CH]
                    src = SMF[:, f * CH:(f + 1) * CH]
                    p.emit("vector", lambda E, dst=dst, src=src: E.tensor_mul(
                        out=dst, in0=src, in1=pb[4][:, 0:CH]))
                    c = p.emit("vector", lambda E, dst=dst: E.tensor_sub(
                        out=dst, in0=dst, in1=pb[5][:, 0:CH]), inc="dve")
                seal45({"dve": c})
                smf_free = dict(p.tok("pool", "act"), dve=c)
            p.fence()
            ar.pop()
            if bdbg:
                dbg_dump("h0", st_f)
            if upto < 2:
                continue

            ar.push()
            WK = ar.bf16(F * D)
            WV = ar.bf16(F * D)
            WR = ar.bf16(F * D)
            WSTG = ar.f32(2 * 1024)
            HBUF = ar.bf16(F * 516)
            MIXK = ar.bf16(F * QT)
            MIXV = ar.bf16(F * QT)
            MIXR = ar.bf16(F * QT)
            RSIG = ar.bf16(F * QT)
            REG = ar._take(18944)
            XCB2 = ar.bf16_at(REG, F * QT)
            SQB2 = ar.bf16_at(REG + 2 * F * QT, F * QT)
            RW = ar.bf16_at(REG, F * QT)
            EK = ar.f32_at(REG + 8192, QT)
            VV = ar.f32_at(REG + 10240, QT)
            BP = ar.f32_at(REG + 12288, QT)
            PBUF = ar.f32_at(REG + 14336, QT + 4)
            QBUF = ar.f32_at(REG + 16400, QT + 4)

            def hb(f):
                return HBUF[:, f * 516:(f + 1) * 516]

            state["wslot"] = 0
            state["wtok"] = [dict(), dict()]
            wkr = wk_d.rearrange("(k p) o -> k p o", p=128)
            wvr = wv_d.rearrange("(k p) o -> k p o", p=128)
            wrr = wr_d.rearrange("(k p) o -> k p o", p=128)
            wor = wo_d.rearrange("(k p) o -> k p o", p=128)
            for ki in range(F):
                load_cast(WK[:, ki * D:(ki + 1) * D], wkr[ki], WSTG, 2, D)
            for ki in range(F):
                load_cast(WV[:, ki * D:(ki + 1) * D], wvr[ki], WSTG, 2, D)
            wtok = dict()
            for ki in range(F):
                wtok = load_cast(WR[:, ki * D:(ki + 1) * D], wrr[ki], WSTG, 2, D)

            prev_wo_pe = dict()
            chain_tok = dict(ew_tok)
            for q in range(NQ):
                t0 = q * QT
                stok = ln_stats(st_f, t0, QT, XCB2, SQB2, prev_wo_pe)
                btok = ln_bcast(QT, stok)
                p.wait("vector", btok)
                p.wait("vector", prev_wo_pe)
                for f in range(F):
                    src = st_f(f)[:, t0:t0 + QT]
                    p.emit("vector", lambda E, src=src: E.tensor_mul(
                        out=SCRF, in0=src, in1=pb[4][:, 0:QT]))
                    p.emit("vector", lambda E: E.tensor_sub(
                        out=SCRF, in0=SCRF, in1=pb[5][:, 0:QT]))
                    p.emit("vector", lambda E, f=f: E.tensor_copy(
                        out=hb(f)[:, 0:1], in_=hcar[:, f:f + 1]))
                    cs = p.emit("vector", lambda E, f=f: E.tensor_scalar(
                        out=hb(f)[:, 1:513], in0=SCRF, scalar1=VEC["lag"][:, f:f + 1],
                        scalar2=VEC["lab"][:, f:f + 1], op0=ALU.mult, op1=ALU.add),
                        inc="dve")
                    p.force_wait("vector", "dve", cs)
                    c = p.emit("vector", lambda E, f=f: E.tensor_copy(
                        out=hcar[:, f:f + 1], in_=hb(f)[:, 512:513]), inc="dve")
                seal45({"dve": c})
                for f in range(F):
                    p.emit("vector", lambda E, f=f: E.tensor_sub(
                        out=SCRB[:, 0:QT], in0=hb(f)[:, 1:513], in1=hb(f)[:, 0:512]))
                    for mx, buf in (("amk", MIXK), ("amv", MIXV), ("amr", MIXR)):
                        p.emit("vector", lambda E, f=f, mx=mx: E.tensor_single_scalar(
                            out=SCRB[:, QT:2 * QT], in_=SCRB[:, 0:QT],
                            scalar=VEC[mx][:, f:f + 1], op=ALU.mult))
                        c = p.emit("vector", lambda E, f=f, buf=buf: E.tensor_add(
                            out=buf[:, f * QT:(f + 1) * QT], in0=SCRB[:, QT:2 * QT],
                            in1=hb(f)[:, 0:512]), inc="dve")
                mix_tok = dict(wtok, dve=c)

                rs_tok = {}
                rsig_war = p.tok("dve")
                for fo in range(F):
                    bk = fo % 4
                    tkr = mm_group(bk, [(WR[:, ki * D + fo * 128:ki * D + fo * 128 + 128],
                                         MIXR[:, ki * QT:(ki + 1) * QT]) for ki in range(F)],
                                   mix_tok)
                    rs_tok = consume(bk, "scalar", lambda E, fo=fo, bk=bk: E.activation(
                        out=RSIG[:, fo * QT:(fo + 1) * QT], in_=pb[bk][:, 0:QT],
                        func=AF.Sigmoid), dict(tkr, **rsig_war), "act")

                for fo in range(F):
                    bk_k = fo % 2
                    bk_v = 2 + (fo % 2)
                    tkk = mm_group(bk_k, [(WK[:, ki * D + fo * 128:ki * D + fo * 128 + 128],
                                           MIXK[:, ki * QT:(ki + 1) * QT]) for ki in range(F)],
                                   mix_tok)
                    tkv = mm_group(bk_v, [(WV[:, ki * D + fo * 128:ki * D + fo * 128 + 128],
                                           MIXV[:, ki * QT:(ki + 1) * QT]) for ki in range(F)],
                                   mix_tok)
                    ek_tok = consume(bk_k, "scalar", lambda E, bk=bk_k: E.activation(
                        out=EK, in_=pb[bk][:, 0:QT], func=AF.Exp),
                        dict(tkk, **chain_tok), "act")
                    vv_tok = consume(bk_v, "scalar", lambda E, bk=bk_v: E.activation(
                        out=VV, in_=pb[bk][:, 0:QT], func=AF.Copy),
                        dict(tkv, **chain_tok), "act")
                    p.wait("vector", ek_tok)
                    p.wait("vector", vv_tok)
                    p.emit("vector", lambda E: E.tensor_mul(out=BP, in0=EK, in1=VV))
                    p.emit("vector", lambda E, f=fo: E.tensor_copy(
                        out=PBUF[:, 0:1], in_=VEC["pcar"][:, f:f + 1]))
                    cs = p.emit("vector", lambda E, f=fo: E.tensor_tensor_scan(
                        out=PBUF[:, 1:513], data0=VEC["ew"][:, f:f + 1].broadcast_to([128, QT]),
                        data1=BP, initial=VEC["pcar"][:, f:f + 1],
                        op0=ALU.mult, op1=ALU.add), inc="dve")
                    p.force_wait("vector", "dve", cs)
                    p.emit("vector", lambda E, f=fo: E.tensor_copy(
                        out=VEC["pcar"][:, f:f + 1], in_=PBUF[:, 512:513]))
                    p.emit("vector", lambda E, f=fo: E.tensor_copy(
                        out=QBUF[:, 0:1], in_=VEC["qcar"][:, f:f + 1]))
                    cs = p.emit("vector", lambda E, f=fo: E.tensor_tensor_scan(
                        out=QBUF[:, 1:513], data0=VEC["ew"][:, f:f + 1].broadcast_to([128, QT]),
                        data1=EK, initial=VEC["qcar"][:, f:f + 1],
                        op0=ALU.mult, op1=ALU.add), inc="dve")
                    p.force_wait("vector", "dve", cs)
                    p.emit("vector", lambda E, f=fo: E.tensor_copy(
                        out=VEC["qcar"][:, f:f + 1], in_=QBUF[:, 512:513]))
                    p.emit("vector", lambda E, f=fo: E.scalar_tensor_tensor(
                        out=BP, in0=BP, scalar=VEC["eu"][:, f:f + 1], in1=PBUF[:, 0:512],
                        op0=ALU.mult, op1=ALU.add))
                    p.emit("vector", lambda E, f=fo: E.scalar_tensor_tensor(
                        out=EK, in0=EK, scalar=VEC["eu"][:, f:f + 1], in1=QBUF[:, 0:512],
                        op0=ALU.mult, op1=ALU.add))
                    cs = p.emit("vector", lambda E: E.reciprocal(out=VV, in_=EK),
                                inc="dve")
                    p.force_wait("vector", "dve", cs)
                    p.emit("vector", lambda E: E.tensor_mul(out=BP, in0=BP, in1=VV))
                    p.wait("vector", rs_tok)
                    c = p.emit("vector", lambda E, fo=fo: E.tensor_mul(
                        out=RW[:, fo * QT:(fo + 1) * QT], in0=BP,
                        in1=RSIG[:, fo * QT:(fo + 1) * QT]), inc="dve")
                    chain_tok = {"dve": c}
                    if "wkv1" in dbgset and b == 0 and q == 1 and fo == 0:
                        t_ = p.tok()
                        p.wait("sync", t_)
                        dma(dbg_d["wkv1"][0], PBUF, sem="out")
                        dma(dbg_d["wkv1"][1], QBUF, sem="out")
                        dma(dbg_d["wkv1"][2][:, 0:QT], BP, sem="out")
                        dma(dbg_d["wkv1"][3][:, 0:QT], EK, sem="out")
                        p.fence()
                    if "wkv0" in dbgset and b == 0 and q == 0 and fo == 0:
                        t_ = p.tok()
                        p.wait("sync", t_)
                        dma(dbg_d["wkv0"][0], PBUF, sem="out")
                        dma(dbg_d["wkv0"][1], QBUF, sem="out")
                        dma(dbg_d["wkv0"][2][:, 0:8], VEC["pcar"], sem="out")
                        dma(dbg_d["wkv0"][3][:, 0:8], VEC["qcar"], sem="out")
                        p.fence()
                rw_tok = {"dve": c}

                kv_pe = p.tok("pe")
                wo_tok = dict()
                for ki in range(F):
                    dst = (MIXK[:, ki * D:(ki + 1) * D] if ki < 4
                           else MIXV[:, (ki - 4) * D:(ki - 3) * D])
                    wo_tok = load_cast(dst, wor[ki], WSTG, 2, D, cast_wait=kv_pe)

                def wo_tile(ki, fo):
                    if ki < 4:
                        return MIXK[:, ki * D + fo * 128:ki * D + fo * 128 + 128]
                    return MIXV[:, (ki - 4) * D + fo * 128:(ki - 4) * D + fo * 128 + 128]

                for fo in range(F):
                    bk = fo % 4
                    tko = mm_group(bk, [(wo_tile(ki, fo), RW[:, ki * QT:(ki + 1) * QT])
                                        for ki in range(F)], dict(rw_tok, **wo_tok))
                    dst = st_f(fo)[:, t0:t0 + QT]
                    consume(bk, "vector", lambda E, dst=dst, bk=bk: E.tensor_add(
                        out=dst, in0=dst, in1=pb[bk][:, 0:QT]), tko, "dve")
                prev_wo_pe = p.tok("pe")
            p.fence()
            ar.pop()
            if bdbg:
                dbg_dump("xa_att", st_f)
            if upto < 3:
                continue

            ar.push()
            FWR = ar.bf16(F * D)
            WKC = ar.bf16(F * CH)
            WVR = ar.bf16(4 * 1024)
            WSTG2 = ar.f32(2 * 1024)
            H2BUF = ar.bf16(F * 516)
            KFIN = ar.bf16(F * QT)
            MIXR2 = ar.bf16(F * QT)
            RFSIG = ar.bf16(F * QT)
            UFULL = ar.bf16(32 * QT)

            def h2(f):
                return H2BUF[:, f * 516:(f + 1) * 516]

            state["wslot"] = 0
            state["wtok"] = [dict(), dict()]
            p.emit("gpsimd", lambda E: E.memset(hcar, 0.0), inc="pool")
            fwrr = fwr_d.rearrange("(k p) o -> k p o", p=128)
            fwr_tok = dict()
            for ki in range(F):
                fwr_tok = load_cast(FWR[:, ki * D:(ki + 1) * D], fwrr[ki], WSTG2, 2, D)

            prev_q_pe = dict()
            for q in range(NQ):
                t0 = q * QT
                stok = ln_stats(st_f, t0, QT, UFULL[:, 0:F * QT],
                                UFULL[:, 2 * F * QT:3 * F * QT], prev_q_pe)
                btok = ln_bcast(QT, stok)
                p.wait("vector", btok)
                p.wait("vector", prev_q_pe)
                for f in range(F):
                    src = st_f(f)[:, t0:t0 + QT]
                    p.emit("vector", lambda E, src=src: E.tensor_mul(
                        out=SCRF, in0=src, in1=pb[4][:, 0:QT]))
                    p.emit("vector", lambda E: E.tensor_sub(
                        out=SCRF, in0=SCRF, in1=pb[5][:, 0:QT]))
                    p.emit("vector", lambda E, f=f: E.tensor_copy(
                        out=h2(f)[:, 0:1], in_=hcar[:, f:f + 1]))
                    cs = p.emit("vector", lambda E, f=f: E.tensor_scalar(
                        out=h2(f)[:, 1:513], in0=SCRF, scalar1=VEC["lfg"][:, f:f + 1],
                        scalar2=VEC["lfb"][:, f:f + 1], op0=ALU.mult, op1=ALU.add),
                        inc="dve")
                    p.force_wait("vector", "dve", cs)
                    c = p.emit("vector", lambda E, f=f: E.tensor_copy(
                        out=hcar[:, f:f + 1], in_=h2(f)[:, 512:513]), inc="dve")
                seal45({"dve": c})
                for f in range(F):
                    p.emit("vector", lambda E, f=f: E.tensor_sub(
                        out=SCRB[:, 0:QT], in0=h2(f)[:, 1:513], in1=h2(f)[:, 0:512]))
                    for mx, buf in (("fmk", KFIN), ("fmr", MIXR2)):
                        p.emit("vector", lambda E, f=f, mx=mx: E.tensor_single_scalar(
                            out=SCRB[:, QT:2 * QT], in_=SCRB[:, 0:QT],
                            scalar=VEC[mx][:, f:f + 1], op=ALU.mult))
                        c = p.emit("vector", lambda E, f=f, buf=buf: E.tensor_add(
                            out=buf[:, f * QT:(f + 1) * QT], in0=SCRB[:, QT:2 * QT],
                            in1=h2(f)[:, 0:512]), inc="dve")
                mix_tok = dict(fwr_tok, dve=c)

                rf_tok = {}
                rf_war = p.tok("dve")
                for fo in range(F):
                    bk = fo % 4
                    tkr = mm_group(bk, [(FWR[:, ki * D + fo * 128:ki * D + fo * 128 + 128],
                                         MIXR2[:, ki * QT:(ki + 1) * QT]) for ki in range(F)],
                                   mix_tok)
                    rf_tok = consume(bk, "scalar", lambda E, fo=fo, bk=bk: E.activation(
                        out=RFSIG[:, fo * QT:(fo + 1) * QT], in_=pb[bk][:, 0:QT],
                        func=AF.Sigmoid), dict(tkr, **rf_war), "act")

                u_tok = dict()
                wkc_war = dict(prev_q_pe)
                first_bq = (b == 0 and q == 0)
                for hc in range(DH // CH):
                    if not first_bq:
                        wk_tok = dma(WKC[:, 0:F * CH], fwk_c_d[hc],
                                     wait_tok=dict(wkc_war,
                                                   **cache_tok[("fwk", hc)]))
                    else:
                        wk_tok = dict()
                        ww = dict(wkc_war, **cache_tok.get(("fwkwr",), {}))
                        for ki in range(F):
                            wk_tok = load_cast(WKC[:, ki * CH:(ki + 1) * CH],
                                               fwk_d[ki * 128:(ki + 1) * 128,
                                                     hc * CH:(hc + 1) * CH],
                                               WSTG2, 2, CH, cast_wait=ww)
                        wt = dma(fwk_c_d[hc], WKC[:, 0:F * CH], wait_tok=wk_tok)
                        cache_tok[("fwk", hc)] = wt
                        cache_tok[("fwkwr",)] = wt
                    for ht in range(CH // 128):
                        bk = 4 + (ht % 2)
                        hk = hc * 4 + ht
                        tku = mm_group(bk, [(WKC[:, ki * CH + ht * 128:ki * CH + ht * 128 + 128],
                                             KFIN[:, ki * QT:(ki + 1) * QT]) for ki in range(F)],
                                       dict(mix_tok, **wk_tok))
                        tmax = consume(bk, "vector", lambda E, bk=bk: E.tensor_single_scalar(
                            out=SCRF, in_=pb[bk][:, 0:QT], scalar=0.0, op=ALU.max),
                            tku, "dve")
                        p.wait("scalar", tmax)
                        ca = p.emit("scalar", lambda E, hk=hk: E.activation(
                            out=UFULL[:, hk * QT:(hk + 1) * QT], in_=SCRF,
                            func=AF.Square), inc="act")
                        p.wait("vector", {"act": ca})
                        u_tok = {"act": ca}
                    wkc_war = p.tok("pe")

                if "bdump" in dbgset and b == 0 and q == 0:
                    t_ = p.tok()
                    p.wait("sync", t_)
                    dma(dbg_d["bdump"][0][:, 0:F * QT], KFIN, sem="out")
                    dma(dbg_d["bdump"][1][:, 0:F * QT], RFSIG, sem="out")
                    dma(dbg_d["bdump"][2][:, 0:F * QT], MIXR2, sem="out")
                    dma(dbg_d["bdump"][3], UFULL, sem="out")
                    p.fence()
                for fo in range(F):
                    p.wait("tensor", bank_state[fo])
                ring_tok = [dict() for _ in range(4)]
                ztoks = {}
                for hk in range(32):
                    s = hk % 4
                    if not first_bq:
                        wvt = dma(WVR[:, s * 1024:(s + 1) * 1024], fwv_c_d[hk],
                                  wait_tok=dict(ring_tok[s],
                                                **cache_tok[("fwv", hk)]))
                    else:
                        ww = dict(ring_tok[s], **cache_tok.get(("fwvwr", s), {}))
                        wvt = load_cast(WVR[:, s * 1024:(s + 1) * 1024],
                                        fwv_d[hk * 128:(hk + 1) * 128, :],
                                        WSTG2, 2, D, cast_wait=ww)
                        wt = dma(fwv_c_d[hk], WVR[:, s * 1024:(s + 1) * 1024],
                                 wait_tok=wvt)
                        cache_tok[("fwv", hk)] = wt
                        cache_tok[("fwvwr", s)] = wt
                    p.wait("tensor", dict(wvt, **u_tok))
                    for fo in range(F):
                        last = (hk == 31)
                        cpe = p.emit("tensor", lambda E, s=s, fo=fo, hk=hk:
                                     E.matmul(pb[fo][:, 0:QT],
                                              WVR[:, s * 1024 + fo * 128:s * 1024 + fo * 128 + 128],
                                              UFULL[:, hk * QT:(hk + 1) * QT],
                                              start=(hk == 0), stop=(hk == 31)),
                                     inc="pe" if (last or fo == F - 1) else None)
                        if last:
                            ztoks[fo] = {"pe": cpe}
                    ring_tok[s] = {"pe": p.cnt.get("pe", 0)}
                for fo in range(F):
                    dst = st_f(fo)[:, t0:t0 + QT]
                    p.wait("vector", ztoks[fo])
                    p.wait("vector", rf_tok)
                    p.emit("vector", lambda E, fo=fo: E.tensor_mul(
                        out=SCRF, in0=pb[fo][:, 0:QT],
                        in1=RFSIG[:, fo * QT:(fo + 1) * QT]))
                    c = p.emit("vector", lambda E, dst=dst: E.tensor_add(
                        out=dst, in0=dst, in1=SCRF), inc="dve")
                    bank_state[fo] = {"dve": c}
                prev_q_pe = p.tok("pe")
            p.fence()
            ar.pop()
            if bdbg:
                dbg_dump("xa_ffn", st_f)
            if upto < 4:
                continue

            ar.push()
            STGC = ar.f32(3 * 1024)
            SMFC = ar.f32(F * CH)
            XCBc = ar.bf16(F * CH)
            SQBc = ar.bf16(F * CH)
            stg_tok = [dict(), dict(), dict()]
            smf_free = dict()
            for ch in range(NCH):
                stok = ln_stats(st_f, ch * CH, CH, XCBc, SQBc, dict())
                tr_tok = load_transpose_x0(b, ch, STGC, SMFC, stg_tok, smf_free)
                btok = ln_bcast(CH, stok)
                p.wait("vector", btok)
                p.wait("vector", tr_tok)
                for f in range(F):
                    dst = st_f(f)[:, ch * CH:(ch + 1) * CH]
                    p.emit("vector", lambda E, dst=dst: E.tensor_mul(
                        out=SCRF, in0=dst, in1=pb[4][:, 0:CH]))
                    p.emit("vector", lambda E: E.tensor_sub(
                        out=SCRF, in0=SCRF, in1=pb[5][:, 0:CH]))
                    c = p.emit("vector", lambda E, dst=dst, f=f: E.tensor_add(
                        out=dst, in0=SCRF, in1=SMFC[:, f * CH:(f + 1) * CH]), inc="dve")
                seal45({"dve": c})
                smf_free = {"dve": c}
            p.fence()
            ar.pop()
            if bdbg:
                dbg_dump("x1", st_f)
            if upto < 5:
                continue

            ar.push()
            ACC = ar.f32(F * HALF)
            HPB = ar.bf16(F * HALF)
            WA0 = ar.bf16(F * HALF)
            WA1 = ar.bf16(F * HALF)
            wbc_off = ar.off
            WBC = ar.bf16(F * HALF)
            UALL = ar.bf16(F * QT)
            WSTG3 = ar.f32(2 * 2048)
            SMFD = ar.f32(2 * D)
            XCBd = WA0
            SQBd = WA1
            rW1 = ar.f32_at(wbc_off, 2048)
            lcd = mk_stager(WSTG3, 2048, 2)

            W1r = rW1_d.rearrange("(k p) h -> k p h", p=128)
            for k in range(F):
                rw1_tok = dma(rW1[:, k * 256:(k + 1) * 256], W1r[k])
            p.wait("tensor", rw1_tok)

            state["wslot"] = 0
            state["wtok"] = [dict(), dict()]

            for ch in range(NCH):
                stok = ln_stats(st_f, ch * CH, CH, XCBd[:, 0:F * CH], SQBd[:, 0:F * CH],
                                dict())
                btok = ln_bcast(CH, stok)
                p.wait("vector", btok)
                for f in range(F):
                    src = st_f(f)[:, ch * CH:(ch + 1) * CH]
                    p.emit("vector", lambda E, src=src: E.tensor_mul(
                        out=SCRF, in0=src, in1=pb[4][:, 0:CH]))
                    cs = p.emit("vector", lambda E: E.tensor_sub(
                        out=SCRF, in0=SCRF, in1=pb[5][:, 0:CH]), inc="dve")
                    p.force_wait("vector", "dve", cs)
                    cs = p.emit("vector", lambda E: E.reduce_sum(
                        out=VEC["ptmp"][:, 0:1], in_=SCRF, axis=AX.X), inc="dve")
                    p.force_wait("vector", "dve", cs)
                    c = p.emit("vector", lambda E, f=f: E.tensor_add(
                        out=VEC["pool8"][:, f:f + 1], in0=VEC["pool8"][:, f:f + 1],
                        in1=VEC["ptmp"][:, 0:1]), inc="dve")
                seal45({"dve": c})
            c = p.emit("vector", lambda E: E.tensor_single_scalar(
                out=VEC["pool8"], in_=VEC["pool8"], scalar=1.0 / L, op=ALU.mult),
                inc="dve")
            pool_tok = {"dve": c}

            for hk in range(2):
                tkh = mm_group(4, [(rW1[:, ki * 256 + hk * 128:ki * 256 + hk * 128 + 128],
                                    VEC["pool8"][:, ki:ki + 1]) for ki in range(F)],
                               pool_tok, bslice=pb[4][:, 0:1])
                consume(4, "scalar", lambda E, hk=hk: E.activation(
                    out=rh2[:, hk:hk + 1], in_=pb[4][:, 0:1], func=AF.Tanh,
                    bias=rb1c[:, hk:hk + 1]), tkh, "act")
            tkl = mm_group(5, [(rh2[:, hk:hk + 1], rW2[:, hk * 8:(hk + 1) * 8])
                               for hk in range(2)], p.tok("act"),
                           bslice=pb[5][0:1, 0:8])
            ltok = consume(5, "vector", lambda E: E.tensor_add(
                out=rrow[0:1, 0:8], in0=pb[5][0:1, 0:8], in1=b2row[0:1, 0:8]),
                tkl, "dve")
            p.wait("vector", ltok)

            def vstep(fn):
                c = p.emit("vector", fn, inc="dve")
                p.force_wait("vector", "dve", c)
                return c

            vstep(lambda E: E.reduce_max(out=srow1[0:1, 0:1],
                                         in_=rrow[0:1, 0:8], axis=AX.X))
            c = vstep(lambda E: E.tensor_single_scalar(
                out=rrow[0:1, 0:8], in_=rrow[0:1, 0:8], scalar=srow1[0:1, 0:1],
                op=ALU.subtract))
            p.wait("scalar", {"dve": c})
            ca = p.emit("scalar", lambda E: E.activation(
                out=rrow[0:1, 0:8], in_=rrow[0:1, 0:8], func=AF.Exp), inc="act")
            p.wait("vector", {"act": ca})
            vstep(lambda E: E.reduce_sum(out=srow1[0:1, 0:1],
                                         in_=rrow[0:1, 0:8], axis=AX.X))
            vstep(lambda E: E.reciprocal(out=srow1[0:1, 4:5],
                                         in_=srow1[0:1, 0:1]))
            vstep(lambda E: E.tensor_single_scalar(
                out=rrow[0:1, 0:8], in_=rrow[0:1, 0:8], scalar=srow1[0:1, 4:5],
                op=ALU.mult))
            vstep(lambda E: E.reduce_max(out=srow1[0:1, 1:2],
                                         in_=rrow[0:1, 0:8], axis=AX.X))
            vstep(lambda E: E.tensor_single_scalar(
                out=rrow2[0:1, 0:8], in_=rrow[0:1, 0:8], scalar=srow1[0:1, 1:2],
                op=ALU.is_ge))
            vstep(lambda E: E.scalar_tensor_tensor(
                out=rrow2[0:1, 0:8], in0=rrow2[0:1, 0:8], scalar=-1e30,
                in1=rrow[0:1, 0:8], op0=ALU.mult, op1=ALU.add))
            vstep(lambda E: E.reduce_max(out=srow1[0:1, 2:3],
                                         in_=rrow2[0:1, 0:8], axis=AX.X))
            vstep(lambda E: E.tensor_single_scalar(
                out=srow2[0:1, 0:8], in_=rrow2[0:1, 0:8], scalar=srow1[0:1, 2:3],
                op=ALU.is_ge))
            vstep(lambda E: E.scalar_tensor_tensor(
                out=rrow2[0:1, 0:8], in0=srow2[0:1, 0:8], scalar=-1e30,
                in1=rrow2[0:1, 0:8], op0=ALU.mult, op1=ALU.add))
            vstep(lambda E: E.reduce_max(out=srow1[0:1, 3:4],
                                         in_=rrow2[0:1, 0:8], axis=AX.X))
            vstep(lambda E: E.tensor_single_scalar(
                out=srow2[0:1, 0:8], in_=rrow[0:1, 0:8], scalar=srow1[0:1, 3:4],
                op=ALU.is_ge))
            vstep(lambda E: E.tensor_mul(
                out=rrow2[0:1, 0:8], in0=rrow[0:1, 0:8], in1=srow2[0:1, 0:8]))
            vstep(lambda E: E.reduce_sum(out=srow1[0:1, 0:1],
                                         in_=rrow2[0:1, 0:8], axis=AX.X))
            vstep(lambda E: E.reciprocal(out=srow1[0:1, 5:6],
                                         in_=srow1[0:1, 0:1]))
            c = vstep(lambda E: E.tensor_single_scalar(
                out=rrow2[0:1, 0:8], in_=rrow2[0:1, 0:8], scalar=srow1[0:1, 5:6],
                op=ALU.mult))
            tkw = mm_group(4, [(ones_row_f[0:1, :], rrow2[0:1, 0:8])], {"dve": c},
                           bslice=pb[4][:, 0:8])
            wbc_tok = consume(4, "vector", lambda E: E.tensor_copy(
                out=VEC["wbc"], in_=pb[4][:, 0:8]), tkw, "dve")
            if bdbg and "rdump" in dbgset:
                t_ = p.tok()
                p.wait("sync", t_)
                dma(dbg_d["rdump"][0][:, 0:8], VEC["pool8"], sem="out")
                dma(dbg_d["rdump"][1][:, 0:2], rh2, sem="out")
                dma(dbg_d["rdump"][2][0:1, 0:16], rrow[0:1, 0:16], sem="out")
                dma(dbg_d["rdump"][3][0:1, 0:16], rrow2[0:1, 0:16], sem="out")
                p.fence()
            if bdbg:
                dbg_dump("wrouter", lambda f: VEC["wbc"], nf=1)
            if upto < 6:
                p.fence()
                ar.pop()
                continue

            for hf in range(2):
                p.fence()
                ht0 = hf * HALF
                acc_tok = dict()
                cp = None
                for ch in range(2):
                    t0 = ht0 + ch * CH
                    stok = ln_stats(st_f, t0, CH, XCBd[:, 0:F * CH], SQBd[:, 0:F * CH],
                                    acc_tok)
                    btok = ln_bcast(CH, stok)
                    p.wait("vector", btok)
                    for f in range(F):
                        src = st_f(f)[:, t0:t0 + CH]
                        dst = ACC[:, f * HALF + ch * CH:f * HALF + (ch + 1) * CH]
                        p.emit("vector", lambda E, src=src, dst=dst: E.tensor_mul(
                            out=dst, in0=src, in1=pb[4][:, 0:CH]))
                        c = p.emit("vector", lambda E, dst=dst: E.tensor_sub(
                            out=dst, in0=dst, in1=pb[5][:, 0:CH]), inc="dve")
                        p.wait("gpsimd", {"dve": c})
                        cp = p.emit("gpsimd", lambda E, dst=dst, f=f, ch=ch:
                                    E.tensor_copy(out=HPB[:, f * HALF + ch * CH:
                                                          f * HALF + (ch + 1) * CH],
                                                  in_=dst), inc="pool")
                    seal45({"dve": c})
                    acc_tok = {"dve": c, "pool": cp}
                hp_tok = dict(acc_tok, **wbc_tok)

                wa_bufs = [WA0, WA1]
                war3 = wa_d.rearrange("z (ki p) h -> z p ki h", p=128)
                wbr3 = wb_d.rearrange("z (hk p) o -> z p hk o", p=128)

                first_pass = (b == 0 and hf == 0)

                def load_wa(it, war_tok):
                    ez, hc = divmod(it, 4)
                    wab = wa_bufs[it % 2]
                    if not first_pass:
                        return dma(wab[:, 0:F * HALF], wa_c_d[ez, hc],
                                   wait_tok=dict(war_tok, **cache_tok[("wa", it)]))
                    t = dict()
                    ww = dict(war_tok, **cache_tok.get(("wawr", it % 2), {}))
                    for k2 in range(4):
                        t = lcd(wab[:, k2 * 2 * HALF:(k2 + 1) * 2 * HALF],
                                war3[ez][:, 2 * k2:2 * k2 + 2,
                                          hc * HALF:(hc + 1) * HALF],
                                2 * HALF, eng="gpsimd", cast_wait=ww)
                    wt = dma(wa_c_d[ez, hc], wab[:, 0:F * HALF], wait_tok=t)
                    cache_tok[("wa", it)] = wt
                    cache_tok[("wawr", it % 2)] = wt
                    return t

                def load_wb(it, war_tok):
                    ez, hc = divmod(it, 4)
                    if not first_pass:
                        return dma(WBC[:, 0:F * HALF], wb_c_d[ez, hc],
                                   wait_tok=dict(war_tok, **cache_tok[("wb", it)]))
                    t = dict()
                    ww = dict(war_tok, **cache_tok.get(("wbwr",), {}))
                    for j2 in range(4):
                        t = lcd(WBC[:, j2 * 2 * HALF:(j2 + 1) * 2 * HALF],
                                wbr3[ez][:, hc * 8 + 2 * j2:hc * 8 + 2 * j2 + 2, :],
                                2 * HALF, eng="gpsimd", cast_wait=ww)
                    wt = dma(wb_c_d[ez, hc], WBC[:, 0:F * HALF], wait_tok=t)
                    cache_tok[("wb", it)] = wt
                    cache_tok[("wbwr",)] = wt
                    return t

                watoks = {0: load_wa(0, dict(hp_tok))}
                for ez in range(NZ):
                    for hc in range(4):
                        it = ez * 4 + hc
                        wab = wa_bufs[it % 2]
                        wbtok = load_wb(it, dict(hp_tok) if it == 0 else p.tok("pe"))
                        if it + 1 < NZ * 4:
                            watoks[it + 1] = load_wa(it + 1, p.tok("pe"))
                        watok = watoks[it]
                        for tw in range(2):
                            ug = dict()
                            for ht in range(F):
                                bk = 4 + (ht % 2)
                                tku = mm_group(bk, [
                                    (wab[:, ki * HALF + ht * 128:ki * HALF + ht * 128 + 128],
                                     HPB[:, ki * HALF + tw * QT:ki * HALF + (tw + 1) * QT])
                                    for ki in range(F)], dict(watok, **hp_tok))
                                ug = consume(bk, "scalar", lambda E, ht=ht, bk=bk:
                                             E.activation(out=UALL[:, ht * QT:(ht + 1) * QT],
                                                          in_=pb[bk][:, 0:QT],
                                                          func=AF.Gelu_apprx_tanh),
                                             tku, "act")
                            for fg in range(2):
                                for j in range(4):
                                    fo = fg * 4 + j
                                    tkz = mm_group(j, [
                                        (WBC[:, ht * HALF + fo * 128:ht * HALF + fo * 128 + 128],
                                         UALL[:, ht * QT:(ht + 1) * QT])
                                        for ht in range(F)], dict(wbtok, **ug))
                                    dst = ACC[:, fo * HALF + tw * QT:fo * HALF + (tw + 1) * QT]
                                    consume(j, "vector", lambda E, dst=dst, j=j, ez=ez:
                                            E.scalar_tensor_tensor(
                                                out=dst, in0=pb[j][:, 0:QT],
                                                scalar=VEC["wbc"][:, ez:ez + 1],
                                                in1=dst, op0=ALU.mult, op1=ALU.add),
                                            tkz, "dve")
                if bdbg and hf == 0:
                    dbg_dump("acc0", lambda f: ACC[:, f * HALF:(f + 1) * HALF])
                    dbg_dump("vecdump", lambda f: VEC["wbc"], nf=1)

                if upto < 7:
                    continue
                zdone = p.tok("pe", "dve")
                smfd_tok = [dict(), dict()]
                for ch in range(2):
                    t0 = ht0 + ch * CH
                    stok = ln_stats(lambda f: ACC[:, f * HALF:(f + 1) * HALF],
                                    ch * CH, CH, XCBd[:, 0:F * CH], SQBd[:, 0:F * CH],
                                    zdone)
                    btok = ln_bcast(CH, stok)
                    p.wait("vector", btok)
                    for f in range(F):
                        asrc = ACC[:, f * HALF + ch * CH:f * HALF + (ch + 1) * CH]
                        x1src = st_f(f)[:, t0:t0 + CH]
                        p.emit("vector", lambda E, asrc=asrc: E.tensor_mul(
                            out=SCRF, in0=asrc, in1=pb[4][:, 0:CH]))
                        p.emit("vector", lambda E: E.tensor_sub(
                            out=SCRF, in0=SCRF, in1=pb[5][:, 0:CH]))
                        c = p.emit("vector", lambda E, asrc=asrc, x1src=x1src:
                                   E.tensor_add(out=asrc, in0=SCRF, in1=x1src),
                                   inc="dve")
                    seal45({"dve": c})
                    och = {"dve": c}
                    for i in range(CH // 128):
                        tt = (t0 + i * 128) // 128
                        sd = i % 2
                        ot = dict()
                        for f in range(F):
                            bk = 6 + (f % 2)
                            p.wait("tensor", bank_state[bk])
                            p.wait("tensor", och)
                            cpe = p.emit("tensor", lambda E, bk=bk, f=f, ch=ch, i=i:
                                         E.transpose(pb[bk][:, 0:128],
                                                     ACC[:, f * HALF + ch * CH + i * 128:
                                                         f * HALF + ch * CH + (i + 1) * 128],
                                                     ident), inc="pe")
                            dst = SMFD[:, sd * D + f * 128:sd * D + (f + 1) * 128]
                            ot = consume(bk, "vector", lambda E, dst=dst, bk=bk:
                                         E.tensor_copy(out=dst, in_=pb[bk][:, 0:128]),
                                         dict({"pe": cpe}, **smfd_tok[sd]), "dve")
                        smfd_tok[sd] = dma(out_d[b, tt], SMFD[:, sd * D:(sd + 1) * D],
                                           sem="out", wait_tok=ot)
            p.fence()
            ar.pop()

        p.wait("gpsimd", p.tok("out"))

        block = es.enter_context(nc.Block())

        @block.sync
        def _(E):
            for fn in p.q["sync"]:
                fn(E)

        @block.gpsimd
        def _(E):
            for fn in p.q["gpsimd"]:
                fn(E)

        @block.vector
        def _(E):
            for fn in p.q["vector"]:
                fn(E)

        @block.scalar
        def _(E):
            for fn in p.q["scalar"]:
                fn(E)

        @block.tensor
        def _(E):
            for fn in p.q["tensor"]:
                fn(E)

    return nc


def kernel(**inputs):
    x = np.ascontiguousarray(inputs["L0_attn_x"])
    xs = x.reshape(NCORES, BPC, L // 128, 128, D)
    nc = build(debug_names=tuple(DBG), upto=UPTO)
    in_maps = []
    for c in range(NCORES):
        m = {"x": np.ascontiguousarray(xs[c])}
        for k, v in inputs.items():
            if k != "L0_attn_x":
                m[k] = np.ascontiguousarray(v)
        in_maps.append(m)
    res = run_bass_kernel_spmd(nc, in_maps, core_ids=list(range(NCORES)))
    outs = [np.asarray(r["out"]).reshape(BPC, L, D) for r in res.results]
    return np.concatenate(outs, axis=0)


# version 04
# speedup vs baseline: 1.1972x; 1.0229x over previous
import numpy as np
from contextlib import ExitStack
import concourse.bass as bass
import concourse.mybir as mybir
from concourse.bass_utils import run_bass_kernel_spmd

F32 = mybir.dt.float32
BF16 = mybir.dt.bfloat16
U8 = mybir.dt.uint8
AF = mybir.ActivationFunctionType
ALU = mybir.AluOpType
AX = mybir.AxisListType

NCORES = 8
BPC = 2
L = 2048
D = 1024
F = 8
DH = 4096
NZ = 8
EPS = 1e-5
CH = 512
NCH = L // CH
QT = 512
NQ = L // QT
HALF = 1024
ONES_SCALE = 1.0 / 1024.0

ENGINES = ("sync", "gpsimd", "vector", "scalar", "tensor")

DBG = ()
DBG_BATCH = 0
UPTO = 99


class Prog:
    def __init__(self):
        self.q = {e: [] for e in ENGINES}
        self.cnt = {}
        self.wm = {e: {} for e in ENGINES}
        self.sems = {}
        self.evlog = {e: [] for e in ENGINES}

    def check_deadlock(self):
        ptr = {e: 0 for e in ENGINES}
        val = {}
        progress = True
        while progress:
            progress = False
            for e in ENGINES:
                log = self.evlog[e]
                while ptr[e] < len(log):
                    kind, sem, c = log[ptr[e]]
                    if kind == "w" and val.get(sem, 0) < c:
                        break
                    if kind == "i":
                        val[sem] = val.get(sem, 0) + c
                    ptr[e] += 1
                    progress = True
        stuck = [(e, ptr[e], len(self.evlog[e]), self.evlog[e][ptr[e]])
                 for e in ENGINES if ptr[e] < len(self.evlog[e])]
        return stuck

    def tok(self, *sems):
        if not sems:
            return {s: c for s, c in self.cnt.items() if c > 0}
        return {s: self.cnt.get(s, 0) for s in sems if self.cnt.get(s, 0) > 0}

    def wait(self, eng, tokd):
        assert eng in ENGINES
        for s, c in tokd.items():
            if c > self.wm[eng].get(s, 0):
                self.wm[eng][s] = c
                self.evlog[eng].append(("w", s, c))
                self.q[eng].append(
                    lambda E, self=self, s=s, c=c: E.wait_ge(self.sems[s], c))

    def emit(self, eng, fn, inc=None, step=1):
        assert eng in ENGINES
        if inc is None:
            self.q[eng].append(fn)
            self.evlog[eng].append(("n", None, 0))
            return None
        self.cnt[inc] = self.cnt.get(inc, 0) + step
        c = self.cnt[inc]
        self.evlog[eng].append(("i", inc, step))
        self.q[eng].append(
            lambda E, self=self, fn=fn, inc=inc, step=step:
                fn(E).then_inc(self.sems[inc], step))
        self.wm[eng][inc] = c
        return c

    def force_wait(self, eng, sem, c):
        self.evlog[eng].append(("w", sem, c))
        self.q[eng].append(
            lambda E, self=self, sem=sem, c=c: E.wait_ge(self.sems[sem], c))
        if c > self.wm[eng].get(sem, 0):
            self.wm[eng][sem] = c

    def fence(self):
        t = self.tok()
        for e in ENGINES:
            self.wait(e, t)


class Arena:
    def __init__(self, t, nbytes):
        self.t = t
        self.nbytes = nbytes
        self.off = 0
        self.marks = []
        self.peak = 0

    def push(self):
        self.marks.append(self.off)

    def pop(self):
        self.off = self.marks.pop()

    def _take(self, nbytes):
        o = self.off
        assert o % 4 == 0
        self.off += nbytes + ((4 - nbytes % 4) % 4)
        self.peak = max(self.peak, self.off)
        assert self.off <= self.nbytes, f"arena overflow {self.off} > {self.nbytes}"
        return o

    def f32(self, cols):
        o = self._take(4 * cols)
        return self.t[:, o:o + 4 * cols].bitcast(F32)

    def bf16(self, cols):
        o = self._take(2 * cols)
        return self.t[:, o:o + 2 * cols].bitcast(BF16)

    def f32_at(self, o, cols):
        return self.t[:, o:o + 4 * cols].bitcast(F32)

    def bf16_at(self, o, cols):
        return self.t[:, o:o + 2 * cols].bitcast(BF16)


def build(debug_names=(), upto=99):
    nc = bass.Bass("TRN2", target_bir_lowering=False)
    p = Prog()
    dbgset = {t[0] for t in debug_names}

    x_d = nc.dram_tensor("x", [BPC, L // 128, 128, D], F32, kind="ExternalInput").ap()
    out_d = nc.dram_tensor("out", [BPC, L // 128, 128, D], F32, kind="ExternalOutput").ap()
    vnames = {
        "lag": "L0_attn_ln_att_g", "lab": "L0_attn_ln_att_b",
        "lfg": "L0_attn_ln_ffn_g", "lfb": "L0_attn_ln_ffn_b",
        "amk": "L0_attn_att_mix_k", "amv": "L0_attn_att_mix_v",
        "amr": "L0_attn_att_mix_r", "fmk": "L0_attn_ffn_mix_k",
        "fmr": "L0_attn_ffn_mix_r", "td": "L0_attn_att_time_decay",
        "tf": "L0_attn_att_time_first"}
    vec_d = {k: nc.dram_tensor(n, [D], F32, kind="ExternalInput").ap()
             for k, n in vnames.items()}
    wk_d = nc.dram_tensor("L0_attn_att_wk", [D, D], F32, kind="ExternalInput").ap()
    wv_d = nc.dram_tensor("L0_attn_att_wv", [D, D], F32, kind="ExternalInput").ap()
    wr_d = nc.dram_tensor("L0_attn_att_wr", [D, D], F32, kind="ExternalInput").ap()
    wo_d = nc.dram_tensor("L0_attn_att_wo", [D, D], F32, kind="ExternalInput").ap()
    fwk_d = nc.dram_tensor("L0_attn_ffn_wk", [D, DH], F32, kind="ExternalInput").ap()
    fwv_d = nc.dram_tensor("L0_attn_ffn_wv", [DH, D], F32, kind="ExternalInput").ap()
    fwr_d = nc.dram_tensor("L0_attn_ffn_wr", [D, D], F32, kind="ExternalInput").ap()
    rW1_d = nc.dram_tensor("L0_ffn_W1", [D, 256], F32, kind="ExternalInput").ap()
    rb1_d = nc.dram_tensor("L0_ffn_b1", [256], F32, kind="ExternalInput").ap()
    rW2_d = nc.dram_tensor("L0_ffn_W2", [256, NZ], F32, kind="ExternalInput").ap()
    rb2_d = nc.dram_tensor("L0_ffn_b2", [NZ], F32, kind="ExternalInput").ap()
    wa_d = nc.dram_tensor("L0_ffn_Wa", [NZ, D, DH], F32, kind="ExternalInput").ap()
    wb_d = nc.dram_tensor("L0_ffn_Wb", [NZ, DH, D], F32, kind="ExternalInput").ap()
    wa_c_d = nc.dram_tensor("wa_cache", [NZ, 4, 128, F * HALF], BF16,
                            kind="Internal").ap()
    wb_c_d = nc.dram_tensor("wb_cache", [NZ, 4, 128, F * HALF], BF16,
                            kind="Internal").ap()
    fwk_c_d = nc.dram_tensor("fwk_cache", [DH // CH, 128, F * CH], BF16,
                             kind="Internal").ap()
    fwv_c_d = nc.dram_tensor("fwv_cache", [32, 128, D], BF16, kind="Internal").ap()
    dbg_d = {}
    for t in debug_names:
        n, shp = t[0], t[1]
        dt = t[2] if len(t) > 2 else F32
        dbg_d[n] = nc.dram_tensor("dbg_" + n, shp, dt, kind="ExternalOutput").ap()

    es = ExitStack()
    with es:
        AR_BYTES = 206 * 1024
        ar_t = es.enter_context(nc.sbuf_tensor("arena", [128, AR_BYTES], U8))
        ar = Arena(ar_t, AR_BYTES)
        pb = [es.enter_context(nc.psum_tensor(f"pb{i}", [128, 512], F32))
              for i in range(8)]
        for s in ("in", "out", "pe", "dve", "act", "pool",
                  "wa0", "wa1", "wbs", "wks", "wv0", "wv1", "wv2", "wv3",
                  "cwa", "cwb", "cwk", "cwv"):
            p.sems[s] = es.enter_context(nc.semaphore("s_" + s))

        ident = ar.f32(128)
        ones_col_b = ar.bf16(2)
        ones_row_f = ar.f32(128)
        VEC = {}
        for n in ("lag", "lab", "lfg", "lfb", "amk", "amv", "amr", "fmk", "fmr",
                  "td", "tf", "ew", "eu", "wbc", "pcar", "qcar", "pool8", "ptmp"):
            VEC[n] = ar.f32(8)
        hcar = ar.bf16(8)
        rb1c = ar.f32(2)
        rh2 = ar.f32(2)
        rW2 = ar.f32(16)
        cEPS = ar.f32(1)
        nc.const_aps.register(F32, EPS, cEPS[:, 0:1])
        m_row = ar.f32(512)
        rs_row = ar.f32(512)
        mrs_row = ar.f32(512)
        srow1 = ar.f32(512)
        srow2 = ar.f32(512)
        rrow = ar.f32(16)
        rrow2 = ar.f32(16)
        SCRF = ar.f32(512)
        SCRB = SCRF.bitcast(BF16)

        bank_state = [dict() for _ in range(8)]
        state = {"lnmm": {}, "wslot": 0, "wtok": [dict(), dict()]}

        def mm_group(bank, pairs, opd_tok, bslice=None, free=512):
            p.wait("tensor", bank_state[bank])
            p.wait("tensor", opd_tok)
            tgt = pb[bank][:, 0:free] if bslice is None else bslice
            n = len(pairs)
            c = None
            for i, (lt, rh) in enumerate(pairs):
                if i < n - 1:
                    p.emit("tensor", lambda E, tgt=tgt, lt=lt, rh=rh, i=i:
                           E.matmul(tgt, lt, rh, start=(i == 0), stop=False))
                else:
                    c = p.emit("tensor", lambda E, tgt=tgt, lt=lt, rh=rh, i=i:
                               E.matmul(tgt, lt, rh, start=(i == 0), stop=True),
                               inc="pe")
            return {"pe": c}

        def consume(bank, eng, fn, tokd, sem):
            p.wait(eng, tokd)
            c = p.emit(eng, fn, inc=sem)
            bank_state[bank] = {sem: c}
            return {sem: c}

        def dma(dst, src, sem="in", wait_tok=None):
            if wait_tok:
                p.wait("sync", wait_tok)
            return {sem: p.emit("sync", lambda E, dst=dst, src=src:
                                E.dma_start(out=dst, in_=src), inc=sem, step=16)}

        def load_cast(dst, dram_src, wstg, nst, cols, cast_wait=None, scale_col=None):
            s = state["wslot"] % nst
            state["wslot"] += 1
            while len(state["wtok"]) < nst:
                state["wtok"].append(dict())
            stg = wstg[:, s * 1024:s * 1024 + cols]
            t = dma(stg, dram_src, wait_tok=state["wtok"][s])
            p.wait("gpsimd", t)
            if cast_wait:
                p.wait("gpsimd", cast_wait)
            if scale_col is None:
                c = p.emit("gpsimd", lambda E, dst=dst, stg=stg:
                           E.tensor_copy(out=dst, in_=stg), inc="pool")
            else:
                c = p.emit("gpsimd", lambda E, dst=dst, stg=stg, sc=scale_col:
                           E.tensor_single_scalar(out=dst, in_=stg, scalar=sc,
                                                  op=ALU.mult), inc="pool")
            state["wtok"][s] = {"pool": c}
            return {"pool": c}

        def mk_stager(wstg_ap, slot_cols, nslots):
            stt = {"i": 0, "tok": [dict() for _ in range(nslots)]}

            def lc(dst, dram_src, cols, eng="vector", cast_wait=None):
                s = stt["i"] % nslots
                stt["i"] += 1
                stg = wstg_ap[:, s * slot_cols:s * slot_cols + cols]
                t = dma(stg, dram_src, wait_tok=stt["tok"][s])
                sem = {"vector": "dve", "scalar": "act", "gpsimd": "pool"}[eng]
                p.wait(eng, t)
                if cast_wait:
                    p.wait(eng, cast_wait)
                if eng == "scalar":
                    c = p.emit("scalar", lambda E, dst=dst, stg=stg:
                               E.activation(out=dst, in_=stg, func=AF.Copy),
                               inc="act")
                else:
                    c = p.emit(eng, lambda E, dst=dst, stg=stg:
                               E.tensor_copy(out=dst, in_=stg), inc=sem)
                stt["tok"][s] = {sem: c}
                return {sem: c}
            return lc

        p.emit("gpsimd", lambda E: E.memset(ident, 1.0))
        p.emit("gpsimd", lambda E: E.affine_select(
            out=ident, in_=ident, pattern=[[-1, 128]], compare_op=ALU.is_equal,
            fill=0.0, base=0, channel_multiplier=1))
        p.emit("gpsimd", lambda E: E.memset(ones_col_b[:, 0:1], ONES_SCALE))
        p.emit("gpsimd", lambda E: E.memset(ones_col_b[:, 1:2], 1.0))
        p.emit("gpsimd", lambda E: E.memset(cEPS, EPS))
        c = p.emit("gpsimd", lambda E: E.memset(ones_row_f[0:1, :], 1.0), inc="pool")
        const_tok = {"pool": c}
        for e in ENGINES:
            p.wait(e, const_tok)

        vtok = {}
        for key in vnames:
            src = vec_d[key].rearrange("(f p o) -> f p o", p=128, o=1)
            for f in range(F):
                vtok = dma(VEC[key][:, f:f + 1], src[f])
        W2r = rW2_d.rearrange("(k p) z -> k p z", p=128)
        for k in range(2):
            vtok = dma(rW2[:, k * 8:(k + 1) * 8], W2r[k])
        b1r = rb1_d.rearrange("(k p o) -> k p o", p=128, o=1)
        for k in range(2):
            vtok = dma(rb1c[:, k:k + 1], b1r[k])
        b2row = ar.f32(8)
        vtok = dma(b2row[0:1, 0:8], rb2_d.rearrange("(o z) -> o z", o=1))

        p.wait("scalar", vtok)
        p.emit("scalar", lambda E: E.activation(out=VEC["ew"], in_=VEC["td"], func=AF.Exp))
        p.emit("scalar", lambda E: E.activation(out=VEC["ew"], in_=VEC["ew"], func=AF.Exp, scale=-1.0))
        c = p.emit("scalar", lambda E: E.activation(out=VEC["eu"], in_=VEC["tf"], func=AF.Exp), inc="act")
        ew_tok = {"act": c}

        STREAM = ar.f32(F * L)

        def st_f(f):
            return STREAM[:, f * L:(f + 1) * L]

        def ln_stats(src_f, t0, n, xcb, sqb, scratch_tok):
            p.wait("gpsimd", scratch_tok)
            p.wait("gpsimd", state["lnmm"])
            p.wait("scalar", scratch_tok)
            p.wait("scalar", state["lnmm"])
            for f in range(F):
                src = src_f(f)[:, t0:t0 + n]
                p.emit("gpsimd", lambda E, d=xcb[:, f * n:(f + 1) * n], s=src:
                       E.tensor_copy(out=d, in_=s), inc="pool")
                p.emit("scalar", lambda E, d=sqb[:, f * n:(f + 1) * n], s=src:
                       E.activation(out=d, in_=s, func=AF.Square), inc="act")
            cast_tok = p.tok("pool", "act")
            tokS = mm_group(6, [(ones_col_b[:, 0:1], xcb[:, f * n:(f + 1) * n])
                                for f in range(F)], cast_tok, bslice=pb[6][0:1, 0:n])
            tokQ = mm_group(7, [(ones_col_b[:, 0:1], sqb[:, f * n:(f + 1) * n])
                                for f in range(F)], cast_tok, bslice=pb[7][0:1, 0:n])
            state["lnmm"] = {"pe": tokQ["pe"]}
            tokm = consume(6, "vector", lambda E: E.tensor_copy(
                out=m_row[0:1, 0:n], in_=pb[6][0:1, 0:n]), tokS, "dve")
            p.wait("scalar", tokm)
            ca = p.emit("scalar", lambda E: E.activation(
                out=srow1[0:1, 0:n], in_=m_row[0:1, 0:n], func=AF.Square), inc="act")
            tq = dict(tokQ)
            tq["act"] = ca
            tokv = consume(7, "vector", lambda E: E.tensor_sub(
                out=srow2[0:1, 0:n], in0=pb[7][0:1, 0:n], in1=srow1[0:1, 0:n]),
                tq, "dve")
            p.wait("scalar", tokv)
            p.emit("scalar", lambda E: E.activation(
                out=rs_row[0:1, 0:n], in_=srow2[0:1, 0:n], func=AF.Ln, bias=EPS))
            ca = p.emit("scalar", lambda E: E.activation(
                out=rs_row[0:1, 0:n], in_=rs_row[0:1, 0:n], func=AF.Exp, scale=-0.5),
                inc="act")
            p.wait("vector", {"act": ca})
            p.wait("vector", state["lnmm"])
            cd = p.emit("vector", lambda E: E.tensor_mul(
                out=mrs_row[0:1, 0:n], in0=m_row[0:1, 0:n], in1=rs_row[0:1, 0:n]),
                inc="dve")
            return {"dve": cd}

        def ln_bcast(n, rows_tok):
            mm_group(4, [(ones_row_f[0:1, :], rs_row[0:1, 0:n])], rows_tok,
                     bslice=pb[4][:, 0:n])
            t2 = mm_group(5, [(ones_row_f[0:1, :], mrs_row[0:1, 0:n])], rows_tok,
                          bslice=pb[5][:, 0:n])
            state["lnmm"] = {"pe": t2["pe"]}
            return {"pe": t2["pe"]}

        def seal45(tokd):
            bank_state[4] = dict(tokd)
            bank_state[5] = dict(tokd)

        def dbg_dump(name, src_fn, nf=F):
            if name not in dbgset:
                return
            t = p.tok()
            p.wait("sync", t)
            for f in range(nf):
                dma(dbg_d[name][f], src_fn(f), sem="out")
            p.fence()

        def load_transpose_x0(b, ch, STGx, SMFx, stg_tok, smf_free):
            ct = dict()
            for i in range(CH // 128):
                tt = ch * (CH // 128) + i
                s = (ch * (CH // 128) + i) % 3
                t = dma(STGx[:, s * 1024:(s + 1) * 1024], x_d[b, tt],
                        wait_tok=stg_tok[s])
                for f in range(F):
                    bk = 6 + (f % 2)
                    p.wait("tensor", bank_state[bk])
                    p.wait("tensor", t)
                    cpe = p.emit("tensor", lambda E, bk=bk, s=s, f=f, STGx=STGx:
                                 E.transpose(pb[bk][:, 0:128],
                                             STGx[:, s * 1024 + f * 128:s * 1024 + (f + 1) * 128],
                                             ident), inc="pe")
                    dst = SMFx[:, f * CH + i * 128:f * CH + (i + 1) * 128]
                    ct = consume(bk, "vector", lambda E, dst=dst, bk=bk:
                                 E.tensor_copy(out=dst, in_=pb[bk][:, 0:128]),
                                 dict({"pe": cpe}, **smf_free), "dve")
                stg_tok[s] = {"pe": cpe}
            return ct

        cache_tok = {}

        for b in range(BPC):
            bdbg = (b == DBG_BATCH)
            p.fence()
            p.emit("gpsimd", lambda E: E.memset(VEC["pcar"], 0.0))
            p.emit("gpsimd", lambda E: E.memset(VEC["qcar"], 0.0))
            p.emit("gpsimd", lambda E: E.memset(hcar, 0.0))
            c = p.emit("gpsimd", lambda E: E.memset(VEC["pool8"], 0.0), inc="pool")
            p.fence()

            ar.push()
            STG = ar.f32(3 * 1024)
            SMF = ar.f32(F * CH)
            XCB = ar.bf16(F * CH)
            SQB = ar.bf16(F * CH)
            stg_tok = [dict(), dict(), dict()]
            smf_free = dict(const_tok)
            for ch in range(NCH):
                tr_tok = load_transpose_x0(b, ch, STG, SMF, stg_tok, smf_free)
                stok = ln_stats(lambda f: SMF[:, f * CH:(f + 1) * CH], 0, CH,
                                XCB, SQB, tr_tok)
                btok = ln_bcast(CH, stok)
                p.wait("vector", btok)
                p.wait("vector", tr_tok)
                for f in range(F):
                    dst = st_f(f)[:, ch * CH:(ch + 1) * CH]
                    src = SMF[:, f * CH:(f + 1) * CH]
                    p.emit("vector", lambda E, dst=dst, src=src: E.tensor_mul(
                        out=dst, in0=src, in1=pb[4][:, 0:CH]))
                    c = p.emit("vector", lambda E, dst=dst: E.tensor_sub(
                        out=dst, in0=dst, in1=pb[5][:, 0:CH]), inc="dve")
                seal45({"dve": c})
                smf_free = dict(p.tok("pool", "act"), dve=c)
            p.fence()
            ar.pop()
            if bdbg:
                dbg_dump("h0", st_f)
            if upto < 2:
                continue

            ar.push()
            WK = ar.bf16(F * D)
            WV = ar.bf16(F * D)
            WR = ar.bf16(F * D)
            WSTG = ar.f32(2 * 1024)
            HBUF = ar.bf16(F * 516)
            MIXK = ar.bf16(F * QT)
            MIXV = ar.bf16(F * QT)
            MIXR = ar.bf16(F * QT)
            RSIG = ar.bf16(F * QT)
            REG = ar._take(18944)
            XCB2 = ar.bf16_at(REG, F * QT)
            SQB2 = ar.bf16_at(REG + 2 * F * QT, F * QT)
            RW = ar.bf16_at(REG, F * QT)
            EK = ar.f32_at(REG + 8192, QT)
            VV = ar.f32_at(REG + 10240, QT)
            BP = ar.f32_at(REG + 12288, QT)
            PBUF = ar.f32_at(REG + 14336, QT + 4)
            QBUF = ar.f32_at(REG + 16400, QT + 4)

            def hb(f):
                return HBUF[:, f * 516:(f + 1) * 516]

            state["wslot"] = 0
            state["wtok"] = [dict(), dict()]
            wkr = wk_d.rearrange("(k p) o -> k p o", p=128)
            wvr = wv_d.rearrange("(k p) o -> k p o", p=128)
            wrr = wr_d.rearrange("(k p) o -> k p o", p=128)
            wor = wo_d.rearrange("(k p) o -> k p o", p=128)
            for ki in range(F):
                load_cast(WK[:, ki * D:(ki + 1) * D], wkr[ki], WSTG, 2, D)
            for ki in range(F):
                load_cast(WV[:, ki * D:(ki + 1) * D], wvr[ki], WSTG, 2, D)
            wtok = dict()
            for ki in range(F):
                wtok = load_cast(WR[:, ki * D:(ki + 1) * D], wrr[ki], WSTG, 2, D)

            prev_wo_pe = dict()
            chain_tok = dict(ew_tok)
            for q in range(NQ):
                t0 = q * QT
                stok = ln_stats(st_f, t0, QT, XCB2, SQB2, prev_wo_pe)
                btok = ln_bcast(QT, stok)
                p.wait("vector", btok)
                p.wait("vector", prev_wo_pe)
                for f in range(F):
                    src = st_f(f)[:, t0:t0 + QT]
                    p.emit("vector", lambda E, src=src: E.tensor_mul(
                        out=SCRF, in0=src, in1=pb[4][:, 0:QT]))
                    p.emit("vector", lambda E: E.tensor_sub(
                        out=SCRF, in0=SCRF, in1=pb[5][:, 0:QT]))
                    p.emit("vector", lambda E, f=f: E.tensor_copy(
                        out=hb(f)[:, 0:1], in_=hcar[:, f:f + 1]))
                    cs = p.emit("vector", lambda E, f=f: E.tensor_scalar(
                        out=hb(f)[:, 1:513], in0=SCRF, scalar1=VEC["lag"][:, f:f + 1],
                        scalar2=VEC["lab"][:, f:f + 1], op0=ALU.mult, op1=ALU.add),
                        inc="dve")
                    p.force_wait("vector", "dve", cs)
                    c = p.emit("vector", lambda E, f=f: E.tensor_copy(
                        out=hcar[:, f:f + 1], in_=hb(f)[:, 512:513]), inc="dve")
                seal45({"dve": c})
                for f in range(F):
                    p.emit("vector", lambda E, f=f: E.tensor_sub(
                        out=SCRB[:, 0:QT], in0=hb(f)[:, 1:513], in1=hb(f)[:, 0:512]))
                    for mx, buf in (("amk", MIXK), ("amv", MIXV), ("amr", MIXR)):
                        p.emit("vector", lambda E, f=f, mx=mx: E.tensor_single_scalar(
                            out=SCRB[:, QT:2 * QT], in_=SCRB[:, 0:QT],
                            scalar=VEC[mx][:, f:f + 1], op=ALU.mult))
                        c = p.emit("vector", lambda E, f=f, buf=buf: E.tensor_add(
                            out=buf[:, f * QT:(f + 1) * QT], in0=SCRB[:, QT:2 * QT],
                            in1=hb(f)[:, 0:512]), inc="dve")
                mix_tok = dict(wtok, dve=c)

                rs_tok = {}
                rsig_war = p.tok("dve")
                for fo in range(F):
                    bk = fo % 4
                    tkr = mm_group(bk, [(WR[:, ki * D + fo * 128:ki * D + fo * 128 + 128],
                                         MIXR[:, ki * QT:(ki + 1) * QT]) for ki in range(F)],
                                   mix_tok)
                    rs_tok = consume(bk, "scalar", lambda E, fo=fo, bk=bk: E.activation(
                        out=RSIG[:, fo * QT:(fo + 1) * QT], in_=pb[bk][:, 0:QT],
                        func=AF.Sigmoid), dict(tkr, **rsig_war), "act")

                for fo in range(F):
                    bk_k = fo % 2
                    bk_v = 2 + (fo % 2)
                    tkk = mm_group(bk_k, [(WK[:, ki * D + fo * 128:ki * D + fo * 128 + 128],
                                           MIXK[:, ki * QT:(ki + 1) * QT]) for ki in range(F)],
                                   mix_tok)
                    tkv = mm_group(bk_v, [(WV[:, ki * D + fo * 128:ki * D + fo * 128 + 128],
                                           MIXV[:, ki * QT:(ki + 1) * QT]) for ki in range(F)],
                                   mix_tok)
                    ek_tok = consume(bk_k, "scalar", lambda E, bk=bk_k: E.activation(
                        out=EK, in_=pb[bk][:, 0:QT], func=AF.Exp),
                        dict(tkk, **chain_tok), "act")
                    vv_tok = consume(bk_v, "scalar", lambda E, bk=bk_v: E.activation(
                        out=VV, in_=pb[bk][:, 0:QT], func=AF.Copy),
                        dict(tkv, **chain_tok), "act")
                    p.wait("vector", ek_tok)
                    p.wait("vector", vv_tok)
                    p.emit("vector", lambda E: E.tensor_mul(out=BP, in0=EK, in1=VV))
                    p.emit("vector", lambda E, f=fo: E.tensor_copy(
                        out=PBUF[:, 0:1], in_=VEC["pcar"][:, f:f + 1]))
                    cs = p.emit("vector", lambda E, f=fo: E.tensor_tensor_scan(
                        out=PBUF[:, 1:513], data0=VEC["ew"][:, f:f + 1].broadcast_to([128, QT]),
                        data1=BP, initial=VEC["pcar"][:, f:f + 1],
                        op0=ALU.mult, op1=ALU.add), inc="dve")
                    p.force_wait("vector", "dve", cs)
                    p.emit("vector", lambda E, f=fo: E.tensor_copy(
                        out=VEC["pcar"][:, f:f + 1], in_=PBUF[:, 512:513]))
                    p.emit("vector", lambda E, f=fo: E.tensor_copy(
                        out=QBUF[:, 0:1], in_=VEC["qcar"][:, f:f + 1]))
                    cs = p.emit("vector", lambda E, f=fo: E.tensor_tensor_scan(
                        out=QBUF[:, 1:513], data0=VEC["ew"][:, f:f + 1].broadcast_to([128, QT]),
                        data1=EK, initial=VEC["qcar"][:, f:f + 1],
                        op0=ALU.mult, op1=ALU.add), inc="dve")
                    p.force_wait("vector", "dve", cs)
                    p.emit("vector", lambda E, f=fo: E.tensor_copy(
                        out=VEC["qcar"][:, f:f + 1], in_=QBUF[:, 512:513]))
                    p.emit("vector", lambda E, f=fo: E.scalar_tensor_tensor(
                        out=BP, in0=BP, scalar=VEC["eu"][:, f:f + 1], in1=PBUF[:, 0:512],
                        op0=ALU.mult, op1=ALU.add))
                    p.emit("vector", lambda E, f=fo: E.scalar_tensor_tensor(
                        out=EK, in0=EK, scalar=VEC["eu"][:, f:f + 1], in1=QBUF[:, 0:512],
                        op0=ALU.mult, op1=ALU.add))
                    cs = p.emit("vector", lambda E: E.reciprocal(out=VV, in_=EK),
                                inc="dve")
                    p.force_wait("vector", "dve", cs)
                    p.emit("vector", lambda E: E.tensor_mul(out=BP, in0=BP, in1=VV))
                    p.wait("vector", rs_tok)
                    c = p.emit("vector", lambda E, fo=fo: E.tensor_mul(
                        out=RW[:, fo * QT:(fo + 1) * QT], in0=BP,
                        in1=RSIG[:, fo * QT:(fo + 1) * QT]), inc="dve")
                    chain_tok = {"dve": c}
                    if "wkv1" in dbgset and b == 0 and q == 1 and fo == 0:
                        t_ = p.tok()
                        p.wait("sync", t_)
                        dma(dbg_d["wkv1"][0], PBUF, sem="out")
                        dma(dbg_d["wkv1"][1], QBUF, sem="out")
                        dma(dbg_d["wkv1"][2][:, 0:QT], BP, sem="out")
                        dma(dbg_d["wkv1"][3][:, 0:QT], EK, sem="out")
                        p.fence()
                    if "wkv0" in dbgset and b == 0 and q == 0 and fo == 0:
                        t_ = p.tok()
                        p.wait("sync", t_)
                        dma(dbg_d["wkv0"][0], PBUF, sem="out")
                        dma(dbg_d["wkv0"][1], QBUF, sem="out")
                        dma(dbg_d["wkv0"][2][:, 0:8], VEC["pcar"], sem="out")
                        dma(dbg_d["wkv0"][3][:, 0:8], VEC["qcar"], sem="out")
                        p.fence()
                rw_tok = {"dve": c}

                kv_pe = p.tok("pe")
                wo_tok = dict()
                for ki in range(F):
                    dst = (MIXK[:, ki * D:(ki + 1) * D] if ki < 4
                           else MIXV[:, (ki - 4) * D:(ki - 3) * D])
                    wo_tok = load_cast(dst, wor[ki], WSTG, 2, D, cast_wait=kv_pe)

                def wo_tile(ki, fo):
                    if ki < 4:
                        return MIXK[:, ki * D + fo * 128:ki * D + fo * 128 + 128]
                    return MIXV[:, (ki - 4) * D + fo * 128:(ki - 4) * D + fo * 128 + 128]

                for fo in range(F):
                    bk = fo % 4
                    tko = mm_group(bk, [(wo_tile(ki, fo), RW[:, ki * QT:(ki + 1) * QT])
                                        for ki in range(F)], dict(rw_tok, **wo_tok))
                    dst = st_f(fo)[:, t0:t0 + QT]
                    consume(bk, "vector", lambda E, dst=dst, bk=bk: E.tensor_add(
                        out=dst, in0=dst, in1=pb[bk][:, 0:QT]), tko, "dve")
                prev_wo_pe = p.tok("pe")
            p.fence()
            ar.pop()
            if bdbg:
                dbg_dump("xa_att", st_f)
            if upto < 3:
                continue

            ar.push()
            FWR = ar.bf16(F * D)
            WKC = ar.bf16(F * CH)
            WVR = ar.bf16(4 * 1024)
            WSTG2 = ar.f32(2 * 1024)
            H2BUF = ar.bf16(F * 516)
            KFIN = ar.bf16(F * QT)
            MIXR2 = ar.bf16(F * QT)
            RFSIG = ar.bf16(F * QT)
            UFULL = ar.bf16(32 * QT)

            def h2(f):
                return H2BUF[:, f * 516:(f + 1) * 516]

            state["wslot"] = 0
            state["wtok"] = [dict(), dict()]
            p.emit("gpsimd", lambda E: E.memset(hcar, 0.0), inc="pool")
            fwrr = fwr_d.rearrange("(k p) o -> k p o", p=128)
            fwr_tok = dict()
            for ki in range(F):
                fwr_tok = load_cast(FWR[:, ki * D:(ki + 1) * D], fwrr[ki], WSTG2, 2, D)

            prev_q_pe = dict()
            for q in range(NQ):
                t0 = q * QT
                stok = ln_stats(st_f, t0, QT, UFULL[:, 0:F * QT],
                                UFULL[:, 2 * F * QT:3 * F * QT], prev_q_pe)
                btok = ln_bcast(QT, stok)
                p.wait("vector", btok)
                p.wait("vector", prev_q_pe)
                for f in range(F):
                    src = st_f(f)[:, t0:t0 + QT]
                    p.emit("vector", lambda E, src=src: E.tensor_mul(
                        out=SCRF, in0=src, in1=pb[4][:, 0:QT]))
                    p.emit("vector", lambda E: E.tensor_sub(
                        out=SCRF, in0=SCRF, in1=pb[5][:, 0:QT]))
                    p.emit("vector", lambda E, f=f: E.tensor_copy(
                        out=h2(f)[:, 0:1], in_=hcar[:, f:f + 1]))
                    cs = p.emit("vector", lambda E, f=f: E.tensor_scalar(
                        out=h2(f)[:, 1:513], in0=SCRF, scalar1=VEC["lfg"][:, f:f + 1],
                        scalar2=VEC["lfb"][:, f:f + 1], op0=ALU.mult, op1=ALU.add),
                        inc="dve")
                    p.force_wait("vector", "dve", cs)
                    c = p.emit("vector", lambda E, f=f: E.tensor_copy(
                        out=hcar[:, f:f + 1], in_=h2(f)[:, 512:513]), inc="dve")
                seal45({"dve": c})
                for f in range(F):
                    p.emit("vector", lambda E, f=f: E.tensor_sub(
                        out=SCRB[:, 0:QT], in0=h2(f)[:, 1:513], in1=h2(f)[:, 0:512]))
                    for mx, buf in (("fmk", KFIN), ("fmr", MIXR2)):
                        p.emit("vector", lambda E, f=f, mx=mx: E.tensor_single_scalar(
                            out=SCRB[:, QT:2 * QT], in_=SCRB[:, 0:QT],
                            scalar=VEC[mx][:, f:f + 1], op=ALU.mult))
                        c = p.emit("vector", lambda E, f=f, buf=buf: E.tensor_add(
                            out=buf[:, f * QT:(f + 1) * QT], in0=SCRB[:, QT:2 * QT],
                            in1=h2(f)[:, 0:512]), inc="dve")
                mix_tok = dict(fwr_tok, dve=c)

                rf_tok = {}
                rf_war = p.tok("dve")
                for fo in range(F):
                    bk = fo % 4
                    tkr = mm_group(bk, [(FWR[:, ki * D + fo * 128:ki * D + fo * 128 + 128],
                                         MIXR2[:, ki * QT:(ki + 1) * QT]) for ki in range(F)],
                                   mix_tok)
                    rf_tok = consume(bk, "scalar", lambda E, fo=fo, bk=bk: E.activation(
                        out=RFSIG[:, fo * QT:(fo + 1) * QT], in_=pb[bk][:, 0:QT],
                        func=AF.Sigmoid), dict(tkr, **rf_war), "act")

                u_tok = dict()
                wkc_war = dict(prev_q_pe)
                first_bq = (b == 0 and q == 0)
                for hc in range(DH // CH):
                    if not first_bq:
                        wk_tok = dma(WKC[:, 0:F * CH], fwk_c_d[hc], sem="wks",
                                     wait_tok=dict(wkc_war, **p.tok("cwk")))
                    else:
                        wk_tok = dict()
                        ww = dict(wkc_war, **cache_tok.get(("fwkwr",), {}))
                        for ki in range(F):
                            wk_tok = load_cast(WKC[:, ki * CH:(ki + 1) * CH],
                                               fwk_d[ki * 128:(ki + 1) * 128,
                                                     hc * CH:(hc + 1) * CH],
                                               WSTG2, 2, CH, cast_wait=ww)
                        wt = dma(fwk_c_d[hc], WKC[:, 0:F * CH], sem="cwk",
                                 wait_tok=wk_tok)
                        cache_tok[("fwkwr",)] = wt
                    for ht in range(CH // 128):
                        bk = 4 + (ht % 2)
                        hk = hc * 4 + ht
                        tku = mm_group(bk, [(WKC[:, ki * CH + ht * 128:ki * CH + ht * 128 + 128],
                                             KFIN[:, ki * QT:(ki + 1) * QT]) for ki in range(F)],
                                       dict(mix_tok, **wk_tok))
                        tmax = consume(bk, "vector", lambda E, bk=bk: E.tensor_single_scalar(
                            out=SCRF, in_=pb[bk][:, 0:QT], scalar=0.0, op=ALU.max),
                            tku, "dve")
                        p.wait("scalar", tmax)
                        ca = p.emit("scalar", lambda E, hk=hk: E.activation(
                            out=UFULL[:, hk * QT:(hk + 1) * QT], in_=SCRF,
                            func=AF.Square), inc="act")
                        p.wait("vector", {"act": ca})
                        u_tok = {"act": ca}
                    wkc_war = p.tok("pe")

                if "bdump" in dbgset and b == 0 and q == 0:
                    t_ = p.tok()
                    p.wait("sync", t_)
                    dma(dbg_d["bdump"][0][:, 0:F * QT], KFIN, sem="out")
                    dma(dbg_d["bdump"][1][:, 0:F * QT], RFSIG, sem="out")
                    dma(dbg_d["bdump"][2][:, 0:F * QT], MIXR2, sem="out")
                    dma(dbg_d["bdump"][3], UFULL, sem="out")
                    p.fence()
                for fo in range(F):
                    p.wait("tensor", bank_state[fo])
                ring_tok = [dict() for _ in range(4)]
                ztoks = {}
                for hk in range(32):
                    s = hk % 4
                    if not first_bq:
                        wvt = dma(WVR[:, s * 1024:(s + 1) * 1024], fwv_c_d[hk],
                                  sem="wv%d" % s,
                                  wait_tok=dict(ring_tok[s], **p.tok("cwv")))
                    else:
                        ww = dict(ring_tok[s], **cache_tok.get(("fwvwr", s), {}))
                        wvt = load_cast(WVR[:, s * 1024:(s + 1) * 1024],
                                        fwv_d[hk * 128:(hk + 1) * 128, :],
                                        WSTG2, 2, D, cast_wait=ww)
                        wt = dma(fwv_c_d[hk], WVR[:, s * 1024:(s + 1) * 1024],
                                 sem="cwv", wait_tok=wvt)
                        cache_tok[("fwvwr", s)] = wt
                    p.wait("tensor", dict(wvt, **u_tok))
                    for fo in range(F):
                        last = (hk == 31)
                        cpe = p.emit("tensor", lambda E, s=s, fo=fo, hk=hk:
                                     E.matmul(pb[fo][:, 0:QT],
                                              WVR[:, s * 1024 + fo * 128:s * 1024 + fo * 128 + 128],
                                              UFULL[:, hk * QT:(hk + 1) * QT],
                                              start=(hk == 0), stop=(hk == 31)),
                                     inc="pe" if (last or fo == F - 1) else None)
                        if last:
                            ztoks[fo] = {"pe": cpe}
                    ring_tok[s] = {"pe": p.cnt.get("pe", 0)}
                for fo in range(F):
                    dst = st_f(fo)[:, t0:t0 + QT]
                    p.wait("vector", ztoks[fo])
                    p.wait("vector", rf_tok)
                    p.emit("vector", lambda E, fo=fo: E.tensor_mul(
                        out=SCRF, in0=pb[fo][:, 0:QT],
                        in1=RFSIG[:, fo * QT:(fo + 1) * QT]))
                    c = p.emit("vector", lambda E, dst=dst: E.tensor_add(
                        out=dst, in0=dst, in1=SCRF), inc="dve")
                    bank_state[fo] = {"dve": c}
                prev_q_pe = p.tok("pe")
            p.fence()
            ar.pop()
            if bdbg:
                dbg_dump("xa_ffn", st_f)
            if upto < 4:
                continue

            ar.push()
            STGC = ar.f32(3 * 1024)
            SMFC = ar.f32(F * CH)
            XCBc = ar.bf16(F * CH)
            SQBc = ar.bf16(F * CH)
            stg_tok = [dict(), dict(), dict()]
            smf_free = dict()
            for ch in range(NCH):
                stok = ln_stats(st_f, ch * CH, CH, XCBc, SQBc, dict())
                tr_tok = load_transpose_x0(b, ch, STGC, SMFC, stg_tok, smf_free)
                btok = ln_bcast(CH, stok)
                p.wait("vector", btok)
                p.wait("vector", tr_tok)
                for f in range(F):
                    dst = st_f(f)[:, ch * CH:(ch + 1) * CH]
                    p.emit("vector", lambda E, dst=dst: E.tensor_mul(
                        out=SCRF, in0=dst, in1=pb[4][:, 0:CH]))
                    p.emit("vector", lambda E: E.tensor_sub(
                        out=SCRF, in0=SCRF, in1=pb[5][:, 0:CH]))
                    c = p.emit("vector", lambda E, dst=dst, f=f: E.tensor_add(
                        out=dst, in0=SCRF, in1=SMFC[:, f * CH:(f + 1) * CH]), inc="dve")
                seal45({"dve": c})
                smf_free = {"dve": c}
            p.fence()
            ar.pop()
            if bdbg:
                dbg_dump("x1", st_f)
            if upto < 5:
                continue

            ar.push()
            ACC = ar.f32(F * HALF)
            HPB = ar.bf16(F * HALF)
            WA0 = ar.bf16(F * HALF)
            WA1 = ar.bf16(F * HALF)
            wbc_off = ar.off
            WBC = ar.bf16(F * HALF)
            UALL = ar.bf16(F * QT)
            WSTG3 = ar.f32(2 * 2048)
            SMFD = ar.f32(2 * D)
            XCBd = WA0
            SQBd = WA1
            rW1 = ar.f32_at(wbc_off, 2048)
            lcd = mk_stager(WSTG3, 2048, 2)

            W1r = rW1_d.rearrange("(k p) h -> k p h", p=128)
            for k in range(F):
                rw1_tok = dma(rW1[:, k * 256:(k + 1) * 256], W1r[k])
            p.wait("tensor", rw1_tok)

            state["wslot"] = 0
            state["wtok"] = [dict(), dict()]

            for ch in range(NCH):
                stok = ln_stats(st_f, ch * CH, CH, XCBd[:, 0:F * CH], SQBd[:, 0:F * CH],
                                dict())
                btok = ln_bcast(CH, stok)
                p.wait("vector", btok)
                for f in range(F):
                    src = st_f(f)[:, ch * CH:(ch + 1) * CH]
                    p.emit("vector", lambda E, src=src: E.tensor_mul(
                        out=SCRF, in0=src, in1=pb[4][:, 0:CH]))
                    cs = p.emit("vector", lambda E: E.tensor_sub(
                        out=SCRF, in0=SCRF, in1=pb[5][:, 0:CH]), inc="dve")
                    p.force_wait("vector", "dve", cs)
                    cs = p.emit("vector", lambda E: E.reduce_sum(
                        out=VEC["ptmp"][:, 0:1], in_=SCRF, axis=AX.X), inc="dve")
                    p.force_wait("vector", "dve", cs)
                    c = p.emit("vector", lambda E, f=f: E.tensor_add(
                        out=VEC["pool8"][:, f:f + 1], in0=VEC["pool8"][:, f:f + 1],
                        in1=VEC["ptmp"][:, 0:1]), inc="dve")
                seal45({"dve": c})
            c = p.emit("vector", lambda E: E.tensor_single_scalar(
                out=VEC["pool8"], in_=VEC["pool8"], scalar=1.0 / L, op=ALU.mult),
                inc="dve")
            pool_tok = {"dve": c}

            for hk in range(2):
                tkh = mm_group(4, [(rW1[:, ki * 256 + hk * 128:ki * 256 + hk * 128 + 128],
                                    VEC["pool8"][:, ki:ki + 1]) for ki in range(F)],
                               pool_tok, bslice=pb[4][:, 0:1])
                consume(4, "scalar", lambda E, hk=hk: E.activation(
                    out=rh2[:, hk:hk + 1], in_=pb[4][:, 0:1], func=AF.Tanh,
                    bias=rb1c[:, hk:hk + 1]), tkh, "act")
            tkl = mm_group(5, [(rh2[:, hk:hk + 1], rW2[:, hk * 8:(hk + 1) * 8])
                               for hk in range(2)], p.tok("act"),
                           bslice=pb[5][0:1, 0:8])
            ltok = consume(5, "vector", lambda E: E.tensor_add(
                out=rrow[0:1, 0:8], in0=pb[5][0:1, 0:8], in1=b2row[0:1, 0:8]),
                tkl, "dve")
            p.wait("vector", ltok)

            def vstep(fn):
                c = p.emit("vector", fn, inc="dve")
                p.force_wait("vector", "dve", c)
                return c

            vstep(lambda E: E.reduce_max(out=srow1[0:1, 0:1],
                                         in_=rrow[0:1, 0:8], axis=AX.X))
            c = vstep(lambda E: E.tensor_single_scalar(
                out=rrow[0:1, 0:8], in_=rrow[0:1, 0:8], scalar=srow1[0:1, 0:1],
                op=ALU.subtract))
            p.wait("scalar", {"dve": c})
            ca = p.emit("scalar", lambda E: E.activation(
                out=rrow[0:1, 0:8], in_=rrow[0:1, 0:8], func=AF.Exp), inc="act")
            p.wait("vector", {"act": ca})
            vstep(lambda E: E.reduce_sum(out=srow1[0:1, 0:1],
                                         in_=rrow[0:1, 0:8], axis=AX.X))
            vstep(lambda E: E.reciprocal(out=srow1[0:1, 4:5],
                                         in_=srow1[0:1, 0:1]))
            vstep(lambda E: E.tensor_single_scalar(
                out=rrow[0:1, 0:8], in_=rrow[0:1, 0:8], scalar=srow1[0:1, 4:5],
                op=ALU.mult))
            vstep(lambda E: E.reduce_max(out=srow1[0:1, 1:2],
                                         in_=rrow[0:1, 0:8], axis=AX.X))
            vstep(lambda E: E.tensor_single_scalar(
                out=rrow2[0:1, 0:8], in_=rrow[0:1, 0:8], scalar=srow1[0:1, 1:2],
                op=ALU.is_ge))
            vstep(lambda E: E.scalar_tensor_tensor(
                out=rrow2[0:1, 0:8], in0=rrow2[0:1, 0:8], scalar=-1e30,
                in1=rrow[0:1, 0:8], op0=ALU.mult, op1=ALU.add))
            vstep(lambda E: E.reduce_max(out=srow1[0:1, 2:3],
                                         in_=rrow2[0:1, 0:8], axis=AX.X))
            vstep(lambda E: E.tensor_single_scalar(
                out=srow2[0:1, 0:8], in_=rrow2[0:1, 0:8], scalar=srow1[0:1, 2:3],
                op=ALU.is_ge))
            vstep(lambda E: E.scalar_tensor_tensor(
                out=rrow2[0:1, 0:8], in0=srow2[0:1, 0:8], scalar=-1e30,
                in1=rrow2[0:1, 0:8], op0=ALU.mult, op1=ALU.add))
            vstep(lambda E: E.reduce_max(out=srow1[0:1, 3:4],
                                         in_=rrow2[0:1, 0:8], axis=AX.X))
            vstep(lambda E: E.tensor_single_scalar(
                out=srow2[0:1, 0:8], in_=rrow[0:1, 0:8], scalar=srow1[0:1, 3:4],
                op=ALU.is_ge))
            vstep(lambda E: E.tensor_mul(
                out=rrow2[0:1, 0:8], in0=rrow[0:1, 0:8], in1=srow2[0:1, 0:8]))
            vstep(lambda E: E.reduce_sum(out=srow1[0:1, 0:1],
                                         in_=rrow2[0:1, 0:8], axis=AX.X))
            vstep(lambda E: E.reciprocal(out=srow1[0:1, 5:6],
                                         in_=srow1[0:1, 0:1]))
            c = vstep(lambda E: E.tensor_single_scalar(
                out=rrow2[0:1, 0:8], in_=rrow2[0:1, 0:8], scalar=srow1[0:1, 5:6],
                op=ALU.mult))
            tkw = mm_group(4, [(ones_row_f[0:1, :], rrow2[0:1, 0:8])], {"dve": c},
                           bslice=pb[4][:, 0:8])
            wbc_tok = consume(4, "vector", lambda E: E.tensor_copy(
                out=VEC["wbc"], in_=pb[4][:, 0:8]), tkw, "dve")
            if bdbg and "rdump" in dbgset:
                t_ = p.tok()
                p.wait("sync", t_)
                dma(dbg_d["rdump"][0][:, 0:8], VEC["pool8"], sem="out")
                dma(dbg_d["rdump"][1][:, 0:2], rh2, sem="out")
                dma(dbg_d["rdump"][2][0:1, 0:16], rrow[0:1, 0:16], sem="out")
                dma(dbg_d["rdump"][3][0:1, 0:16], rrow2[0:1, 0:16], sem="out")
                p.fence()
            if bdbg:
                dbg_dump("wrouter", lambda f: VEC["wbc"], nf=1)
            if upto < 6:
                p.fence()
                ar.pop()
                continue

            for hf in range(2):
                p.fence()
                ht0 = hf * HALF
                acc_tok = dict()
                cp = None
                for ch in range(2):
                    t0 = ht0 + ch * CH
                    stok = ln_stats(st_f, t0, CH, XCBd[:, 0:F * CH], SQBd[:, 0:F * CH],
                                    acc_tok)
                    btok = ln_bcast(CH, stok)
                    p.wait("vector", btok)
                    for f in range(F):
                        src = st_f(f)[:, t0:t0 + CH]
                        dst = ACC[:, f * HALF + ch * CH:f * HALF + (ch + 1) * CH]
                        p.emit("vector", lambda E, src=src, dst=dst: E.tensor_mul(
                            out=dst, in0=src, in1=pb[4][:, 0:CH]))
                        c = p.emit("vector", lambda E, dst=dst: E.tensor_sub(
                            out=dst, in0=dst, in1=pb[5][:, 0:CH]), inc="dve")
                        p.wait("gpsimd", {"dve": c})
                        cp = p.emit("gpsimd", lambda E, dst=dst, f=f, ch=ch:
                                    E.tensor_copy(out=HPB[:, f * HALF + ch * CH:
                                                          f * HALF + (ch + 1) * CH],
                                                  in_=dst), inc="pool")
                    seal45({"dve": c})
                    acc_tok = {"dve": c, "pool": cp}
                hp_tok = dict(acc_tok, **wbc_tok)

                wa_bufs = [WA0, WA1]
                war3 = wa_d.rearrange("z (ki p) h -> z p ki h", p=128)
                wbr3 = wb_d.rearrange("z (hk p) o -> z p hk o", p=128)

                first_pass = (b == 0 and hf == 0)

                def load_wa(it, war_tok):
                    ez, hc = divmod(it, 4)
                    wab = wa_bufs[it % 2]
                    if not first_pass:
                        return dma(wab[:, 0:F * HALF], wa_c_d[ez, hc],
                                   sem="wa%d" % (it % 2),
                                   wait_tok=dict(war_tok, **p.tok("cwa")))
                    t = dict()
                    ww = dict(war_tok, **cache_tok.get(("wawr", it % 2), {}))
                    for k2 in range(4):
                        t = lcd(wab[:, k2 * 2 * HALF:(k2 + 1) * 2 * HALF],
                                war3[ez][:, 2 * k2:2 * k2 + 2,
                                          hc * HALF:(hc + 1) * HALF],
                                2 * HALF, eng="gpsimd", cast_wait=ww)
                    wt = dma(wa_c_d[ez, hc], wab[:, 0:F * HALF], sem="cwa",
                             wait_tok=t)
                    cache_tok[("wawr", it % 2)] = wt
                    return t

                def load_wb(it, war_tok):
                    ez, hc = divmod(it, 4)
                    if not first_pass:
                        return dma(WBC[:, 0:F * HALF], wb_c_d[ez, hc], sem="wbs",
                                   wait_tok=dict(war_tok, **p.tok("cwb")))
                    t = dict()
                    ww = dict(war_tok, **cache_tok.get(("wbwr",), {}))
                    for j2 in range(4):
                        t = lcd(WBC[:, j2 * 2 * HALF:(j2 + 1) * 2 * HALF],
                                wbr3[ez][:, hc * 8 + 2 * j2:hc * 8 + 2 * j2 + 2, :],
                                2 * HALF, eng="gpsimd", cast_wait=ww)
                    wt = dma(wb_c_d[ez, hc], WBC[:, 0:F * HALF], sem="cwb",
                             wait_tok=t)
                    cache_tok[("wbwr",)] = wt
                    return t

                watoks = {0: load_wa(0, dict(hp_tok))}
                for ez in range(NZ):
                    for hc in range(4):
                        it = ez * 4 + hc
                        wab = wa_bufs[it % 2]
                        wbtok = load_wb(it, dict(hp_tok) if it == 0 else p.tok("pe"))
                        if it + 1 < NZ * 4:
                            watoks[it + 1] = load_wa(it + 1, p.tok("pe"))
                        watok = watoks[it]
                        for tw in range(2):
                            ug = dict()
                            for ht in range(F):
                                bk = 4 + (ht % 2)
                                tku = mm_group(bk, [
                                    (wab[:, ki * HALF + ht * 128:ki * HALF + ht * 128 + 128],
                                     HPB[:, ki * HALF + tw * QT:ki * HALF + (tw + 1) * QT])
                                    for ki in range(F)], dict(watok, **hp_tok))
                                ug = consume(bk, "scalar", lambda E, ht=ht, bk=bk:
                                             E.activation(out=UALL[:, ht * QT:(ht + 1) * QT],
                                                          in_=pb[bk][:, 0:QT],
                                                          func=AF.Gelu_apprx_tanh),
                                             tku, "act")
                            for fg in range(2):
                                for j in range(4):
                                    fo = fg * 4 + j
                                    tkz = mm_group(j, [
                                        (WBC[:, ht * HALF + fo * 128:ht * HALF + fo * 128 + 128],
                                         UALL[:, ht * QT:(ht + 1) * QT])
                                        for ht in range(F)], dict(wbtok, **ug))
                                    dst = ACC[:, fo * HALF + tw * QT:fo * HALF + (tw + 1) * QT]
                                    consume(j, "vector", lambda E, dst=dst, j=j, ez=ez:
                                            E.scalar_tensor_tensor(
                                                out=dst, in0=pb[j][:, 0:QT],
                                                scalar=VEC["wbc"][:, ez:ez + 1],
                                                in1=dst, op0=ALU.mult, op1=ALU.add),
                                            tkz, "dve")
                if bdbg and hf == 0:
                    dbg_dump("acc0", lambda f: ACC[:, f * HALF:(f + 1) * HALF])
                    dbg_dump("vecdump", lambda f: VEC["wbc"], nf=1)

                if upto < 7:
                    continue
                zdone = p.tok("pe", "dve")
                smfd_tok = [dict(), dict()]
                for ch in range(2):
                    t0 = ht0 + ch * CH
                    stok = ln_stats(lambda f: ACC[:, f * HALF:(f + 1) * HALF],
                                    ch * CH, CH, XCBd[:, 0:F * CH], SQBd[:, 0:F * CH],
                                    zdone)
                    btok = ln_bcast(CH, stok)
                    p.wait("vector", btok)
                    for f in range(F):
                        asrc = ACC[:, f * HALF + ch * CH:f * HALF + (ch + 1) * CH]
                        x1src = st_f(f)[:, t0:t0 + CH]
                        p.emit("vector", lambda E, asrc=asrc: E.tensor_mul(
                            out=SCRF, in0=asrc, in1=pb[4][:, 0:CH]))
                        p.emit("vector", lambda E: E.tensor_sub(
                            out=SCRF, in0=SCRF, in1=pb[5][:, 0:CH]))
                        c = p.emit("vector", lambda E, asrc=asrc, x1src=x1src:
                                   E.tensor_add(out=asrc, in0=SCRF, in1=x1src),
                                   inc="dve")
                    seal45({"dve": c})
                    och = {"dve": c}
                    for i in range(CH // 128):
                        tt = (t0 + i * 128) // 128
                        sd = i % 2
                        ot = dict()
                        for f in range(F):
                            bk = 6 + (f % 2)
                            p.wait("tensor", bank_state[bk])
                            p.wait("tensor", och)
                            cpe = p.emit("tensor", lambda E, bk=bk, f=f, ch=ch, i=i:
                                         E.transpose(pb[bk][:, 0:128],
                                                     ACC[:, f * HALF + ch * CH + i * 128:
                                                         f * HALF + ch * CH + (i + 1) * 128],
                                                     ident), inc="pe")
                            dst = SMFD[:, sd * D + f * 128:sd * D + (f + 1) * 128]
                            ot = consume(bk, "vector", lambda E, dst=dst, bk=bk:
                                         E.tensor_copy(out=dst, in_=pb[bk][:, 0:128]),
                                         dict({"pe": cpe}, **smfd_tok[sd]), "dve")
                        smfd_tok[sd] = dma(out_d[b, tt], SMFD[:, sd * D:(sd + 1) * D],
                                           sem="out", wait_tok=ot)
            p.fence()
            ar.pop()

        p.wait("gpsimd", p.tok("out"))

        block = es.enter_context(nc.Block())

        @block.sync
        def _(E):
            for fn in p.q["sync"]:
                fn(E)

        @block.gpsimd
        def _(E):
            for fn in p.q["gpsimd"]:
                fn(E)

        @block.vector
        def _(E):
            for fn in p.q["vector"]:
                fn(E)

        @block.scalar
        def _(E):
            for fn in p.q["scalar"]:
                fn(E)

        @block.tensor
        def _(E):
            for fn in p.q["tensor"]:
                fn(E)

    return nc


def kernel(**inputs):
    x = np.ascontiguousarray(inputs["L0_attn_x"])
    xs = x.reshape(NCORES, BPC, L // 128, 128, D)
    nc = build(debug_names=tuple(DBG), upto=UPTO)
    in_maps = []
    for c in range(NCORES):
        m = {"x": np.ascontiguousarray(xs[c])}
        for k, v in inputs.items():
            if k != "L0_attn_x":
                m[k] = np.ascontiguousarray(v)
        in_maps.append(m)
    res = run_bass_kernel_spmd(nc, in_maps, core_ids=list(range(NCORES)))
    outs = [np.asarray(r["out"]).reshape(BPC, L, D) for r in res.results]
    return np.concatenate(outs, axis=0)
